# version 04
# baseline (speedup 1.0000x reference)
import contextlib
import numpy as np

import concourse.bass as bass
import concourse.mybir as mybir
from concourse.bass_utils import run_bass_kernel_spmd

F32 = mybir.dt.float32
BF16 = mybir.dt.bfloat16
ALU = mybir.AluOpType
AF = mybir.ActivationFunctionType
AX = mybir.AxisListType

D = 2048
E = 6144
FF = 5632
GU = 11264
T = 2048
S1_PART = "full"
TS = 512
NTT = TS // 128
NST = T // TS
NDC = D // 128
NFC = FF // 128
ROPE_THETA = 10000.0
EPS = 1e-5


class Ev:
    __slots__ = ("sem", "val")

    def __init__(self, sem, val):
        self.sem = sem
        self.val = val


class SemCtr:
    def __init__(self, handle):
        self.h = handle
        self.n = 0

    def next(self, k):
        self.n += k
        return Ev(self, self.n)


class Prog:
    ENGS = ("sync", "gpsimd", "vector", "scalar", "tensor")

    def __init__(self, nc, st):
        self.nc = nc
        self.items = {e: [] for e in self.ENGS}
        self.sems = {}
        for name in ("pe", "ve", "se", "bar"):
            self.sems[name] = SemCtr(st.enter_context(nc.semaphore("s_" + name)))
        self.pools = {"d1": [], "d2": []}
        for name, n in (("d1", 6), ("d2", 4)):
            for i in range(n):
                s = SemCtr(st.enter_context(nc.semaphore(f"s_{name}_{i}")))
                self.sems[f"{name}_{i}"] = s
                self.pools[name].append(s)
        self.rr = {"d1": 0, "d2": 0}
        self.chains = {}

    def op(self, eng, fn, waits=(), inc=None, inc_k=1, chain=None):
        ws = [w for w in waits if w is not None]
        if chain is not None and chain in self.chains:
            ws.append(self.chains[chain])
        if inc in ("d1", "d2"):
            sem = self.pools[inc][self.rr[inc] % len(self.pools[inc])]
            self.rr[inc] += 1
            ws.append(Ev(sem, sem.n))
            ev = sem.next(inc_k)
        else:
            ev = self.sems[inc].next(inc_k) if inc else None

        def run(h, ws=ws, ev=ev, fn=fn, inc_k=inc_k):
            for w in ws:
                h.wait_ge(w.sem.h, w.val)
            ins = fn(h)
            if ev is not None:
                ins.then_inc(ev.sem.h, inc_k)
        self.items[eng].append(run)
        if chain is not None:
            assert ev is not None
            self.chains[chain] = ev
        return ev

    ESEM = {"vector": "ve", "scalar": "se", "tensor": "pe", "sync": "ve", "gpsimd": "ve"}

    def mset(self, eng, ap, val, waits=()):
        ev = self.op(eng, lambda h: h.memset(ap, val), waits=waits, inc=self.ESEM[eng])
        self.wait(eng, [ev])
        return ev

    def wait(self, eng, waits):
        ws = [w for w in waits if w is not None]

        def run(h, ws=ws):
            for w in ws:
                h.wait_ge(w.sem.h, w.val)
        self.items[eng].append(run)

    def barrier(self):
        bar = self.sems["bar"]
        pre = [Ev(s, s.n) for k, s in self.sems.items() if k != "bar"]
        for e in self.ENGS:
            self.wait(e, pre)
            self.op(e, lambda h: h.nop(), inc="bar")
        tgt = Ev(bar, bar.n)
        for e in self.ENGS:
            self.wait(e, [tgt])

    def install(self, block):
        for name in self.ENGS:
            prog = self.items[name]
            section = getattr(block, name)

            def body(h, prog=prog):
                for fn in prog:
                    fn(h)
            section(body)


_NAME_N = [0]


def _nm(base):
    _NAME_N[0] += 1
    return f"{base}_{_NAME_N[0]}"


def _rope_tables(pos):
    inv_freq = 1.0 / ROPE_THETA ** (np.arange(0, 1024, 2, dtype=np.float64) / 1024.0)
    fr = pos[:, None].astype(np.float64) * inv_freq[None, :]
    out = []
    for tab in (np.cos(fr), np.sin(fr)):
        hi = tab.astype(np.float32)
        lo = (tab - hi.astype(np.float64)).astype(np.float32)
        out.append(np.ascontiguousarray(np.concatenate([hi, lo], axis=1)))
    return out[0], out[1]


def build_program(debug=(), upto="all", s1_jobs=None, skip_p0=False):
    debug = set(debug)
    nc = bass.Bass("TRN2", target_bir_lowering=False)

    inp = {}
    for name, shape, dt in [
        ("x", [T, D], F32), ("xo", [T, D], F32),
        ("q_cos", [T, 1024], F32), ("q_sin", [T, 1024], F32),
        ("o_cos", [T, 1024], F32), ("o_sin", [T, 1024], F32),
        ("w_in", [E, D], F32), ("w_out", [D, D], F32),
        ("w_gu", [GU, D], F32), ("w_dn", [D, FF], F32),
        ("b_in_bc", [128, E], F32), ("b_out_bc", [128, D], F32),
        ("nw_bc", [128, D], F32),
    ]:
        inp[name] = nc.dram_tensor(name, shape, dt, kind="ExternalInput").ap()
    out = nc.dram_tensor("out", [T, D], F32, kind="ExternalOutput").ap()

    scr = {}
    for name, shape, dt in [
        ("winT_hi", [D, E], BF16), ("winT_lo", [D, E], BF16),
        ("woutT_hi", [D, D], BF16), ("woutT_lo", [D, D], BF16),
        ("guT_q", [D, GU], BF16), ("dnT_q", [FF, D], BF16),
        ("q_sp", [T, D], F32), ("h_sp", [TS, FF], F32),
    ]:
        scr[name] = nc.dram_tensor(name, shape, dt).ap()

    dbg_out = {}

    def dbg(name, shape, dt=F32):
        dbg_out[name] = nc.dram_tensor(name, shape, dt, kind="ExternalOutput").ap()
        return dbg_out[name]

    with contextlib.ExitStack() as st:
        P = Prog(nc, st)

        g = {}
        g["identf"] = st.enter_context(nc.sbuf_tensor("identf", [128, 128], F32))
        g["identb"] = st.enter_context(nc.sbuf_tensor("identb", [128, 128], BF16))
        g["iot"] = st.enter_context(nc.sbuf_tensor("iot", [128, 128], F32))
        g["ones_col"] = st.enter_context(nc.sbuf_tensor("ones_col", [128, 8], F32))
        g["ones_row"] = st.enter_context(nc.sbuf_tensor("ones_row", [1, 128], F32))
        g["wconst"] = st.enter_context(nc.sbuf_tensor("wconst", [128, 64], F32))
        g["A_acc"] = [st.enter_context(nc.sbuf_tensor(f"A_acc{i}", [128, 64], F32))
                      for i in range(2)]

        ev_iota = P.op("gpsimd", lambda h: h.iota(
            g["iot"][:], [[1, 128]], channel_multiplier=-1,
            allow_small_or_imprecise_dtypes=True), inc="ve")
        P.op("vector", lambda h: h.tensor_single_scalar(
            g["identf"][:], g["iot"][:], 0.0, ALU.is_equal), waits=[ev_iota], inc="ve")
        P.op("vector", lambda h: h.tensor_single_scalar(
            g["identb"][:], g["iot"][:], 0.0, ALU.is_equal), inc="ve")
        P.mset("vector", g["ones_col"][:], 1.0)
        P.mset("vector", g["ones_row"][:], 1.0)
        P.mset("vector", g["wconst"][:], 1.0)
        P.mset("vector", g["A_acc"][0][:], 0.0)
        P.mset("vector", g["A_acc"][1][:], 0.0)
        P.barrier()

        if not skip_p0:
            with contextlib.ExitStack() as ps:
                emit_wprep_hilo(nc, P, ps, inp["w_in"], scr["winT_hi"], scr["winT_lo"],
                                rows=E, g=g, scale_col=None)
            P.barrier()
        if "winT" in debug:
            emit_scratch_dump(nc, P, dbg("dbg_winT_hi", [D, E], BF16), scr["winT_hi"], D, E, BF16)
            emit_scratch_dump(nc, P, dbg("dbg_winT_lo", [D, E], BF16), scr["winT_lo"], D, E, BF16)
            P.barrier()
        if upto == "p0a":
            emit_dummy_out(nc, P, out)
            with nc.Block() as block:
                P.install(block)
            return nc

        if not skip_p0:
            with contextlib.ExitStack() as ps:
                emit_ffn_means(nc, P, ps, inp["w_gu"], inp["w_dn"], g)
            P.barrier()

        if not skip_p0:
            with contextlib.ExitStack() as ps:
                emit_wprep_quant(nc, P, ps, inp["w_gu"], scr["guT_q"], rows=GU,
                                 thr_pos=g["wconst"][:, 0:1], thr_neg=g["wconst"][:, 1:2],
                                 g=g, cols=D)
            P.barrier()
            with contextlib.ExitStack() as ps:
                emit_wprep_quant(nc, P, ps, inp["w_dn"], scr["dnT_q"], rows=D,
                                 thr_pos=g["wconst"][:, 3:4], thr_neg=g["wconst"][:, 4:5],
                                 g=g, cols=FF)
            P.barrier()
        if "wq" in debug:
            emit_scratch_dump(nc, P, dbg("dbg_guT", [D, GU], BF16), scr["guT_q"], D, GU, BF16)
            emit_scratch_dump(nc, P, dbg("dbg_dnT", [FF, D], BF16), scr["dnT_q"], FF, D, BF16)
            o5 = dbg("dbg_wconst", [128, 64], F32)
            evd = P.op("gpsimd", lambda h: h.dma_start(o5, g["wconst"][:]), inc="d2", inc_k=16)
            P.wait("gpsimd", [evd])
            P.barrier()
        if upto == "p0":
            emit_dummy_out(nc, P, out)
            with nc.Block() as block:
                P.install(block)
            return nc

        with contextlib.ExitStack() as ps:
            emit_sweep1(nc, P, ps, inp, scr, g, s1_jobs=s1_jobs)
        P.barrier()

        P.barrier()
        if "s1" in debug:
            emit_scratch_dump(nc, P, dbg("dbg_qsp", [T, D], F32), scr["q_sp"], T, D, F32)
            o7 = dbg("dbg_A", [128, 64], F32)
            n_t = (s1_jobs if s1_jobs is not None else 2 * NST) * NTT
            evd = P.op("gpsimd", lambda h: h.dma_start(o7, g["A_acc"][n_t % 2][:]),
                       inc="d2", inc_k=16)
            P.wait("gpsimd", [evd])
            P.barrier()
        if upto == "p1":
            emit_dummy_out(nc, P, out)
            with nc.Block() as block:
                P.install(block)
            return nc

        n_s1_tiles = (s1_jobs if s1_jobs is not None else 2 * NST) * NTT
        with contextlib.ExitStack() as ps:
            emit_wprep_hilo(nc, P, ps, inp["w_out"], scr["woutT_hi"], scr["woutT_lo"],
                            rows=D, g=g, scale_col=g["A_acc"][n_s1_tiles % 2])
        P.barrier()

        with contextlib.ExitStack() as ps:
            emit_sweep2(nc, P, ps, inp, scr, out, g)
        P.barrier()

        with nc.Block() as block:
            P.install(block)
    return nc


def emit_dummy_out(nc, P, out):
    with contextlib.ExitStack() as ps:
        z = ps.enter_context(nc.sbuf_tensor(_nm("zz"), [128, D], F32))
        ev = P.mset("vector", z[:], 0.0)
        evs = []
        for r in range(T // 128):
            evs.append(P.op("gpsimd", lambda h, r=r: h.dma_start(
                out[r * 128:(r + 1) * 128, :], z[:]), waits=[ev], inc="d2", inc_k=16))
        P.wait("gpsimd", [evs[-1]])
        P.barrier()


def emit_scratch_dump(nc, P, dst, src, rows, cols, dt):
    with contextlib.ExitStack() as ps:
        buf = ps.enter_context(nc.sbuf_tensor(_nm("dump_buf"), [128, cols], dt))
        ev_st = None
        for r in range(rows // 128):
            ev_ld = P.op("sync", lambda h, r=r: h.dma_start(
                buf[:], src[r * 128:(r + 1) * 128, :]), waits=[ev_st], inc="d1", inc_k=16)
            ev_st = P.op("sync", lambda h, r=r: h.dma_start(
                dst[r * 128:(r + 1) * 128, :], buf[:]), waits=[ev_ld], inc="d1", inc_k=16)
        P.wait("sync", [ev_st])
    P.barrier()


def emit_wprep_hilo(nc, P, ps, w_ap, out_hi, out_lo, rows, g, scale_col):
    PTR = [ps.enter_context(nc.psum_tensor(_nm("ptrf"), [128, 512], F32)) for i in range(2)]
    """Stream w [rows, D] fp32 column-slab-wise; PE-transpose; optionally scale
    per-partition by scale_col[:, dc]; split to bf16 hi/lo; store [D, rows]."""
    nec = rows // 128
    raw = ps.enter_context(nc.sbuf_tensor(_nm("wp_raw"), [128, rows], F32))
    hi = ps.enter_context(nc.sbuf_tensor(_nm("wp_hi"), [128, rows], BF16))
    lo = ps.enter_context(nc.sbuf_tensor(_nm("wp_lo"), [128, rows], BF16))
    wsc = ps.enter_context(nc.sbuf_tensor(_nm("wp_sc"), [128, 512], F32))
    ev_out = []
    tg = {}
    gno = [0]
    for dc in range(NDC):
        src = w_ap[:, dc * 128:(dc + 1) * 128].rearrange("(a p) d -> p a d", p=128)
        dst = raw[:].rearrange("p (a d) -> p a d", d=128)
        ev_ld = P.op("sync", lambda h, src=src, dst=dst: h.dma_start(dst, src),
                     waits=ev_out, inc="d1", inc_k=16)
        ngr = (nec + 3) // 4
        ev_sp = None
        for gi in range(ngr):
            n_in = min(4, nec - gi * 4)
            bank = PTR[gno[0] % 2]

            def tr(h, gi=gi, n_in=n_in, bank=bank):
                ins = None
                for k in range(n_in):
                    ec = gi * 4 + k
                    ins = h.transpose(bank[:, k * 128:(k + 1) * 128],
                                      raw[:, ec * 128:(ec + 1) * 128], g["identf"][:])
                return ins
            ev_t = P.op("tensor", tr, waits=[ev_ld, tg.get(gno[0] - 2)], inc="pe")
            w = n_in * 128
            colsl = slice(gi * 512, gi * 512 + w)
            if scale_col is not None:
                ev_s = P.op("vector", lambda h, bank=bank, w=w, dc=dc: h.tensor_scalar(
                    wsc[:, 0:w], bank[:, 0:w], scale_col[:, dc:dc + 1], None, ALU.mult),
                    waits=[ev_t], inc="ve", chain="wprep_sc")
                ev_hi = P.op("scalar", lambda h, w=w, colsl=colsl: h.copy(
                    hi[:, colsl], wsc[:, 0:w]), waits=[ev_s], inc="se")
                ev_lo = P.op("vector", lambda h, w=w, colsl=colsl: h.tensor_sub(
                    lo[:, colsl], wsc[:, 0:w], hi[:, colsl]), waits=[ev_hi], inc="ve",
                    chain="wprep_sc")
            else:
                ev_hi = P.op("scalar", lambda h, bank=bank, w=w, colsl=colsl: h.copy(
                    hi[:, colsl], bank[:, 0:w]), waits=[ev_t], inc="se")
                ev_lo = P.op("vector", lambda h, bank=bank, w=w, colsl=colsl: h.tensor_sub(
                    lo[:, colsl], bank[:, 0:w], hi[:, colsl]), waits=[ev_hi], inc="ve")
            tg[gno[0]] = ev_lo
            gno[0] += 1
            ev_sp = ev_lo
        ev_o1 = P.op("gpsimd", lambda h, dc=dc: h.dma_start(
            out_hi[dc * 128:(dc + 1) * 128, :], hi[:, 0:rows]), waits=[ev_sp], inc="d2", inc_k=16)
        ev_o2 = P.op("gpsimd", lambda h, dc=dc: h.dma_start(
            out_lo[dc * 128:(dc + 1) * 128, :], lo[:, 0:rows]), waits=[ev_sp], inc="d2", inc_k=16)
        ev_out = [ev_o1, ev_o2]
    P.wait("gpsimd", ev_out)


def emit_ffn_means(nc, P, ps, w_gu, w_dn, g):
    PMISC = ps.enter_context(nc.psum_tensor(_nm("pmisc"), [128, 512], F32))
    """mean|w| for gu and dn -> wconst cols (thr, -thr, m)."""
    slab = ps.enter_context(nc.sbuf_tensor("mn_slab", [128, 2048], F32))
    parts = ps.enter_context(nc.sbuf_tensor("mn_parts", [128, 192], F32))
    rowb = ps.enter_context(nc.sbuf_tensor("mn_rows", [1, 256], F32))
    acc = ps.enter_context(nc.sbuf_tensor("mn_acc", [1, 512], F32))
    ch1 = ps.enter_context(nc.sbuf_tensor("mn_c1", [1, 512], F32))
    ch2 = ps.enter_context(nc.sbuf_tensor("mn_c2", [1, 512], F32))
    P.mset("vector", parts[:], 0.0)
    P.mset("vector", acc[:], 1.0)
    P.mset("vector", rowb[:], 0.0)

    specs = [(w_gu, GU * D, 0), (w_dn, D * FF, 1)]
    ev_red = None
    col = 0
    cols_used = [[], []]
    for w_ap, total, mi in specs:
        flat = w_ap.rearrange("a b -> (a b)")
        n_sl = total // (128 * 2048)
        for i in range(n_sl):
            src = flat[i * 128 * 2048:(i + 1) * 128 * 2048].rearrange("(p m) -> p m", p=128)
            ev_ld = P.op("sync", lambda h, src=src: h.dma_start(slab[:], src),
                         waits=[ev_red], inc="d1", inc_k=16)
            ev_red = P.op("vector", lambda h, col=col: h.reduce_sum(
                parts[:, col:col + 1], slab[:], axis=AX.X, apply_absolute_value=True),
                waits=[ev_ld], inc="ve")
            cols_used[mi].append(col)
            col += 1
    ev_p = []
    for mi in range(2):
        c0, c1 = cols_used[mi][0], cols_used[mi][-1] + 1
        ev_p.append(P.op("vector", lambda h, mi=mi, c0=c0, c1=c1: h.reduce_sum(
            parts[:, 184 + mi:185 + mi], parts[:, c0:c1], axis=AX.X), waits=[ev_red], inc="ve"))
    ev_t, ev_r = [], []
    for mi in range(2):
        ev_t.append(P.op("tensor", lambda h, mi=mi: h.transpose(
            PMISC[0:1, mi * 128:(mi + 1) * 128],
            parts[:, 184 + mi:185 + mi], g["identf"][:]),
            waits=[ev_p[mi]] + (ev_r[-1:] if ev_r else []), inc="pe"))
        ev_r.append(P.op("vector", lambda h, mi=mi: h.tensor_copy(
            rowb[0:1, mi * 128:(mi + 1) * 128], PMISC[0:1, mi * 128:(mi + 1) * 128]),
            waits=[ev_t[mi]], inc="ve"))
    ev_s0 = P.op("vector", lambda h: h.reduce_sum(
        acc[0:1, 0:1], rowb[0:1, 0:128], axis=AX.X), waits=ev_r, inc="ve")
    ev_s1 = P.op("vector", lambda h: h.reduce_sum(
        acc[0:1, 256:257], rowb[0:1, 128:256], axis=AX.X), inc="ve")
    P.mset("vector", ch1[:], 1.0)
    P.mset("vector", ch2[:], 1.0)
    Ninv = [1.0 / (GU * D), 1.0 / (D * FF)]
    ev = P.op("vector", lambda h: h.tensor_scalar(
        ch1[0:1, 0:256], acc[0:1, 0:256], Ninv[0], None, ALU.mult),
        waits=[ev_s0, ev_s1], inc="ve")
    ev = P.op("vector", lambda h: h.tensor_scalar(
        ch1[0:1, 256:512], acc[0:1, 256:512], Ninv[1], None, ALU.mult), waits=[ev], inc="ve")
    ev = P.op("vector", lambda h: h.tensor_single_scalar(
        ch2[:], ch1[:], 1e-5, ALU.max), waits=[ev], inc="ve")
    ev = P.op("vector", lambda h: h.tensor_scalar(
        ch1[:], ch2[:], 0.5, None, ALU.mult), waits=[ev], inc="ve")
    items = [(0, ch1, 0), (2, ch2, 0), (3, ch1, 256), (5, ch2, 256)]
    ev_b = []
    for k, (colw, srcbuf, sc) in enumerate(items):
        evm = P.op("tensor", lambda h, k=k, srcbuf=srcbuf, sc=sc: h.matmul(
            PMISC[:, 256 + k:257 + k], g["ones_row"][:], srcbuf[0:1, sc:sc + 1],
            start=True, stop=True),
            waits=[ev] + (ev_b[-1:] if ev_b else []), inc="pe")
        ev_b.append(P.op("vector", lambda h, k=k, colw=colw: h.tensor_copy(
            g["wconst"][:, colw:colw + 1], PMISC[:, 256 + k:257 + k]), waits=[evm], inc="ve"))
    ev = P.op("vector", lambda h: h.tensor_scalar(
        g["wconst"][:, 1:2], g["wconst"][:, 0:1], -1.0, None, ALU.mult), waits=ev_b, inc="ve")
    P.op("vector", lambda h: h.tensor_scalar(
        g["wconst"][:, 4:5], g["wconst"][:, 3:4], -1.0, None, ALU.mult), waits=[ev], inc="ve")


def emit_wprep_quant(nc, P, ps, w_ap, out_q, rows, thr_pos, thr_neg, g, cols):
    PTR = [ps.enter_context(nc.psum_tensor(_nm("ptrb"), [128, 1024], BF16)) for i in range(2)]
    """wq' = sign(w-thr)+sign(w+thr) (natural layout) then bf16 transpose to
    out_q [cols, rows]."""
    nec = rows // 128
    ncc = cols // 128
    raw = ps.enter_context(nc.sbuf_tensor(_nm("wq_raw"), [128, rows], F32))
    sa = ps.enter_context(nc.sbuf_tensor(_nm("wq_sa"), [128, rows], BF16))
    sb2 = ps.enter_context(nc.sbuf_tensor(_nm("wq_sb"), [128, rows], BF16))
    wq = ps.enter_context(nc.sbuf_tensor(_nm("wq_wq"), [128, rows], BF16))
    wqT = ps.enter_context(nc.sbuf_tensor(_nm("wq_wqT"), [128, rows], BF16))
    ev_out = None
    qtg = {}
    qgno = [0]
    for dc in range(ncc):
        src = w_ap[:, dc * 128:(dc + 1) * 128].rearrange("(a p) d -> p a d", p=128)
        dst = raw[:].rearrange("p (a d) -> p a d", d=128)
        ev_ld = P.op("sync", lambda h, src=src, dst=dst: h.dma_start(dst, src),
                     waits=[ev_out], inc="d1", inc_k=16)
        ev_a = P.op("scalar", lambda h: h.activation(sa[:, 0:rows], raw[:, 0:rows],
                                                     AF.Sign, bias=thr_neg),
                    waits=[ev_ld], inc="se")
        ev_b = P.op("scalar", lambda h: h.activation(sb2[:, 0:rows], raw[:, 0:rows],
                                                     AF.Sign, bias=thr_pos),
                    waits=[ev_ld], inc="se")
        ev_w = P.op("vector", lambda h: h.tensor_add(wq[:, 0:rows], sa[:, 0:rows], sb2[:, 0:rows]),
                    waits=[ev_a, ev_b], inc="ve")
        ngr = (nec + 7) // 8
        ev_last = None
        for gi in range(ngr):
            n_in = min(8, nec - gi * 8)
            bank = PTR[(qgno[0] + gi) % 2][:]

            def tr(h, gi=gi, n_in=n_in, bank=bank):
                ins = None
                for k in range(n_in):
                    ec = gi * 8 + k
                    ins = h.transpose(bank[:, k * 128:(k + 1) * 128],
                                      wq[:, ec * 128:(ec + 1) * 128], g["identb"][:])
                return ins
            ev_t = P.op("tensor", tr, waits=[ev_w, qtg.get(qgno[0] + gi - 2)], inc="pe")
            w = n_in * 128
            ev_e = P.op("vector", lambda h, gi=gi, w=w, bank=bank: h.tensor_copy(
                wqT[:, gi * 1024:gi * 1024 + w], bank[:, 0:w]), waits=[ev_t], inc="ve")
            qtg[qgno[0] + gi] = ev_e
            ev_last = ev_e
        qgno[0] += ngr
        ev_out = P.op("gpsimd", lambda h, dc=dc: h.dma_start(
            out_q[dc * 128:(dc + 1) * 128, :], wqT[:, 0:rows]), waits=[ev_last], inc="d2", inc_k=16)
    P.wait("gpsimd", [ev_out])


def emit_ln_chain(P, stats, C_SUM, C_SQS, C_M, C_T, C_RS, waits):
    P.mset("vector", stats[:, C_T:C_T + 64], 1.0)
    P.mset("vector", stats[:, C_T:C_T + 64], 1.0)
    ev = P.op("vector", lambda h: h.tensor_scalar(
        stats[:, C_M:C_M + 64], stats[:, C_SUM:C_SUM + 64], 1.0 / D, None, ALU.mult),
        waits=waits, inc="ve")
    ev = P.op("vector", lambda h: h.tensor_mul(
        stats[:, C_T:C_T + 64], stats[:, C_M:C_M + 64], stats[:, C_M:C_M + 64]),
        waits=[ev], inc="ve")
    ev = P.op("vector", lambda h: h.scalar_tensor_tensor(
        stats[:, C_T:C_T + 64], stats[:, C_SQS:C_SQS + 64], 1.0 / D, stats[:, C_T:C_T + 64],
        ALU.mult, ALU.subtract), waits=[ev], inc="ve")
    ev = P.op("vector", lambda h: h.tensor_scalar(
        stats[:, C_RS:C_RS + 64], stats[:, C_T:C_T + 64], EPS, None, ALU.add),
        waits=[ev], inc="ve")
    ev = P.op("scalar", lambda h: h.sqrt(stats[:, C_T:C_T + 64], stats[:, C_RS:C_RS + 64]),
              waits=[ev], inc="se")
    ev = P.op("vector", lambda h: h.reciprocal(stats[:, C_RS:C_RS + 64], stats[:, C_T:C_T + 64]),
              waits=[ev], inc="ve")
    return ev


def emit_rope(P, buf, base, cs_ap, sn_ap, rtmp, waits, tag="rope"):
    EVN = buf[:, base:base + 1024:2]
    ODD = buf[:, base + 1:base + 1024:2]
    P.op("vector", lambda h: h.tensor_mul(rtmp[:, 0:512], EVN, cs_ap), waits=waits,
         inc="ve", chain=tag)
    P.op("vector", lambda h: h.tensor_mul(rtmp[:, 512:1024], ODD, cs_ap), inc="ve", chain=tag)
    P.op("vector", lambda h: h.scalar_tensor_tensor(
        rtmp[:, 1024:1536], ODD, -1.0, sn_ap, ALU.mult, ALU.mult), inc="ve", chain=tag)
    P.op("vector", lambda h: h.tensor_mul(rtmp[:, 1536:2048], EVN, sn_ap), inc="ve", chain=tag)
    P.op("vector", lambda h: h.tensor_add(EVN, rtmp[:, 0:512], rtmp[:, 1024:1536]),
         inc="ve", chain=tag)
    return P.op("vector", lambda h: h.tensor_add(ODD, rtmp[:, 512:1024], rtmp[:, 1536:2048]),
                inc="ve", chain=tag)


def emit_quant_ladder(P, x_ap, w, sx_col, xq_ap, lad, waits, a_chain=None):
    a, vv, s1, b, r0, r1, sgn = (lad[k] for k in ("a", "vv", "s1", "b", "r0", "r1", "sgn"))
    ev_g = P.op("scalar", lambda h: h.activation(sgn[:, 0:w], x_ap, AF.Sign, scale=sx_col),
                waits=waits, inc="se")
    ev_a = P.op("scalar", lambda h: h.activation(a[:, 0:w], x_ap, AF.Abs, scale=sx_col),
                waits=waits, inc="se", chain=a_chain)
    ev = P.op("vector", lambda h: h.tensor_scalar(
        vv[:, 0:w], a[:, 0:w], 128.0, None, ALU.add), waits=[ev_a], inc="ve")
    ev = P.op("vector", lambda h: h.tensor_scalar(
        s1[:, 0:w], vv[:, 0:w], -128.0, None, ALU.add), waits=[ev], inc="ve")
    r = s1
    rbuf = [r0, r1]
    for i, k in enumerate([64, 32, 16, 8, 4, 2, 1]):
        nxt = rbuf[i % 2]
        ev = P.op("vector", lambda h, r=r, k=k, w=w: h.tensor_scalar(
            b[:, 0:w], r[:, 0:w], float(k) - 0.5, float(-k), ALU.is_ge, ALU.mult),
            waits=[ev], inc="ve")
        ev = P.op("vector", lambda h, r=r, nxt=nxt, w=w: h.tensor_add(
            nxt[:, 0:w], r[:, 0:w], b[:, 0:w]), waits=[ev], inc="ve")
        r = nxt
    ev = P.op("vector", lambda h, r=r, w=w: h.tensor_sub(
        b[:, 0:w], s1[:, 0:w], r[:, 0:w]), waits=[ev], inc="ve")
    return P.op("vector", lambda h, w=w: h.tensor_mul(
        xq_ap, b[:, 0:w], sgn[:, 0:w]), waits=[ev, ev_g], inc="ve")


def emit_sweep1(nc, P, ps, inp, scr, g, s1_jobs=None):
    PMM = [ps.enter_context(nc.psum_tensor(_nm("pmm"), [128, 512], F32)) for i in range(4)]
    PTR = [ps.enter_context(nc.psum_tensor(_nm("ptrf"), [128, 512], F32)) for i in range(2)]
    PA = ps.enter_context(nc.psum_tensor(_nm("papart"), [128, 512], F32))
    xT_hi = ps.enter_context(nc.sbuf_tensor("s1_xT_hi", [128, NDC * TS], BF16))
    xT_lo = ps.enter_context(nc.sbuf_tensor("s1_xT_lo", [128, NDC * TS], BF16))
    kst = ps.enter_context(nc.sbuf_tensor("s1_k", [128, NTT * D], F32))
    vst = ps.enter_context(nc.sbuf_tensor("s1_v", [128, NTT * D], F32))
    xln = ps.enter_context(nc.sbuf_tensor("s1_xln", [128, D], F32))
    qev = [ps.enter_context(nc.sbuf_tensor(f"s1_qe{i}", [128, 512], F32)) for i in range(2)]
    sejunk = ps.enter_context(nc.sbuf_tensor("s1_sejunk", [128, D], F32))
    fmkv = ps.enter_context(nc.sbuf_tensor("s1_fmkv", [128, D], F32))
    cs = ps.enter_context(nc.sbuf_tensor("s1_cs", [128, NTT * 512], F32))
    sn = ps.enter_context(nc.sbuf_tensor("s1_sn", [128, NTT * 512], F32))
    csr = ps.enter_context(nc.sbuf_tensor("s1_csr", [128, 1024], F32))
    rtmp = ps.enter_context(nc.sbuf_tensor("s1_rt", [128, D], F32))
    binb = [ps.enter_context(nc.sbuf_tensor(f"s1_bin{i}", [128, 512], F32)) for i in range(2)]
    wt = [ps.enter_context(nc.sbuf_tensor(f"s1_wt{i}", [128, 512], BF16)) for i in range(4)]
    stats = ps.enter_context(nc.sbuf_tensor("s1_st", [128, 512], F32))
    SUM, SQS, M, TT_, RS, KS, T2, RN = 0, 64, 128, 192, 256, 320, 384, 448
    _ev = P.mset("vector", stats[:], 1.0)
    P.wait("scalar", [_ev])

    n_tiles_total = (NST + NST) * NTT
    tile_no = [0]
    a_rd_prev = [None]
    tg = {}
    gno = [0]
    prev_xln = [None]
    prev_kst_done = [None] * NTT
    prev_cs_done = [None]
    prev_xT_done = [None]
    prev_bias_done = [None, None]
    prev_qe_done = [None, None]
    ev_bank_free = [None] * NTT

    jobs = [("own", s) for s in range(NST)] + [("dup", s) for s in range(NST)]
    if s1_jobs is not None:
        jobs = jobs[:s1_jobs]
    n_tiles_total = len(jobs) * NTT
    for kind, s in jobs:
        x_ap = inp["x"] if kind == "own" else inp["xo"]
        cos_ap = inp["q_cos"] if kind == "own" else inp["o_cos"]
        sin_ap = inp["q_sin"] if kind == "own" else inp["o_sin"]
        oc_list = list(range(12)) if kind == "own" else list(range(4, 12))
        tok0 = s * TS

        ev_x, ev_cs = [], []
        for tt in range(NTT):
            r0 = tok0 + tt * 128
            ev_x.append(P.op("sync", lambda h, tt=tt, r0=r0, x_ap=x_ap: h.dma_start(
                kst[:, tt * D:(tt + 1) * D], x_ap[r0:r0 + 128, :]),
                waits=[prev_kst_done[tt]], inc="d1", inc_k=16))
            evl1 = P.op("sync", lambda h, r0=r0, cos_ap=cos_ap: h.dma_start(
                csr[:], cos_ap[r0:r0 + 128, :]),
                waits=[prev_cs_done[0]], inc="d1", inc_k=16, chain="s1_csr")
            ev_cs.append(P.op("vector", lambda h, tt=tt: h.tensor_add(
                cs[:, tt * 512:(tt + 1) * 512], csr[:, 0:512], csr[:, 512:1024]),
                waits=[evl1], inc="ve", chain="s1_csr"))
            evl2 = P.op("sync", lambda h, r0=r0, sin_ap=sin_ap: h.dma_start(
                csr[:], sin_ap[r0:r0 + 128, :]), waits=[ev_cs[-1]], inc="d1",
                inc_k=16, chain="s1_csr")
            ev_cs.append(P.op("vector", lambda h, tt=tt: h.tensor_add(
                sn[:, tt * 512:(tt + 1) * 512], csr[:, 0:512], csr[:, 512:1024]),
                waits=[evl2], inc="ve", chain="s1_csr"))
        if S1_PART == "dma":
            for tt in range(NTT):
                prev_kst_done[tt] = ev_x[tt]
            continue
        ev_red, ev_sq = [], []
        for tt in range(NTT):
            ev_red.append(P.op("vector", lambda h, tt=tt: h.reduce_sum(
                stats[:, SUM + tt:SUM + tt + 1], kst[:, tt * D:(tt + 1) * D], axis=AX.X),
                waits=[ev_x[tt]], inc="ve"))
            if S1_PART != "nosq":
                ev_sq.append(P.op("scalar", lambda h, tt=tt: h.activation(
                    sejunk[:], kst[:, tt * D:(tt + 1) * D], AF.Square,
                    accum_out=stats[:, SQS + tt:SQS + tt + 1]), waits=[ev_x[tt]], inc="se",
                    chain="s1jnk"))
        if S1_PART == "stats" or S1_PART == "nosq":
            for tt in range(NTT):
                prev_kst_done[tt] = ev_red[tt]
            continue
        ev_ch = emit_ln_chain(P, stats, SUM, SQS, M, TT_, RS, waits=ev_red + ev_sq)

        if S1_PART == "chain":
            for tt in range(NTT):
                prev_kst_done[tt] = ev_red[tt]
            continue
        ev_xT = []
        for tt in range(NTT):
            ev_ap = P.op("vector", lambda h, tt=tt: h.tensor_scalar(
                xln[:], kst[:, tt * D:(tt + 1) * D],
                stats[:, M + tt:M + tt + 1], stats[:, RS + tt:RS + tt + 1],
                ALU.subtract, ALU.mult), waits=[ev_ch, prev_xln[0]], inc="ve")
            if S1_PART == "apply":
                prev_xln[0] = ev_ap
                continue
            ev_last = None
            for gi in range(4):
                bank = PTR[gno[0] % 2]

                def tr(h, gi=gi, bank=bank):
                    ins = None
                    for k in range(4):
                        dc = gi * 4 + k
                        ins = h.transpose(bank[:, k * 128:(k + 1) * 128],
                                          xln[:, dc * 128:(dc + 1) * 128], g["identf"][:])
                    return ins
                wts = [ev_ap, tg.get(gno[0] - 2)]
                if tt == 0 and gi == 0:
                    wts.append(prev_xT_done[0])
                ev_t = P.op("tensor", tr, waits=wts, inc="pe")
                if S1_PART == "trnoev":
                    tg[gno[0]] = ev_t
                    gno[0] += 1
                    prev_xln[0] = ev_t
                    ev_last = ev_t
                    continue
                hi3 = xT_hi[:].rearrange("p (dc t) -> p dc t", t=TS)[
                    :, gi * 4:(gi + 1) * 4, tt * 128:(tt + 1) * 128]
                lo3 = xT_lo[:].rearrange("p (dc t) -> p dc t", t=TS)[
                    :, gi * 4:(gi + 1) * 4, tt * 128:(tt + 1) * 128]
                bk3 = bank[:].rearrange("p (k c) -> p k c", c=128)
                ev_hi = P.op("scalar", lambda h, hi3=hi3, bk3=bk3: h.copy(hi3, bk3),
                             waits=[ev_t], inc="se")
                ev_lo = P.op("vector", lambda h, lo3=lo3, bk3=bk3, hi3=hi3: h.tensor_sub(
                    lo3, bk3, hi3), waits=[ev_hi], inc="ve")
                tg[gno[0]] = ev_lo
                gno[0] += 1
                ev_last = ev_lo
            prev_xln[0] = ev_last
            ev_xT.append(ev_last)
        if S1_PART in ("ln", "apply", "trnoev"):
            for tt in range(NTT):
                prev_kst_done[tt] = ev_red[tt]
            continue

        ev_wu = {}
        wno = [0]
        ev_mm = None
        for oi, oc in enumerate(oc_list):
            bb = oi % 2
            ev_bias = P.op("sync", lambda h, oc=oc, bb=bb: h.dma_start(
                binb[bb][:], inp["b_in_bc"][:, oc * 512:(oc + 1) * 512]),
                waits=[prev_bias_done[bb]], inc="d1", inc_k=16)
            for dc in range(NDC):
                wi = wno[0]
                ridx = (wi % 2) * 2
                ev_w1 = P.op("sync", lambda h, oc=oc, dc=dc, ridx=ridx: h.dma_start(
                    wt[ridx][:], scr["winT_hi"][dc * 128:(dc + 1) * 128, oc * 512:(oc + 1) * 512]),
                    waits=[ev_wu.get(wi - 2)], inc="d1", inc_k=16)
                ev_w2 = P.op("sync", lambda h, oc=oc, dc=dc, ridx=ridx: h.dma_start(
                    wt[ridx + 1][:], scr["winT_lo"][dc * 128:(dc + 1) * 128, oc * 512:(oc + 1) * 512]),
                    inc="d1", inc_k=16)

                def mm(h, dc=dc, ridx=ridx):
                    ins = None
                    for tt in range(NTT):
                        stsl = slice(dc * TS + tt * 128, dc * TS + (tt + 1) * 128)
                        ins = h.matmul(PMM[tt][:], xT_hi[:, stsl], wt[ridx][:],
                                       start=(dc == 0), stop=False)
                        ins = h.matmul(PMM[tt][:], xT_hi[:, stsl], wt[ridx + 1][:],
                                       start=False, stop=False)
                        ins = h.matmul(PMM[tt][:], xT_lo[:, stsl], wt[ridx][:],
                                       start=False, stop=(dc == NDC - 1))
                    return ins
                wts = [ev_w1, ev_w2]
                if dc == 0:
                    wts += ev_xT + ev_bank_free
                ev_mm = P.op("tensor", mm, waits=wts, inc="pe")
                ev_wu[wi] = ev_mm
                wno[0] += 1
            for tt in range(NTT):
                if oc < 4:
                    qb_i = tt % 2
                    ev_e = P.op("vector", lambda h, tt=tt, qb_i=qb_i, bb=bb: h.tensor_add(
                        qev[qb_i][:], PMM[tt][:], binb[bb][:]),
                        waits=[ev_mm, ev_bias, prev_qe_done[qb_i]], inc="ve")
                    r0 = tok0 + tt * 128
                    ev_sp = P.op("gpsimd", lambda h, r0=r0, oc=oc, qb_i=qb_i: h.dma_start(
                        scr["q_sp"][r0:r0 + 128, oc * 512:(oc + 1) * 512], qev[qb_i][:]),
                        waits=[ev_e], inc="d2", inc_k=16)
                    prev_qe_done[qb_i] = ev_sp
                    ev_bank_free[tt] = ev_e
                elif oc < 8:
                    ev_e = P.op("vector", lambda h, tt=tt, oc=oc, bb=bb: h.tensor_add(
                        kst[:, tt * D + (oc - 4) * 512: tt * D + (oc - 3) * 512],
                        PMM[tt][:], binb[bb][:]), waits=[ev_mm, ev_bias], inc="ve")
                    ev_bank_free[tt] = ev_e
                else:
                    ev_e = P.op("vector", lambda h, tt=tt, oc=oc, bb=bb: h.tensor_add(
                        vst[:, tt * D + (oc - 8) * 512: tt * D + (oc - 7) * 512],
                        PMM[tt][:], binb[bb][:]), waits=[ev_mm, ev_bias], inc="ve")
                    ev_bank_free[tt] = ev_e
            prev_bias_done[bb] = ev_bank_free[NTT - 1]
        prev_xT_done[0] = ev_mm
        if S1_PART == "mm":
            for tt in range(NTT):
                prev_kst_done[tt] = ev_bank_free[tt]
            continue

        ev_ks = []
        ev_rope_last = None
        for tt in range(NTT):
            base = tt * D
            ev_r = emit_rope(P, kst, base, cs[:, tt * 512:(tt + 1) * 512],
                             sn[:, tt * 512:(tt + 1) * 512], rtmp,
                             waits=[ev_bank_free[tt]] + ev_cs)
            ev_rope_last = ev_r
            ev_ks.append(P.op("scalar", lambda h, base=base, tt=tt: h.activation(
                sejunk[:], kst[:, base:base + D], AF.Square,
                accum_out=stats[:, KS + tt:KS + tt + 1]), waits=[ev_r], inc="se",
                chain="s1jnk"))
        prev_cs_done[0] = ev_cs[-1] if ev_cs else None
        _ = ev_rope_last
        P.mset("vector", stats[:, T2:T2 + 64], 1.0)
        ev_msk = P.mset("vector", stats[:, T2:T2 + 64], 1.0)
        ev_sd = P.op("scalar", lambda h: h.sqrt(
            stats[:, T2:T2 + 64], stats[:, KS:KS + 64]), waits=ev_ks + [ev_msk], inc="se")
        ev_cl = P.op("vector", lambda h: h.tensor_single_scalar(
            stats[:, TT_:TT_ + 64], stats[:, T2:T2 + 64], 1e-12, ALU.max),
            waits=[ev_sd], inc="ve")
        ev_rn = P.op("vector", lambda h: h.reciprocal(
            stats[:, RN:RN + 64], stats[:, TT_:TT_ + 64]), waits=[ev_cl], inc="ve")
        if S1_PART == "rope":
            for tt in range(NTT):
                prev_kst_done[tt] = ev_ks[tt]
            continue
        ev_am_prev = None
        for tt in range(NTT):
            ksl = slice(tt * D, (tt + 1) * D)
            ev_f = P.op("vector", lambda h, ksl=ksl, tt=tt: h.scalar_tensor_tensor(
                fmkv[:], kst[:, ksl], stats[:, RN + tt:RN + tt + 1], vst[:, ksl],
                ALU.mult, ALU.mult), waits=[ev_rn, ev_am_prev], inc="ve")
            tile_no[0] += 1
            tn = tile_no[0]

            def amm(h):
                ins = None
                for dc in range(NDC):
                    ins = h.matmul(PA[:, dc:dc + 1], fmkv[:, dc * 128:(dc + 1) * 128],
                                   g["ones_col"][:, 0:1],
                                   start=(dc == 0), stop=(dc == NDC - 1),
                                   skip_group_check=True)
                return ins
            ev_am = P.op("tensor", amm, waits=[ev_f, a_rd_prev[0]], inc="pe")
            ev_aa = P.op("vector", lambda h, tn=tn: h.tensor_add(
                g["A_acc"][tn % 2][:, 0:16], g["A_acc"][(tn - 1) % 2][:, 0:16],
                PA[:, 0:16]), waits=[ev_am], inc="ve")
            a_rd_prev[0] = ev_aa
            ev_am_prev = ev_am
            prev_kst_done[tt] = ev_am


def emit_sweep2(nc, P, ps, inp, scr, out, g):
    PMM = [ps.enter_context(nc.psum_tensor(_nm("pmm"), [128, 512], F32)) for i in range(4)]
    PTR = [ps.enter_context(nc.psum_tensor(_nm("ptrf"), [128, 512], F32)) for i in range(2)]
    PTB = [ps.enter_context(nc.psum_tensor(_nm("ptrb"), [128, 1024], BF16)) for i in range(2)]
    xst = ps.enter_context(nc.sbuf_tensor("s2_xst", [128, NTT * D], F32))
    att = ps.enter_context(nc.sbuf_tensor("s2_att", [128, NTT * D], F32))
    tmpA = ps.enter_context(nc.sbuf_tensor("s2_tmpA", [128, D], F32))
    tmpB = ps.enter_context(nc.sbuf_tensor("s2_tmpB", [128, D], F32))
    ghold = ps.enter_context(nc.sbuf_tensor("s2_gh", [128, NTT * 512], F32))
    uh = ps.enter_context(nc.sbuf_tensor("s2_uh", [128, 512], F32))
    sqg = ps.enter_context(nc.sbuf_tensor("s2_sqg", [128, 512], F32))
    hbuf = [ps.enter_context(nc.sbuf_tensor(f"s2_hb{i}", [128, 512], F32)) for i in range(2)]
    cs = [ps.enter_context(nc.sbuf_tensor("s2_cs0", [128, 512], F32))]
    sn = [ps.enter_context(nc.sbuf_tensor("s2_sn0", [128, 512], F32))]
    bout = [ps.enter_context(nc.sbuf_tensor(f"s2_bo{i}", [128, 512], F32)) for i in range(2)]
    nwb = ps.enter_context(nc.sbuf_tensor("s2_nw", [128, D], F32))
    stats = ps.enter_context(nc.sbuf_tensor("s2_st", [128, 768], F32))
    qT_hi = ps.enter_context(nc.sbuf_tensor("s2_qTh", [128, NDC * TS], BF16))
    qT_lo = ps.enter_context(nc.sbuf_tensor("s2_qTl", [128, NDC * TS], BF16))
    hqA = ps.enter_context(nc.sbuf_tensor("s2_hqA", [128, 32 * TS], BF16))
    hq = ps.enter_context(nc.sbuf_tensor("s2_hq", [128, FF], BF16))
    vvb = ps.enter_context(nc.sbuf_tensor("s2_vv", [128, D], BF16))
    sgnb = ps.enter_context(nc.sbuf_tensor("s2_sgn", [128, D], BF16))
    wt = [ps.enter_context(nc.sbuf_tensor(f"s2_wt{i}", [128, 512], BF16)) for i in range(4)]

    sejunk = tmpB[:]
    gx4 = ghold[:, 0:D]
    lad = {"a": tmpB, "vv": vvb, "s1": xst[:, 0:D], "b": xst[:, D:2 * D],
           "r0": xst[:, 2 * D:3 * D], "r1": xst[:, 3 * D:4 * D], "sgn": sgnb}

    def hqT_ap(fc, c0, c1):
        if fc < 32:
            base = fc * TS
            return hqA[:, base + c0:base + c1]
        base = (fc - 32) * TS
        return qT_lo[:, base + c0:base + c1]

    ffn = xst

    SUM1, SQS1, M1, T1_, RS1 = 0, 64, 128, 192, 256
    SUM2, SQS2, M2, RS2 = 320, 384, 448, 512
    AMX, SX, SCG = 576, 608, 640
    HM = 704
    _ev1 = P.mset("vector", stats[:, 0:512], 1.0)
    _ev2 = P.mset("vector", stats[:, 512:768], 1.0)
    P.wait("scalar", [_ev1, _ev2])
    ev_nw = P.op("sync", lambda h: h.dma_start(nwb[:], inp["nw_bc"]), inc="d1", inc_k=16)

    prev_stage_done = [None] * NTT
    prev_cs_done = [None]
    prev_qT_done = [None]
    prev_bo_done = [None, None]
    prev_e2 = [None]
    ev_bank_free = [None] * NTT
    ev_wu = {}
    wno = [0]
    tg = {}
    gno = [0]

    def wload(src_ap, r0, c0, waits=()):
        wi = wno[0]
        ridx = wi % 4
        ev = P.op("sync", lambda h, ridx=ridx: h.dma_start(
            wt[ridx][:], src_ap[r0:r0 + 128, c0:c0 + 512]),
            waits=[ev_wu.get(wi - 4)] + list(waits), inc="d1", inc_k=16)
        wno[0] += 1
        return wi, ridx, ev

    for s in range(NST):
        tok0 = s * TS
        ev_x, ev_q = [], []
        for tt in range(NTT):
            r0 = tok0 + tt * 128
            ev_x.append(P.op("sync", lambda h, tt=tt, r0=r0: h.dma_start(
                xst[:, tt * D:(tt + 1) * D], inp["x"][r0:r0 + 128, :]),
                waits=[prev_stage_done[tt]], inc="d1", inc_k=16))
            ev_q.append(P.op("sync", lambda h, tt=tt, r0=r0: h.dma_start(
                att[:, tt * D:(tt + 1) * D], scr["q_sp"][r0:r0 + 128, :]),
                waits=[prev_stage_done[tt]], inc="d1", inc_k=16))
        ev_red, ev_sq = [], []
        for tt in range(NTT):
            ev_red.append(P.op("vector", lambda h, tt=tt: h.reduce_sum(
                stats[:, SUM1 + tt:SUM1 + tt + 1], xst[:, tt * D:(tt + 1) * D], axis=AX.X),
                waits=[ev_x[tt]], inc="ve"))
            ev_sq.append(P.op("scalar", lambda h, tt=tt: h.activation(
                sejunk, xst[:, tt * D:(tt + 1) * D], AF.Square,
                accum_out=stats[:, SQS1 + tt:SQS1 + tt + 1]), waits=[ev_x[tt]], inc="se",
                chain="s2_tmpB"))
        ev_ch1 = emit_ln_chain(P, stats, SUM1, SQS1, M1, T1_, RS1, waits=ev_red + ev_sq)
        ev_qT = []
        for tt in range(NTT):
            base = tt * D
            pp = 0
            r0 = tok0 + tt * 128
            evl1 = P.op("sync", lambda h, r0=r0: h.dma_start(
                tmpA[:, 0:1024], inp["q_cos"][r0:r0 + 128, :]),
                waits=[prev_cs_done[pp]], inc="d1", inc_k=16, chain="s2_tmpA")
            evl2 = P.op("sync", lambda h, r0=r0: h.dma_start(
                tmpA[:, 1024:2048], inp["q_sin"][r0:r0 + 128, :]), inc="d1",
                inc_k=16, chain="s2_tmpA")
            evc = P.op("vector", lambda h, pp=pp: h.tensor_add(
                cs[pp][:], tmpA[:, 0:512], tmpA[:, 512:1024]), waits=[evl1],
                inc="ve", chain="s2_tmpA")
            evs_ = P.op("vector", lambda h, pp=pp: h.tensor_add(
                sn[pp][:], tmpA[:, 1024:1536], tmpA[:, 1536:2048]), waits=[evl2],
                inc="ve", chain="s2_tmpA")
            ev_r = emit_rope(P, att, base, cs[pp][:], sn[pp][:], tmpA,
                             waits=[ev_q[tt], evc, evs_], tag="s2_tmpA")
            prev_cs_done[pp] = ev_r
            ev_last = None
            for gi in range(4):
                bank = PTR[gno[0] % 2]

                def tr(h, gi=gi, bank=bank, base=base):
                    ins = None
                    for k in range(4):
                        dc = gi * 4 + k
                        ins = h.transpose(bank[:, k * 128:(k + 1) * 128],
                                          att[:, base + dc * 128:base + (dc + 1) * 128],
                                          g["identf"][:])
                    return ins
                ev_t = P.op("tensor", tr, waits=[ev_r, tg.get(gno[0] - 2)], inc="pe")
                hi3 = qT_hi[:].rearrange("p (dc t) -> p dc t", t=TS)[
                    :, gi * 4:(gi + 1) * 4, tt * 128:(tt + 1) * 128]
                lo3 = qT_lo[:].rearrange("p (dc t) -> p dc t", t=TS)[
                    :, gi * 4:(gi + 1) * 4, tt * 128:(tt + 1) * 128]
                bk3 = bank[:].rearrange("p (k c) -> p k c", c=128)
                ev_hi = P.op("scalar", lambda h, hi3=hi3, bk3=bk3: h.copy(hi3, bk3),
                             waits=[ev_t], inc="se")
                ev_lo = P.op("vector", lambda h, lo3=lo3, bk3=bk3, hi3=hi3: h.tensor_sub(
                    lo3, bk3, hi3), waits=[ev_hi], inc="ve")
                tg[gno[0]] = ev_lo
                gno[0] += 1
                ev_last = ev_lo
            ev_qT.append(ev_last)
        ev_mm = None
        for oc2 in range(4):
            bb = oc2 % 2
            ev_bo = P.op("sync", lambda h, oc2=oc2, bb=bb: h.dma_start(
                bout[bb][:], inp["b_out_bc"][:, oc2 * 512:(oc2 + 1) * 512]),
                waits=[prev_bo_done[bb]], inc="d1", inc_k=16)
            for dc in range(NDC):
                wi1, r1_, ev_w1 = wload(scr["woutT_hi"], dc * 128, oc2 * 512)
                wi2, r2_, ev_w2 = wload(scr["woutT_lo"], dc * 128, oc2 * 512)

                def mm(h, dc=dc, r1_=r1_, r2_=r2_):
                    ins = None
                    for tt in range(NTT):
                        stsl = slice(dc * TS + tt * 128, dc * TS + (tt + 1) * 128)
                        ins = h.matmul(PMM[tt][:], qT_hi[:, stsl], wt[r1_][:],
                                       start=(dc == 0), stop=False)
                        ins = h.matmul(PMM[tt][:], qT_hi[:, stsl], wt[r2_][:],
                                       start=False, stop=False)
                        ins = h.matmul(PMM[tt][:], qT_lo[:, stsl], wt[r1_][:],
                                       start=False, stop=(dc == NDC - 1))
                    return ins
                wts = [ev_w1, ev_w2]
                if dc == 0:
                    wts += ev_qT + ev_bank_free
                ev_mm = P.op("tensor", mm, waits=wts, inc="pe")
                ev_wu[wi1] = ev_mm
                ev_wu[wi2] = ev_mm
            for tt in range(NTT):
                csl = slice(oc2 * 512, (oc2 + 1) * 512)
                ev_e1 = P.op("vector", lambda h, tt=tt, bb=bb: h.tensor_add(
                    uh[:], PMM[tt][:], bout[bb][:]),
                    waits=[ev_mm, ev_bo, prev_e2[0]], inc="ve", chain="s2_uh")
                ev_t1 = P.op("vector", lambda h, tt=tt, csl=csl: h.tensor_scalar(
                    tmpA[:, 0:512], xst[:, tt * D + csl.start:tt * D + csl.stop],
                    stats[:, M1 + tt:M1 + tt + 1], stats[:, RS1 + tt:RS1 + tt + 1],
                    ALU.subtract, ALU.mult), waits=[ev_ch1], inc="ve", chain="s2_tmpA")
                ev_e2 = P.op("vector", lambda h, tt=tt, csl=csl: h.tensor_add(
                    att[:, tt * D + csl.start:tt * D + csl.stop], uh[:], tmpA[:, 0:512]),
                    waits=[ev_e1, ev_t1], inc="ve", chain="s2_tmpA")
                prev_e2[0] = ev_e2
                ev_bank_free[tt] = ev_e2
            prev_bo_done[bb] = ev_bank_free[NTT - 1]
        prev_qT_done[0] = ev_mm
        ev_red2, ev_sq2 = [], []
        for tt in range(NTT):
            ev_red2.append(P.op("vector", lambda h, tt=tt: h.reduce_sum(
                stats[:, SUM2 + tt:SUM2 + tt + 1], att[:, tt * D:(tt + 1) * D], axis=AX.X),
                waits=[ev_bank_free[tt]], inc="ve"))
            ev_sq2.append(P.op("scalar", lambda h, tt=tt: h.activation(
                sejunk, att[:, tt * D:(tt + 1) * D], AF.Square,
                accum_out=stats[:, SQS2 + tt:SQS2 + tt + 1]),
                waits=[ev_bank_free[tt]], inc="se", chain="s2_tmpB"))
        ev_ch2 = emit_ln_chain(P, stats, SUM2, SQS2, M2, T1_, RS2, waits=ev_red2 + ev_sq2)
        ev_x2 = []
        ev_sq3 = []
        for tt in range(NTT):
            tsl = slice(tt * D, (tt + 1) * D)
            ev_a = P.op("vector", lambda h, tsl=tsl, tt=tt: h.tensor_scalar(
                tmpA[:], att[:, tsl], stats[:, M2 + tt:M2 + tt + 1],
                stats[:, RS2 + tt:RS2 + tt + 1], ALU.subtract, ALU.mult),
                waits=[ev_ch2], inc="ve", chain="s2_tmpA")
            ev_b = P.op("vector", lambda h, tsl=tsl, tt=tt: h.scalar_tensor_tensor(
                att[:, tsl], tmpA[:], 1.0, xst[:, tsl], ALU.mult, ALU.add,
                accum_out=stats[:, SUM1 + tt:SUM1 + tt + 1]), waits=[ev_a], inc="ve",
                chain="s2_tmpA")
            ev_x2.append(ev_b)
            ev_sq3.append(P.op("scalar", lambda h, tsl=tsl, tt=tt: h.activation(
                sejunk, att[:, tsl], AF.Square,
                accum_out=stats[:, SQS1 + tt:SQS1 + tt + 1]), waits=[ev_b], inc="se",
                chain="s2_tmpB"))
        ev_ch3 = emit_ln_chain(P, stats, SUM1, SQS1, M1, T1_, RS1, waits=ev_x2 + ev_sq3)
        ev = P.op("vector", lambda h: h.tensor_mul(
            stats[:, T1_:T1_ + 64], stats[:, M1:M1 + 64], stats[:, M1:M1 + 64]),
            waits=[ev_ch3], inc="ve")
        ev = P.op("vector", lambda h: h.scalar_tensor_tensor(
            stats[:, T1_:T1_ + 64], stats[:, SQS1:SQS1 + 64], 1.0 / D, stats[:, T1_:T1_ + 64],
            ALU.mult, ALU.subtract), waits=[ev], inc="ve")
        ev = P.op("vector", lambda h: h.tensor_mul(
            stats[:, SUM2:SUM2 + 64], stats[:, RS1:RS1 + 64], stats[:, RS1:RS1 + 64]),
            waits=[ev], inc="ve")
        ev = P.op("vector", lambda h: h.tensor_mul(
            stats[:, T1_:T1_ + 64], stats[:, T1_:T1_ + 64], stats[:, SUM2:SUM2 + 64]),
            waits=[ev], inc="ve")
        ev = P.op("vector", lambda h: h.tensor_scalar(
            stats[:, SQS2:SQS2 + 64], stats[:, T1_:T1_ + 64], EPS, None, ALU.add),
            waits=[ev], inc="ve")
        ev = P.op("scalar", lambda h: h.sqrt(
            stats[:, T1_:T1_ + 64], stats[:, SQS2:SQS2 + 64]), waits=[ev], inc="se")
        ev = P.op("vector", lambda h: h.reciprocal(
            stats[:, SQS2:SQS2 + 64], stats[:, T1_:T1_ + 64]), waits=[ev], inc="ve")
        ev_c3 = P.op("vector", lambda h: h.tensor_mul(
            stats[:, RS2:RS2 + 64], stats[:, RS1:RS1 + 64], stats[:, SQS2:SQS2 + 64]),
            waits=[ev], inc="ve")
        ev_xqT = []
        ev_lad_prev = None
        for tt in range(NTT):
            tsl = slice(tt * D, (tt + 1) * D)
            ev_p1 = P.op("vector", lambda h, tsl=tsl, tt=tt: h.tensor_scalar(
                tmpA[:], att[:, tsl], stats[:, M1 + tt:M1 + tt + 1],
                stats[:, RS2 + tt:RS2 + tt + 1], ALU.subtract, ALU.mult),
                waits=[ev_c3, ev_lad_prev], inc="ve", chain="s2_tmpA")
            ev_x4 = P.op("vector", lambda h: h.tensor_mul(
                gx4, tmpA[:], nwb[:]), waits=[ev_p1, ev_nw], inc="ve", chain="s2_tmpA")
            ev_mx = P.op("vector", lambda h, tt=tt: h.reduce_max(
                stats[:, AMX + tt:AMX + tt + 1], gx4, axis=AX.X,
                apply_absolute_value=True), waits=[ev_x4], inc="ve")
            P.mset("vector", stats[:, T1_:T1_ + 64], 1.0)
            ev = P.op("vector", lambda h: h.tensor_single_scalar(
                stats[:, T1_:T1_ + 64], stats[:, AMX:AMX + 64], 1e-5, ALU.max),
                waits=[ev_mx], inc="ve")
            ev = P.op("vector", lambda h: h.reciprocal(
                stats[:, SUM2:SUM2 + 64], stats[:, T1_:T1_ + 64]), waits=[ev], inc="ve")
            ev_sx = P.op("vector", lambda h: h.tensor_scalar(
                stats[:, SX:SX + 64], stats[:, SUM2:SUM2 + 64], 127.0, None, ALU.mult),
                waits=[ev], inc="ve")
            ev_sc = P.op("vector", lambda h: h.tensor_scalar(
                stats[:, SCG:SCG + 64], stats[:, T1_:T1_ + 64], g["wconst"][:, 2:3],
                0.5 / 127.0, ALU.mult, ALU.mult), waits=[ev_sx], inc="ve")
            ev_lq = emit_quant_ladder(P, gx4, D, stats[:, SX + tt:SX + tt + 1],
                                      hq[:, 0:D], lad, waits=[ev_sx, ev_lad_prev],
                                      a_chain="s2_tmpB")
            ev_last = None
            for gi in range(4):
                bank = PTB[gno[0] % 2]

                def tr(h, gi=gi, bank=bank):
                    ins = None
                    for k in range(4):
                        dc = gi * 4 + k
                        ins = h.transpose(bank[:, k * 128:(k + 1) * 128],
                                          hq[:, dc * 128:(dc + 1) * 128], g["identb"][:])
                    return ins
                ev_t = P.op("tensor", tr, waits=[ev_lq, tg.get(gno[0] - 2)], inc="pe")
                dst3 = qT_hi[:].rearrange("p (dc t) -> p dc t", t=TS)[
                    :, gi * 4:(gi + 1) * 4, tt * 128:(tt + 1) * 128]
                bk3 = bank[:].rearrange("p (k c) -> p k c", c=128)[:, 0:4, :]
                ev_e = P.op("vector", lambda h, dst3=dst3, bk3=bk3: h.tensor_copy(dst3, bk3),
                            waits=[ev_t], inc="ve")
                tg[gno[0]] = ev_e
                gno[0] += 1
                ev_last = ev_e
            ev_lad_prev = ev_last
            ev_xqT.append(ev_last)
        ev_h_sp = {}
        ev_hb_guard = [None, None]
        ev_hmax = []
        for j in range(11):
            for phase, oc3 in (("g", j), ("u", 11 + j)):
                for dc in range(NDC):
                    wi1, r1_, ev_w1 = wload(scr["guT_q"], dc * 128, oc3 * 512)

                    def mm(h, dc=dc, r1_=r1_):
                        ins = None
                        for tt in range(NTT):
                            stsl = slice(dc * TS + tt * 128, dc * TS + (tt + 1) * 128)
                            ins = h.matmul(PMM[tt][:], qT_hi[:, stsl], wt[r1_][:],
                                           start=(dc == 0), stop=(dc == NDC - 1))
                        return ins
                    wts = [ev_w1]
                    if dc == 0:
                        wts += ev_xqT + ev_bank_free
                    ev_mm = P.op("tensor", mm, waits=wts, inc="pe")
                    ev_wu[wi1] = ev_mm
                for tt in range(NTT):
                    if phase == "g":
                        ev_e = P.op("vector", lambda h, tt=tt: h.tensor_scalar(
                            ghold[:, tt * 512:(tt + 1) * 512], PMM[tt][:],
                            stats[:, SCG + tt:SCG + tt + 1], 0.0, ALU.mult, ALU.max),
                            waits=[ev_mm], inc="ve", chain="s2_gx4")
                        ev_bank_free[tt] = ev_e
                    else:
                        ev_u = P.op("vector", lambda h, tt=tt: h.tensor_scalar(
                            uh[:], PMM[tt][:], stats[:, SCG + tt:SCG + tt + 1], None,
                            ALU.mult), waits=[ev_mm], inc="ve", chain="s2_uh")
                        ev_s = P.op("scalar", lambda h, tt=tt: h.square(
                            sqg[:], ghold[:, tt * 512:(tt + 1) * 512]),
                            waits=[ev_bank_free[tt], ev_h_sp.get(("hh_prev",))], inc="se",
                            chain="sqg")
                        hb = hbuf[tt % 2]
                        ev_hh = P.op("vector", lambda h, hb=hb: h.tensor_mul(
                            hb[:], sqg[:], uh[:]),
                            waits=[ev_u, ev_s, ev_hb_guard[tt % 2]], inc="ve",
                            chain="s2_uh")
                        ev_h_sp[("hh_prev",)] = ev_hh
                        ev_hm = P.op("vector", lambda h, tt=tt, j=j, hb=hb: h.reduce_max(
                            stats[:, HM + tt * 16 + j:HM + tt * 16 + j + 1], hb[:],
                            axis=AX.X, apply_absolute_value=True), waits=[ev_hh], inc="ve")
                        ev_hmax.append(ev_hm)
                        ev_d = P.op("gpsimd", lambda h, tt=tt, j=j, hb=hb: h.dma_start(
                            scr["h_sp"][tt * 128:(tt + 1) * 128, j * 512:(j + 1) * 512], hb[:]),
                            waits=[ev_hh], inc="d2", inc_k=16)
                        ev_h_sp[(j, tt)] = ev_d
                        ev_hb_guard[tt % 2] = ev_d
                        ev_bank_free[tt] = ev_hh
        ev_hqT = [None] * NTT
        ev_lad2 = None
        for tt in range(NTT):
            ev_am = P.op("vector", lambda h, tt=tt: h.reduce_max(
                stats[:, AMX + tt:AMX + tt + 1],
                stats[:, HM + tt * 16:HM + tt * 16 + 11], axis=AX.X), waits=ev_hmax, inc="ve")
            P.mset("vector", stats[:, T1_:T1_ + 64], 1.0)
            ev = P.op("vector", lambda h: h.tensor_single_scalar(
                stats[:, T1_:T1_ + 64], stats[:, AMX:AMX + 64], 1e-5, ALU.max),
                waits=[ev_am], inc="ve")
            ev = P.op("vector", lambda h: h.reciprocal(
                stats[:, SUM2:SUM2 + 64], stats[:, T1_:T1_ + 64]), waits=[ev], inc="ve")
            ev_sx2 = P.op("vector", lambda h: h.tensor_scalar(
                stats[:, SX:SX + 64], stats[:, SUM2:SUM2 + 64], 127.0, None, ALU.mult),
                waits=[ev], inc="ve")
            ev_sc2 = P.op("vector", lambda h: h.tensor_scalar(
                stats[:, SCG:SCG + 64], stats[:, T1_:T1_ + 64], g["wconst"][:, 5:6],
                0.5 / 127.0, ALU.mult, ALU.mult), waits=[ev_sx2], inc="ve")
            for ci, (c0, cw) in enumerate([(0, 2048), (2048, 2048), (4096, 1536)]):
                ev_ld = P.op("sync", lambda h, tt=tt, c0=c0, cw=cw: h.dma_start(
                    tmpA[:, 0:cw], scr["h_sp"][tt * 128:(tt + 1) * 128, c0:c0 + cw]),
                    waits=[ev_h_sp.get((10, tt)), ev_lad2], inc="d1", inc_k=16,
                    chain="s2_tmpA")
                ev_lad2 = emit_quant_ladder(
                    P, tmpA[:, 0:cw], cw, stats[:, SX + tt:SX + tt + 1],
                    hq[:, c0:c0 + cw], lad,
                    waits=[ev_ld, ev_sx2] + (ev_xqT if (tt == 0 and ci == 0) else []),
                    a_chain="s2_tmpB")
            ev_last = None
            for gi in range(6):
                n_in = min(8, NFC - gi * 8)
                bank = PTB[gno[0] % 2]

                def tr(h, gi=gi, n_in=n_in, bank=bank):
                    ins = None
                    for k in range(n_in):
                        fc = gi * 8 + k
                        ins = h.transpose(bank[:, k * 128:(k + 1) * 128],
                                          hq[:, fc * 128:(fc + 1) * 128], g["identb"][:])
                    return ins
                ev_t = P.op("tensor", tr, waits=[ev_lad2, tg.get(gno[0] - 2)], inc="pe")
                arena = hqA[:] if gi < 4 else qT_lo[:]
                fc0 = gi * 8 if gi < 4 else gi * 8 - 32
                dst3 = arena.rearrange("p (fc t) -> p fc t", t=TS)[
                    :, fc0:fc0 + n_in, tt * 128:(tt + 1) * 128]
                bk3 = bank[:].rearrange("p (k c) -> p k c", c=128)[:, 0:n_in, :]
                ev_e = P.op("vector", lambda h, dst3=dst3, bk3=bk3: h.tensor_copy(dst3, bk3),
                            waits=[ev_t], inc="ve")
                tg[gno[0]] = ev_e
                gno[0] += 1
                ev_last = ev_e
            ev_lad2 = ev_last
            ev_hqT[tt] = ev_last
        for oc4 in range(4):
            for fc in range(NFC):
                wi1, r1_, ev_w1 = wload(scr["dnT_q"], fc * 128, oc4 * 512)

                def mm(h, fc=fc, r1_=r1_):
                    ins = None
                    for tt in range(NTT):
                        ins = h.matmul(PMM[tt][:], hqT_ap(fc, tt * 128, (tt + 1) * 128),
                                       wt[r1_][:], start=(fc == 0), stop=(fc == NFC - 1))
                    return ins
                wts = [ev_w1]
                if fc == 0:
                    wts += ev_hqT + ev_bank_free
                ev_mm = P.op("tensor", mm, waits=wts, inc="pe")
                ev_wu[wi1] = ev_mm
            for tt in range(NTT):
                csl = slice(oc4 * 512, (oc4 + 1) * 512)
                ev_e = P.op("vector", lambda h, tt=tt, csl=csl: h.tensor_scalar(
                    ffn[:, tt * D + csl.start: tt * D + csl.stop], PMM[tt][:],
                    stats[:, SCG + tt:SCG + tt + 1], None, ALU.mult),
                    waits=[ev_mm], inc="ve")
                ev_bank_free[tt] = ev_e
        ev_red4, ev_sq4 = [], []
        for tt in range(NTT):
            ev_red4.append(P.op("vector", lambda h, tt=tt: h.reduce_sum(
                stats[:, SUM1 + tt:SUM1 + tt + 1], ffn[:, tt * D:(tt + 1) * D], axis=AX.X),
                waits=[ev_bank_free[tt]], inc="ve"))
            ev_sq4.append(P.op("scalar", lambda h, tt=tt: h.activation(
                sejunk, ffn[:, tt * D:(tt + 1) * D], AF.Square,
                accum_out=stats[:, SQS1 + tt:SQS1 + tt + 1]),
                waits=[ev_bank_free[tt]], inc="se", chain="s2_tmpB"))
        ev_ch4 = emit_ln_chain(P, stats, SUM1, SQS1, M1, T1_, RS1, waits=ev_red4 + ev_sq4)
        prev_st = None
        prev_o = None
        for tt in range(NTT):
            tsl = slice(tt * D, (tt + 1) * D)
            ev_a = P.op("vector", lambda h, tsl=tsl, tt=tt: h.tensor_scalar(
                tmpA[:], ffn[:, tsl], stats[:, M1 + tt:M1 + tt + 1],
                stats[:, RS1 + tt:RS1 + tt + 1], ALU.subtract, ALU.mult),
                waits=[ev_ch4, prev_o], inc="ve", chain="s2_tmpA")
            ev_o = P.op("vector", lambda h, tsl=tsl: h.tensor_add(
                tmpB[:], tmpA[:], att[:, tsl]), waits=[ev_a, prev_st], inc="ve",
                chain="s2_tmpB")
            prev_o = ev_o
            r0 = tok0 + tt * 128
            ev_st = P.op("gpsimd", lambda h, r0=r0: h.dma_start(
                out[r0:r0 + 128, :], tmpB[:]), waits=[ev_o], inc="d2", inc_k=16,
                chain="s2_tmpB")
            prev_st = ev_st
            prev_stage_done[tt] = ev_st
    P.wait("gpsimd", [prev_stage_done[tt] for tt in range(NTT)])


def kernel(L0_attn_X, L0_attn_W_in, L0_attn_b_in, L0_attn_W_out, L0_attn_b_out,
           L0_ffn_w_gate_up, L0_ffn_w_down, L0_ffn_norm_weight):
    X = np.ascontiguousarray(np.asarray(L0_attn_X, dtype=np.float32))
    W_in = np.ascontiguousarray(np.asarray(L0_attn_W_in, dtype=np.float32))
    b_in = np.asarray(L0_attn_b_in, dtype=np.float32)
    W_out = np.ascontiguousarray(np.asarray(L0_attn_W_out, dtype=np.float32))
    b_out = np.asarray(L0_attn_b_out, dtype=np.float32)
    w_gu = np.ascontiguousarray(np.asarray(L0_ffn_w_gate_up, dtype=np.float32))
    w_dn = np.ascontiguousarray(np.asarray(L0_ffn_w_down, dtype=np.float32))
    nw = np.asarray(L0_ffn_norm_weight, dtype=np.float32)
    return _run(X, W_in, b_in, W_out, b_out, w_gu, w_dn, nw, build_program())


def _run(X, W_in, b_in, W_out, b_out, w_gu, w_dn, nw, nc, extra_keys=()):
    b_in_bc = np.ascontiguousarray(np.broadcast_to(b_in[None, :], (128, E)))
    b_out_bc = np.ascontiguousarray(np.broadcast_to(b_out[None, :], (128, D)))
    nw_bc = np.ascontiguousarray(np.broadcast_to(nw[None, :], (128, D)))
    in_maps = []
    for c in range(8):
        b, half = c // 2, c % 2
        qc, qs = _rope_tables(np.arange(half * T, (half + 1) * T))
        oc_, os_ = _rope_tables(np.arange((1 - half) * T, (2 - half) * T))
        in_maps.append({
            "x": np.ascontiguousarray(X[b, half * T:(half + 1) * T, :]),
            "xo": np.ascontiguousarray(X[b, (1 - half) * T:(2 - half) * T, :]),
            "q_cos": qc, "q_sin": qs, "o_cos": oc_, "o_sin": os_,
            "w_in": W_in, "w_out": W_out, "w_gu": w_gu, "w_dn": w_dn,
            "b_in_bc": b_in_bc, "b_out_bc": b_out_bc, "nw_bc": nw_bc,
        })
    res = run_bass_kernel_spmd(nc, in_maps, core_ids=list(range(8)))
    full = np.empty((4, 4096, D), dtype=np.float32)
    for c in range(8):
        b, half = c // 2, c % 2
        full[b, half * T:(half + 1) * T, :] = res.results[c]["out"]
    if extra_keys:
        return full, res
    return full


# version 05
# speedup vs baseline: 1.0380x; 1.0380x over previous
import contextlib
import numpy as np

import concourse.bass as bass
import concourse.mybir as mybir
from concourse.bass_utils import run_bass_kernel_spmd

F32 = mybir.dt.float32
BF16 = mybir.dt.bfloat16
ALU = mybir.AluOpType
AF = mybir.ActivationFunctionType
AX = mybir.AxisListType

D = 2048
E = 6144
FF = 5632
GU = 11264
T = 2048
S1_PART = "full"
TS = 512
NTT = TS // 128
NST = T // TS
NDC = D // 128
NFC = FF // 128
ROPE_THETA = 10000.0
EPS = 1e-5


class Ev:
    __slots__ = ("sem", "val")

    def __init__(self, sem, val):
        self.sem = sem
        self.val = val


class SemCtr:
    def __init__(self, handle):
        self.h = handle
        self.n = 0

    def next(self, k):
        self.n += k
        return Ev(self, self.n)


class Prog:
    ENGS = ("sync", "gpsimd", "vector", "scalar", "tensor")

    def __init__(self, nc, st):
        self.nc = nc
        self.items = {e: [] for e in self.ENGS}
        self.sems = {}
        for name in ("pe", "ve", "se", "bar"):
            self.sems[name] = SemCtr(st.enter_context(nc.semaphore("s_" + name)))
        self.pools = {"d1": [], "d2": []}
        for name, n in (("d1", 6), ("d2", 4)):
            for i in range(n):
                s = SemCtr(st.enter_context(nc.semaphore(f"s_{name}_{i}")))
                self.sems[f"{name}_{i}"] = s
                self.pools[name].append(s)
        self.rr = {"d1": 0, "d2": 0}
        self.chains = {}

    def op(self, eng, fn, waits=(), inc=None, inc_k=1, chain=None):
        ws = [w for w in waits if w is not None]
        if chain is not None and chain in self.chains:
            ws.append(self.chains[chain])
        if inc in ("d1", "d2"):
            sem = self.pools[inc][self.rr[inc] % len(self.pools[inc])]
            self.rr[inc] += 1
            ws.append(Ev(sem, sem.n))
            ev = sem.next(inc_k)
        else:
            ev = self.sems[inc].next(inc_k) if inc else None

        def run(h, ws=ws, ev=ev, fn=fn, inc_k=inc_k):
            for w in ws:
                h.wait_ge(w.sem.h, w.val)
            ins = fn(h)
            if ev is not None:
                ins.then_inc(ev.sem.h, inc_k)
        self.items[eng].append(run)
        if chain is not None:
            assert ev is not None
            self.chains[chain] = ev
        return ev

    ESEM = {"vector": "ve", "scalar": "se", "tensor": "pe", "sync": "ve", "gpsimd": "ve"}

    def mset(self, eng, ap, val, waits=()):
        ev = self.op(eng, lambda h: h.memset(ap, val), waits=waits, inc=self.ESEM[eng])
        self.wait(eng, [ev])
        return ev

    def wait(self, eng, waits):
        ws = [w for w in waits if w is not None]

        def run(h, ws=ws):
            for w in ws:
                h.wait_ge(w.sem.h, w.val)
        self.items[eng].append(run)

    def barrier(self):
        bar = self.sems["bar"]
        pre = [Ev(s, s.n) for k, s in self.sems.items() if k != "bar"]
        for e in self.ENGS:
            self.wait(e, pre)
            self.op(e, lambda h: h.nop(), inc="bar")
        tgt = Ev(bar, bar.n)
        for e in self.ENGS:
            self.wait(e, [tgt])

    def install(self, block):
        for name in self.ENGS:
            prog = self.items[name]
            section = getattr(block, name)

            def body(h, prog=prog):
                for fn in prog:
                    fn(h)
            section(body)


_NAME_N = [0]


def _nm(base):
    _NAME_N[0] += 1
    return f"{base}_{_NAME_N[0]}"


def _rope_tables(pos):
    import jax
    import jax.numpy as jnp
    cpu = jax.devices("cpu")[0]
    with jax.default_device(cpu):
        inv = 1.0 / ROPE_THETA ** (jnp.arange(0, 1024, 2, dtype=jnp.float32) / 1024)
        fr = jnp.asarray(pos, dtype=jnp.float32)[:, None] * inv[None, :]
        c, s = np.asarray(jnp.cos(fr)), np.asarray(jnp.sin(fr))
    return np.ascontiguousarray(c), np.ascontiguousarray(s)


def build_program(debug=(), upto="all", s1_jobs=None, skip_p0=False):
    debug = set(debug)
    nc = bass.Bass("TRN2", target_bir_lowering=False)

    inp = {}
    for name, shape, dt in [
        ("x", [T, D], F32), ("xo", [T, D], F32),
        ("q_cos", [T, 512], F32), ("q_sin", [T, 512], F32),
        ("o_cos", [T, 512], F32), ("o_sin", [T, 512], F32),
        ("w_in", [E, D], F32), ("w_out", [D, D], F32),
        ("w_gu", [GU, D], F32), ("w_dn", [D, FF], F32),
        ("b_in_bc", [128, E], F32), ("b_out_bc", [128, D], F32),
        ("nw_bc", [128, D], F32),
    ]:
        inp[name] = nc.dram_tensor(name, shape, dt, kind="ExternalInput").ap()
    out = nc.dram_tensor("out", [T, D], F32, kind="ExternalOutput").ap()

    scr = {}
    for name, shape, dt in [
        ("winT_hi", [D, E], BF16), ("winT_lo", [D, E], BF16),
        ("woutT_hi", [D, D], BF16), ("woutT_lo", [D, D], BF16),
        ("guT_q", [D, GU], BF16), ("dnT_q", [FF, D], BF16),
        ("q_sp", [T, D], F32), ("h_sp", [TS, FF], F32),
    ]:
        scr[name] = nc.dram_tensor(name, shape, dt).ap()

    dbg_out = {}

    def dbg(name, shape, dt=F32):
        dbg_out[name] = nc.dram_tensor(name, shape, dt, kind="ExternalOutput").ap()
        return dbg_out[name]

    with contextlib.ExitStack() as st:
        P = Prog(nc, st)

        g = {}
        g["identf"] = st.enter_context(nc.sbuf_tensor("identf", [128, 128], F32))
        g["identb"] = st.enter_context(nc.sbuf_tensor("identb", [128, 128], BF16))
        g["iot"] = st.enter_context(nc.sbuf_tensor("iot", [128, 128], F32))
        g["ones_col"] = st.enter_context(nc.sbuf_tensor("ones_col", [128, 8], F32))
        g["ones_row"] = st.enter_context(nc.sbuf_tensor("ones_row", [1, 128], F32))
        g["wconst"] = st.enter_context(nc.sbuf_tensor("wconst", [128, 64], F32))
        g["A_acc"] = [st.enter_context(nc.sbuf_tensor(f"A_acc{i}", [128, 64], F32))
                      for i in range(2)]

        ev_iota = P.op("gpsimd", lambda h: h.iota(
            g["iot"][:], [[1, 128]], channel_multiplier=-1,
            allow_small_or_imprecise_dtypes=True), inc="ve")
        P.op("vector", lambda h: h.tensor_single_scalar(
            g["identf"][:], g["iot"][:], 0.0, ALU.is_equal), waits=[ev_iota], inc="ve")
        P.op("vector", lambda h: h.tensor_single_scalar(
            g["identb"][:], g["iot"][:], 0.0, ALU.is_equal), inc="ve")
        P.mset("vector", g["ones_col"][:], 1.0)
        P.mset("vector", g["ones_row"][:], 1.0)
        P.mset("vector", g["wconst"][:], 1.0)
        P.mset("vector", g["A_acc"][0][:], 0.0)
        P.mset("vector", g["A_acc"][1][:], 0.0)
        P.barrier()

        if not skip_p0:
            with contextlib.ExitStack() as ps:
                emit_wprep_hilo(nc, P, ps, inp["w_in"], scr["winT_hi"], scr["winT_lo"],
                                rows=E, g=g, scale_col=None)
            P.barrier()
        if "winT" in debug:
            emit_scratch_dump(nc, P, dbg("dbg_winT_hi", [D, E], BF16), scr["winT_hi"], D, E, BF16)
            emit_scratch_dump(nc, P, dbg("dbg_winT_lo", [D, E], BF16), scr["winT_lo"], D, E, BF16)
            P.barrier()
        if upto == "p0a":
            emit_dummy_out(nc, P, out)
            with nc.Block() as block:
                P.install(block)
            return nc

        if not skip_p0:
            with contextlib.ExitStack() as ps:
                emit_ffn_means(nc, P, ps, inp["w_gu"], inp["w_dn"], g)
            P.barrier()

        if not skip_p0:
            with contextlib.ExitStack() as ps:
                emit_wprep_quant(nc, P, ps, inp["w_gu"], scr["guT_q"], rows=GU,
                                 thr_pos=g["wconst"][:, 0:1], thr_neg=g["wconst"][:, 1:2],
                                 g=g, cols=D)
            P.barrier()
            with contextlib.ExitStack() as ps:
                emit_wprep_quant(nc, P, ps, inp["w_dn"], scr["dnT_q"], rows=D,
                                 thr_pos=g["wconst"][:, 3:4], thr_neg=g["wconst"][:, 4:5],
                                 g=g, cols=FF)
            P.barrier()
        if "wq" in debug:
            emit_scratch_dump(nc, P, dbg("dbg_guT", [D, GU], BF16), scr["guT_q"], D, GU, BF16)
            emit_scratch_dump(nc, P, dbg("dbg_dnT", [FF, D], BF16), scr["dnT_q"], FF, D, BF16)
            o5 = dbg("dbg_wconst", [128, 64], F32)
            evd = P.op("gpsimd", lambda h: h.dma_start(o5, g["wconst"][:]), inc="d2", inc_k=16)
            P.wait("gpsimd", [evd])
            P.barrier()
        if upto == "p0":
            emit_dummy_out(nc, P, out)
            with nc.Block() as block:
                P.install(block)
            return nc

        with contextlib.ExitStack() as ps:
            emit_sweep1(nc, P, ps, inp, scr, g, s1_jobs=s1_jobs)
        P.barrier()

        P.barrier()
        if "s1" in debug:
            emit_scratch_dump(nc, P, dbg("dbg_qsp", [T, D], F32), scr["q_sp"], T, D, F32)
            o7 = dbg("dbg_A", [128, 64], F32)
            n_t = (s1_jobs if s1_jobs is not None else 2 * NST) * NTT
            evd = P.op("gpsimd", lambda h: h.dma_start(o7, g["A_acc"][n_t % 2][:]),
                       inc="d2", inc_k=16)
            P.wait("gpsimd", [evd])
            P.barrier()
        if upto == "p1":
            emit_dummy_out(nc, P, out)
            with nc.Block() as block:
                P.install(block)
            return nc

        n_s1_tiles = (s1_jobs if s1_jobs is not None else 2 * NST) * NTT
        with contextlib.ExitStack() as ps:
            emit_wprep_hilo(nc, P, ps, inp["w_out"], scr["woutT_hi"], scr["woutT_lo"],
                            rows=D, g=g, scale_col=g["A_acc"][n_s1_tiles % 2])
        P.barrier()

        with contextlib.ExitStack() as ps:
            emit_sweep2(nc, P, ps, inp, scr, out, g)
        P.barrier()

        with nc.Block() as block:
            P.install(block)
    return nc


def emit_dummy_out(nc, P, out):
    with contextlib.ExitStack() as ps:
        z = ps.enter_context(nc.sbuf_tensor(_nm("zz"), [128, D], F32))
        ev = P.mset("vector", z[:], 0.0)
        evs = []
        for r in range(T // 128):
            evs.append(P.op("gpsimd", lambda h, r=r: h.dma_start(
                out[r * 128:(r + 1) * 128, :], z[:]), waits=[ev], inc="d2", inc_k=16))
        P.wait("gpsimd", [evs[-1]])
        P.barrier()


def emit_scratch_dump(nc, P, dst, src, rows, cols, dt):
    with contextlib.ExitStack() as ps:
        buf = ps.enter_context(nc.sbuf_tensor(_nm("dump_buf"), [128, cols], dt))
        ev_st = None
        for r in range(rows // 128):
            ev_ld = P.op("sync", lambda h, r=r: h.dma_start(
                buf[:], src[r * 128:(r + 1) * 128, :]), waits=[ev_st], inc="d1", inc_k=16)
            ev_st = P.op("sync", lambda h, r=r: h.dma_start(
                dst[r * 128:(r + 1) * 128, :], buf[:]), waits=[ev_ld], inc="d1", inc_k=16)
        P.wait("sync", [ev_st])
    P.barrier()


def emit_wprep_hilo(nc, P, ps, w_ap, out_hi, out_lo, rows, g, scale_col):
    PTR = [ps.enter_context(nc.psum_tensor(_nm("ptrf"), [128, 512], F32)) for i in range(2)]
    """Stream w [rows, D] fp32 column-slab-wise; PE-transpose; optionally scale
    per-partition by scale_col[:, dc]; split to bf16 hi/lo; store [D, rows]."""
    nec = rows // 128
    raw = ps.enter_context(nc.sbuf_tensor(_nm("wp_raw"), [128, rows], F32))
    hi = ps.enter_context(nc.sbuf_tensor(_nm("wp_hi"), [128, rows], BF16))
    lo = ps.enter_context(nc.sbuf_tensor(_nm("wp_lo"), [128, rows], BF16))
    wsc = ps.enter_context(nc.sbuf_tensor(_nm("wp_sc"), [128, 512], F32))
    ev_out = []
    tg = {}
    gno = [0]
    for dc in range(NDC):
        src = w_ap[:, dc * 128:(dc + 1) * 128].rearrange("(a p) d -> p a d", p=128)
        dst = raw[:].rearrange("p (a d) -> p a d", d=128)
        ev_ld = P.op("sync", lambda h, src=src, dst=dst: h.dma_start(dst, src),
                     waits=ev_out, inc="d1", inc_k=16)
        ngr = (nec + 3) // 4
        ev_sp = None
        for gi in range(ngr):
            n_in = min(4, nec - gi * 4)
            bank = PTR[gno[0] % 2]

            def tr(h, gi=gi, n_in=n_in, bank=bank):
                ins = None
                for k in range(n_in):
                    ec = gi * 4 + k
                    ins = h.transpose(bank[:, k * 128:(k + 1) * 128],
                                      raw[:, ec * 128:(ec + 1) * 128], g["identf"][:])
                return ins
            ev_t = P.op("tensor", tr, waits=[ev_ld, tg.get(gno[0] - 2)], inc="pe")
            w = n_in * 128
            colsl = slice(gi * 512, gi * 512 + w)
            if scale_col is not None:
                ev_s = P.op("vector", lambda h, bank=bank, w=w, dc=dc: h.tensor_scalar(
                    wsc[:, 0:w], bank[:, 0:w], scale_col[:, dc:dc + 1], None, ALU.mult),
                    waits=[ev_t], inc="ve", chain="wprep_sc")
                ev_hi = P.op("scalar", lambda h, w=w, colsl=colsl: h.copy(
                    hi[:, colsl], wsc[:, 0:w]), waits=[ev_s], inc="se")
                ev_lo = P.op("vector", lambda h, w=w, colsl=colsl: h.tensor_sub(
                    lo[:, colsl], wsc[:, 0:w], hi[:, colsl]), waits=[ev_hi], inc="ve",
                    chain="wprep_sc")
            else:
                ev_hi = P.op("scalar", lambda h, bank=bank, w=w, colsl=colsl: h.copy(
                    hi[:, colsl], bank[:, 0:w]), waits=[ev_t], inc="se")
                ev_lo = P.op("vector", lambda h, bank=bank, w=w, colsl=colsl: h.tensor_sub(
                    lo[:, colsl], bank[:, 0:w], hi[:, colsl]), waits=[ev_hi], inc="ve")
            tg[gno[0]] = ev_lo
            gno[0] += 1
            ev_sp = ev_lo
        ev_o1 = P.op("gpsimd", lambda h, dc=dc: h.dma_start(
            out_hi[dc * 128:(dc + 1) * 128, :], hi[:, 0:rows]), waits=[ev_sp], inc="d2", inc_k=16)
        ev_o2 = P.op("gpsimd", lambda h, dc=dc: h.dma_start(
            out_lo[dc * 128:(dc + 1) * 128, :], lo[:, 0:rows]), waits=[ev_sp], inc="d2", inc_k=16)
        ev_out = [ev_o1, ev_o2]
    P.wait("gpsimd", ev_out)


def emit_ffn_means(nc, P, ps, w_gu, w_dn, g):
    PMISC = ps.enter_context(nc.psum_tensor(_nm("pmisc"), [128, 512], F32))
    """mean|w| for gu and dn -> wconst cols (thr, -thr, m)."""
    slab = ps.enter_context(nc.sbuf_tensor("mn_slab", [128, 2048], F32))
    parts = ps.enter_context(nc.sbuf_tensor("mn_parts", [128, 192], F32))
    rowb = ps.enter_context(nc.sbuf_tensor("mn_rows", [1, 256], F32))
    acc = ps.enter_context(nc.sbuf_tensor("mn_acc", [1, 512], F32))
    ch1 = ps.enter_context(nc.sbuf_tensor("mn_c1", [1, 512], F32))
    ch2 = ps.enter_context(nc.sbuf_tensor("mn_c2", [1, 512], F32))
    P.mset("vector", parts[:], 0.0)
    P.mset("vector", acc[:], 1.0)
    P.mset("vector", rowb[:], 0.0)

    specs = [(w_gu, GU * D, 0), (w_dn, D * FF, 1)]
    ev_red = None
    col = 0
    cols_used = [[], []]
    for w_ap, total, mi in specs:
        flat = w_ap.rearrange("a b -> (a b)")
        n_sl = total // (128 * 2048)
        for i in range(n_sl):
            src = flat[i * 128 * 2048:(i + 1) * 128 * 2048].rearrange("(p m) -> p m", p=128)
            ev_ld = P.op("sync", lambda h, src=src: h.dma_start(slab[:], src),
                         waits=[ev_red], inc="d1", inc_k=16)
            ev_red = P.op("vector", lambda h, col=col: h.reduce_sum(
                parts[:, col:col + 1], slab[:], axis=AX.X, apply_absolute_value=True),
                waits=[ev_ld], inc="ve")
            cols_used[mi].append(col)
            col += 1
    ev_p = []
    for mi in range(2):
        c0, c1 = cols_used[mi][0], cols_used[mi][-1] + 1
        ev_p.append(P.op("vector", lambda h, mi=mi, c0=c0, c1=c1: h.reduce_sum(
            parts[:, 184 + mi:185 + mi], parts[:, c0:c1], axis=AX.X), waits=[ev_red], inc="ve"))
    ev_t, ev_r = [], []
    for mi in range(2):
        ev_t.append(P.op("tensor", lambda h, mi=mi: h.transpose(
            PMISC[0:1, mi * 128:(mi + 1) * 128],
            parts[:, 184 + mi:185 + mi], g["identf"][:]),
            waits=[ev_p[mi]] + (ev_r[-1:] if ev_r else []), inc="pe"))
        ev_r.append(P.op("vector", lambda h, mi=mi: h.tensor_copy(
            rowb[0:1, mi * 128:(mi + 1) * 128], PMISC[0:1, mi * 128:(mi + 1) * 128]),
            waits=[ev_t[mi]], inc="ve"))
    ev_s0 = P.op("vector", lambda h: h.reduce_sum(
        acc[0:1, 0:1], rowb[0:1, 0:128], axis=AX.X), waits=ev_r, inc="ve")
    ev_s1 = P.op("vector", lambda h: h.reduce_sum(
        acc[0:1, 256:257], rowb[0:1, 128:256], axis=AX.X), inc="ve")
    P.mset("vector", ch1[:], 1.0)
    P.mset("vector", ch2[:], 1.0)
    Ninv = [1.0 / (GU * D), 1.0 / (D * FF)]
    ev = P.op("vector", lambda h: h.tensor_scalar(
        ch1[0:1, 0:256], acc[0:1, 0:256], Ninv[0], None, ALU.mult),
        waits=[ev_s0, ev_s1], inc="ve")
    ev = P.op("vector", lambda h: h.tensor_scalar(
        ch1[0:1, 256:512], acc[0:1, 256:512], Ninv[1], None, ALU.mult), waits=[ev], inc="ve")
    ev = P.op("vector", lambda h: h.tensor_single_scalar(
        ch2[:], ch1[:], 1e-5, ALU.max), waits=[ev], inc="ve")
    ev = P.op("vector", lambda h: h.tensor_scalar(
        ch1[:], ch2[:], 0.5, None, ALU.mult), waits=[ev], inc="ve")
    items = [(0, ch1, 0), (2, ch2, 0), (3, ch1, 256), (5, ch2, 256)]
    ev_b = []
    for k, (colw, srcbuf, sc) in enumerate(items):
        evm = P.op("tensor", lambda h, k=k, srcbuf=srcbuf, sc=sc: h.matmul(
            PMISC[:, 256 + k:257 + k], g["ones_row"][:], srcbuf[0:1, sc:sc + 1],
            start=True, stop=True),
            waits=[ev] + (ev_b[-1:] if ev_b else []), inc="pe")
        ev_b.append(P.op("vector", lambda h, k=k, colw=colw: h.tensor_copy(
            g["wconst"][:, colw:colw + 1], PMISC[:, 256 + k:257 + k]), waits=[evm], inc="ve"))
    ev = P.op("vector", lambda h: h.tensor_scalar(
        g["wconst"][:, 1:2], g["wconst"][:, 0:1], -1.0, None, ALU.mult), waits=ev_b, inc="ve")
    P.op("vector", lambda h: h.tensor_scalar(
        g["wconst"][:, 4:5], g["wconst"][:, 3:4], -1.0, None, ALU.mult), waits=[ev], inc="ve")


def emit_wprep_quant(nc, P, ps, w_ap, out_q, rows, thr_pos, thr_neg, g, cols):
    PTR = [ps.enter_context(nc.psum_tensor(_nm("ptrb"), [128, 1024], BF16)) for i in range(2)]
    """wq' = sign(w-thr)+sign(w+thr) (natural layout) then bf16 transpose to
    out_q [cols, rows]."""
    nec = rows // 128
    ncc = cols // 128
    raw = ps.enter_context(nc.sbuf_tensor(_nm("wq_raw"), [128, rows], F32))
    sa = ps.enter_context(nc.sbuf_tensor(_nm("wq_sa"), [128, rows], BF16))
    sb2 = ps.enter_context(nc.sbuf_tensor(_nm("wq_sb"), [128, rows], BF16))
    wq = ps.enter_context(nc.sbuf_tensor(_nm("wq_wq"), [128, rows], BF16))
    wqT = ps.enter_context(nc.sbuf_tensor(_nm("wq_wqT"), [128, rows], BF16))
    ev_out = None
    qtg = {}
    qgno = [0]
    for dc in range(ncc):
        src = w_ap[:, dc * 128:(dc + 1) * 128].rearrange("(a p) d -> p a d", p=128)
        dst = raw[:].rearrange("p (a d) -> p a d", d=128)
        ev_ld = P.op("sync", lambda h, src=src, dst=dst: h.dma_start(dst, src),
                     waits=[ev_out], inc="d1", inc_k=16)
        ev_a = P.op("scalar", lambda h: h.activation(sa[:, 0:rows], raw[:, 0:rows],
                                                     AF.Sign, bias=thr_neg),
                    waits=[ev_ld], inc="se")
        ev_b = P.op("scalar", lambda h: h.activation(sb2[:, 0:rows], raw[:, 0:rows],
                                                     AF.Sign, bias=thr_pos),
                    waits=[ev_ld], inc="se")
        ev_w = P.op("vector", lambda h: h.tensor_add(wq[:, 0:rows], sa[:, 0:rows], sb2[:, 0:rows]),
                    waits=[ev_a, ev_b], inc="ve")
        ngr = (nec + 7) // 8
        ev_last = None
        for gi in range(ngr):
            n_in = min(8, nec - gi * 8)
            bank = PTR[(qgno[0] + gi) % 2][:]

            def tr(h, gi=gi, n_in=n_in, bank=bank):
                ins = None
                for k in range(n_in):
                    ec = gi * 8 + k
                    ins = h.transpose(bank[:, k * 128:(k + 1) * 128],
                                      wq[:, ec * 128:(ec + 1) * 128], g["identb"][:])
                return ins
            ev_t = P.op("tensor", tr, waits=[ev_w, qtg.get(qgno[0] + gi - 2)], inc="pe")
            w = n_in * 128
            ev_e = P.op("vector", lambda h, gi=gi, w=w, bank=bank: h.tensor_copy(
                wqT[:, gi * 1024:gi * 1024 + w], bank[:, 0:w]), waits=[ev_t], inc="ve")
            qtg[qgno[0] + gi] = ev_e
            ev_last = ev_e
        qgno[0] += ngr
        ev_out = P.op("gpsimd", lambda h, dc=dc: h.dma_start(
            out_q[dc * 128:(dc + 1) * 128, :], wqT[:, 0:rows]), waits=[ev_last], inc="d2", inc_k=16)
    P.wait("gpsimd", [ev_out])


def emit_ln_chain(P, stats, C_SUM, C_SQS, C_M, C_T, C_RS, waits):
    P.mset("vector", stats[:, C_T:C_T + 64], 1.0)
    P.mset("vector", stats[:, C_T:C_T + 64], 1.0)
    ev = P.op("vector", lambda h: h.tensor_scalar(
        stats[:, C_M:C_M + 64], stats[:, C_SUM:C_SUM + 64], 1.0 / D, None, ALU.mult),
        waits=waits, inc="ve")
    ev = P.op("vector", lambda h: h.tensor_mul(
        stats[:, C_T:C_T + 64], stats[:, C_M:C_M + 64], stats[:, C_M:C_M + 64]),
        waits=[ev], inc="ve")
    ev = P.op("vector", lambda h: h.scalar_tensor_tensor(
        stats[:, C_T:C_T + 64], stats[:, C_SQS:C_SQS + 64], 1.0 / D, stats[:, C_T:C_T + 64],
        ALU.mult, ALU.subtract), waits=[ev], inc="ve")
    ev = P.op("vector", lambda h: h.tensor_scalar(
        stats[:, C_RS:C_RS + 64], stats[:, C_T:C_T + 64], EPS, None, ALU.add),
        waits=[ev], inc="ve")
    ev = P.op("scalar", lambda h: h.sqrt(stats[:, C_T:C_T + 64], stats[:, C_RS:C_RS + 64]),
              waits=[ev], inc="se")
    ev = P.op("vector", lambda h: h.reciprocal(stats[:, C_RS:C_RS + 64], stats[:, C_T:C_T + 64]),
              waits=[ev], inc="ve")
    return ev


def emit_rope(P, buf, base, cs_ap, sn_ap, rtmp, waits, tag="rope"):
    EVN = buf[:, base:base + 1024:2]
    ODD = buf[:, base + 1:base + 1024:2]
    P.op("vector", lambda h: h.tensor_mul(rtmp[:, 0:512], EVN, cs_ap), waits=waits,
         inc="ve", chain=tag)
    P.op("vector", lambda h: h.tensor_mul(rtmp[:, 512:1024], ODD, cs_ap), inc="ve", chain=tag)
    P.op("vector", lambda h: h.scalar_tensor_tensor(
        rtmp[:, 1024:1536], ODD, -1.0, sn_ap, ALU.mult, ALU.mult), inc="ve", chain=tag)
    P.op("vector", lambda h: h.tensor_mul(rtmp[:, 1536:2048], EVN, sn_ap), inc="ve", chain=tag)
    P.op("vector", lambda h: h.tensor_add(EVN, rtmp[:, 0:512], rtmp[:, 1024:1536]),
         inc="ve", chain=tag)
    return P.op("vector", lambda h: h.tensor_add(ODD, rtmp[:, 512:1024], rtmp[:, 1536:2048]),
                inc="ve", chain=tag)


def emit_quant_ladder(P, x_ap, w, sx_col, xq_ap, lad, waits, a_chain=None):
    a, vv, s1, b, r0, r1, sgn = (lad[k] for k in ("a", "vv", "s1", "b", "r0", "r1", "sgn"))
    ev_g = P.op("scalar", lambda h: h.activation(sgn[:, 0:w], x_ap, AF.Sign, scale=sx_col),
                waits=waits, inc="se")
    ev_a = P.op("scalar", lambda h: h.activation(a[:, 0:w], x_ap, AF.Abs, scale=sx_col),
                waits=waits, inc="se", chain=a_chain)
    ev = P.op("vector", lambda h: h.tensor_scalar(
        vv[:, 0:w], a[:, 0:w], 128.0, None, ALU.add), waits=[ev_a], inc="ve")
    ev = P.op("vector", lambda h: h.tensor_scalar(
        s1[:, 0:w], vv[:, 0:w], -128.0, None, ALU.add), waits=[ev], inc="ve")
    r = s1
    rbuf = [r0, r1]
    for i, k in enumerate([64, 32, 16, 8, 4, 2, 1]):
        nxt = rbuf[i % 2]
        ev = P.op("vector", lambda h, r=r, k=k, w=w: h.tensor_scalar(
            b[:, 0:w], r[:, 0:w], float(k) - 0.5, float(-k), ALU.is_ge, ALU.mult),
            waits=[ev], inc="ve")
        ev = P.op("vector", lambda h, r=r, nxt=nxt, w=w: h.tensor_add(
            nxt[:, 0:w], r[:, 0:w], b[:, 0:w]), waits=[ev], inc="ve")
        r = nxt
    ev = P.op("vector", lambda h, r=r, w=w: h.tensor_sub(
        b[:, 0:w], s1[:, 0:w], r[:, 0:w]), waits=[ev], inc="ve")
    return P.op("vector", lambda h, w=w: h.tensor_mul(
        xq_ap, b[:, 0:w], sgn[:, 0:w]), waits=[ev, ev_g], inc="ve")


def emit_sweep1(nc, P, ps, inp, scr, g, s1_jobs=None):
    PMM = [ps.enter_context(nc.psum_tensor(_nm("pmm"), [128, 512], F32)) for i in range(4)]
    PTR = [ps.enter_context(nc.psum_tensor(_nm("ptrf"), [128, 512], F32)) for i in range(2)]
    PA = ps.enter_context(nc.psum_tensor(_nm("papart"), [128, 512], F32))
    xT_hi = ps.enter_context(nc.sbuf_tensor("s1_xT_hi", [128, NDC * TS], BF16))
    xT_lo = ps.enter_context(nc.sbuf_tensor("s1_xT_lo", [128, NDC * TS], BF16))
    kst = ps.enter_context(nc.sbuf_tensor("s1_k", [128, NTT * D], F32))
    vst = ps.enter_context(nc.sbuf_tensor("s1_v", [128, NTT * D], F32))
    xln = ps.enter_context(nc.sbuf_tensor("s1_xln", [128, D], F32))
    qev = [ps.enter_context(nc.sbuf_tensor(f"s1_qe{i}", [128, 512], F32)) for i in range(2)]
    sejunk = ps.enter_context(nc.sbuf_tensor("s1_sejunk", [128, D], F32))
    fmkv = ps.enter_context(nc.sbuf_tensor("s1_fmkv", [128, D], F32))
    cs = ps.enter_context(nc.sbuf_tensor("s1_cs", [128, NTT * 512], F32))
    sn = ps.enter_context(nc.sbuf_tensor("s1_sn", [128, NTT * 512], F32))
    rtmp = ps.enter_context(nc.sbuf_tensor("s1_rt", [128, D], F32))
    binb = [ps.enter_context(nc.sbuf_tensor(f"s1_bin{i}", [128, 512], F32)) for i in range(2)]
    wt = [ps.enter_context(nc.sbuf_tensor(f"s1_wt{i}", [128, 512], BF16)) for i in range(4)]
    stats = ps.enter_context(nc.sbuf_tensor("s1_st", [128, 512], F32))
    SUM, SQS, M, TT_, RS, KS, T2, RN = 0, 64, 128, 192, 256, 320, 384, 448
    _ev = P.mset("vector", stats[:], 1.0)
    P.wait("scalar", [_ev])

    n_tiles_total = (NST + NST) * NTT
    tile_no = [0]
    a_rd_prev = [None]
    tg = {}
    gno = [0]
    prev_xln = [None]
    prev_kst_done = [None] * NTT
    prev_cs_done = [None]
    prev_xT_done = [None]
    prev_bias_done = [None, None]
    prev_qe_done = [None, None]
    ev_bank_free = [None] * NTT

    jobs = [("own", s) for s in range(NST)] + [("dup", s) for s in range(NST)]
    if s1_jobs is not None:
        jobs = jobs[:s1_jobs]
    n_tiles_total = len(jobs) * NTT
    for kind, s in jobs:
        x_ap = inp["x"] if kind == "own" else inp["xo"]
        cos_ap = inp["q_cos"] if kind == "own" else inp["o_cos"]
        sin_ap = inp["q_sin"] if kind == "own" else inp["o_sin"]
        oc_list = list(range(12)) if kind == "own" else list(range(4, 12))
        tok0 = s * TS

        ev_x, ev_cs = [], []
        for tt in range(NTT):
            r0 = tok0 + tt * 128
            ev_x.append(P.op("sync", lambda h, tt=tt, r0=r0, x_ap=x_ap: h.dma_start(
                kst[:, tt * D:(tt + 1) * D], x_ap[r0:r0 + 128, :]),
                waits=[prev_kst_done[tt]], inc="d1", inc_k=16))
            ev_cs.append(P.op("sync", lambda h, tt=tt, r0=r0, cos_ap=cos_ap: h.dma_start(
                cs[:, tt * 512:(tt + 1) * 512], cos_ap[r0:r0 + 128, :]),
                waits=[prev_cs_done[0]], inc="d1", inc_k=16))
            ev_cs.append(P.op("sync", lambda h, tt=tt, r0=r0, sin_ap=sin_ap: h.dma_start(
                sn[:, tt * 512:(tt + 1) * 512], sin_ap[r0:r0 + 128, :]), inc="d1", inc_k=16))
        if S1_PART == "dma":
            for tt in range(NTT):
                prev_kst_done[tt] = ev_x[tt]
            continue
        ev_red, ev_sq = [], []
        for tt in range(NTT):
            ev_red.append(P.op("vector", lambda h, tt=tt: h.reduce_sum(
                stats[:, SUM + tt:SUM + tt + 1], kst[:, tt * D:(tt + 1) * D], axis=AX.X),
                waits=[ev_x[tt]], inc="ve"))
            if S1_PART != "nosq":
                ev_sq.append(P.op("scalar", lambda h, tt=tt: h.activation(
                    sejunk[:], kst[:, tt * D:(tt + 1) * D], AF.Square,
                    accum_out=stats[:, SQS + tt:SQS + tt + 1]), waits=[ev_x[tt]], inc="se",
                    chain="s1jnk"))
        if S1_PART == "stats" or S1_PART == "nosq":
            for tt in range(NTT):
                prev_kst_done[tt] = ev_red[tt]
            continue
        ev_ch = emit_ln_chain(P, stats, SUM, SQS, M, TT_, RS, waits=ev_red + ev_sq)

        if S1_PART == "chain":
            for tt in range(NTT):
                prev_kst_done[tt] = ev_red[tt]
            continue
        ev_xT = []
        for tt in range(NTT):
            ev_ap = P.op("vector", lambda h, tt=tt: h.tensor_scalar(
                xln[:], kst[:, tt * D:(tt + 1) * D],
                stats[:, M + tt:M + tt + 1], stats[:, RS + tt:RS + tt + 1],
                ALU.subtract, ALU.mult), waits=[ev_ch, prev_xln[0]], inc="ve")
            if S1_PART == "apply":
                prev_xln[0] = ev_ap
                continue
            ev_last = None
            for gi in range(4):
                bank = PTR[gno[0] % 2]

                def tr(h, gi=gi, bank=bank):
                    ins = None
                    for k in range(4):
                        dc = gi * 4 + k
                        ins = h.transpose(bank[:, k * 128:(k + 1) * 128],
                                          xln[:, dc * 128:(dc + 1) * 128], g["identf"][:])
                    return ins
                wts = [ev_ap, tg.get(gno[0] - 2)]
                if tt == 0 and gi == 0:
                    wts.append(prev_xT_done[0])
                ev_t = P.op("tensor", tr, waits=wts, inc="pe")
                if S1_PART == "trnoev":
                    tg[gno[0]] = ev_t
                    gno[0] += 1
                    prev_xln[0] = ev_t
                    ev_last = ev_t
                    continue
                hi3 = xT_hi[:].rearrange("p (dc t) -> p dc t", t=TS)[
                    :, gi * 4:(gi + 1) * 4, tt * 128:(tt + 1) * 128]
                lo3 = xT_lo[:].rearrange("p (dc t) -> p dc t", t=TS)[
                    :, gi * 4:(gi + 1) * 4, tt * 128:(tt + 1) * 128]
                bk3 = bank[:].rearrange("p (k c) -> p k c", c=128)
                ev_hi = P.op("scalar", lambda h, hi3=hi3, bk3=bk3: h.copy(hi3, bk3),
                             waits=[ev_t], inc="se")
                ev_lo = P.op("vector", lambda h, lo3=lo3, bk3=bk3, hi3=hi3: h.tensor_sub(
                    lo3, bk3, hi3), waits=[ev_hi], inc="ve")
                tg[gno[0]] = ev_lo
                gno[0] += 1
                ev_last = ev_lo
            prev_xln[0] = ev_last
            ev_xT.append(ev_last)
        if S1_PART in ("ln", "apply", "trnoev"):
            for tt in range(NTT):
                prev_kst_done[tt] = ev_red[tt]
            continue

        ev_wu = {}
        wno = [0]
        ev_mm = None
        for oi, oc in enumerate(oc_list):
            bb = oi % 2
            ev_bias = P.op("sync", lambda h, oc=oc, bb=bb: h.dma_start(
                binb[bb][:], inp["b_in_bc"][:, oc * 512:(oc + 1) * 512]),
                waits=[prev_bias_done[bb]], inc="d1", inc_k=16)
            for dc in range(NDC):
                wi = wno[0]
                ridx = (wi % 2) * 2
                ev_w1 = P.op("sync", lambda h, oc=oc, dc=dc, ridx=ridx: h.dma_start(
                    wt[ridx][:], scr["winT_hi"][dc * 128:(dc + 1) * 128, oc * 512:(oc + 1) * 512]),
                    waits=[ev_wu.get(wi - 2)], inc="d1", inc_k=16)
                ev_w2 = P.op("sync", lambda h, oc=oc, dc=dc, ridx=ridx: h.dma_start(
                    wt[ridx + 1][:], scr["winT_lo"][dc * 128:(dc + 1) * 128, oc * 512:(oc + 1) * 512]),
                    inc="d1", inc_k=16)

                def mm(h, dc=dc, ridx=ridx):
                    ins = None
                    for tt in range(NTT):
                        stsl = slice(dc * TS + tt * 128, dc * TS + (tt + 1) * 128)
                        ins = h.matmul(PMM[tt][:], xT_hi[:, stsl], wt[ridx][:],
                                       start=(dc == 0), stop=False)
                        ins = h.matmul(PMM[tt][:], xT_hi[:, stsl], wt[ridx + 1][:],
                                       start=False, stop=False)
                        ins = h.matmul(PMM[tt][:], xT_lo[:, stsl], wt[ridx][:],
                                       start=False, stop=(dc == NDC - 1))
                    return ins
                wts = [ev_w1, ev_w2]
                if dc == 0:
                    wts += ev_xT + ev_bank_free
                ev_mm = P.op("tensor", mm, waits=wts, inc="pe")
                ev_wu[wi] = ev_mm
                wno[0] += 1
            for tt in range(NTT):
                if oc < 4:
                    qb_i = tt % 2
                    ev_e = P.op("vector", lambda h, tt=tt, qb_i=qb_i, bb=bb: h.tensor_add(
                        qev[qb_i][:], PMM[tt][:], binb[bb][:]),
                        waits=[ev_mm, ev_bias, prev_qe_done[qb_i]], inc="ve")
                    r0 = tok0 + tt * 128
                    ev_sp = P.op("gpsimd", lambda h, r0=r0, oc=oc, qb_i=qb_i: h.dma_start(
                        scr["q_sp"][r0:r0 + 128, oc * 512:(oc + 1) * 512], qev[qb_i][:]),
                        waits=[ev_e], inc="d2", inc_k=16)
                    prev_qe_done[qb_i] = ev_sp
                    ev_bank_free[tt] = ev_e
                elif oc < 8:
                    ev_e = P.op("vector", lambda h, tt=tt, oc=oc, bb=bb: h.tensor_add(
                        kst[:, tt * D + (oc - 4) * 512: tt * D + (oc - 3) * 512],
                        PMM[tt][:], binb[bb][:]), waits=[ev_mm, ev_bias], inc="ve")
                    ev_bank_free[tt] = ev_e
                else:
                    ev_e = P.op("vector", lambda h, tt=tt, oc=oc, bb=bb: h.tensor_add(
                        vst[:, tt * D + (oc - 8) * 512: tt * D + (oc - 7) * 512],
                        PMM[tt][:], binb[bb][:]), waits=[ev_mm, ev_bias], inc="ve")
                    ev_bank_free[tt] = ev_e
            prev_bias_done[bb] = ev_bank_free[NTT - 1]
        prev_xT_done[0] = ev_mm
        if S1_PART == "mm":
            for tt in range(NTT):
                prev_kst_done[tt] = ev_bank_free[tt]
            continue

        ev_ks = []
        ev_rope_last = None
        for tt in range(NTT):
            base = tt * D
            ev_r = emit_rope(P, kst, base, cs[:, tt * 512:(tt + 1) * 512],
                             sn[:, tt * 512:(tt + 1) * 512], rtmp,
                             waits=[ev_bank_free[tt]] + ev_cs)
            ev_rope_last = ev_r
            ev_ks.append(P.op("scalar", lambda h, base=base, tt=tt: h.activation(
                sejunk[:], kst[:, base:base + D], AF.Square,
                accum_out=stats[:, KS + tt:KS + tt + 1]), waits=[ev_r], inc="se",
                chain="s1jnk"))
        prev_cs_done[0] = ev_rope_last
        P.mset("vector", stats[:, T2:T2 + 64], 1.0)
        ev_msk = P.mset("vector", stats[:, T2:T2 + 64], 1.0)
        ev_sd = P.op("scalar", lambda h: h.sqrt(
            stats[:, T2:T2 + 64], stats[:, KS:KS + 64]), waits=ev_ks + [ev_msk], inc="se")
        ev_cl = P.op("vector", lambda h: h.tensor_single_scalar(
            stats[:, TT_:TT_ + 64], stats[:, T2:T2 + 64], 1e-12, ALU.max),
            waits=[ev_sd], inc="ve")
        ev_rn = P.op("vector", lambda h: h.reciprocal(
            stats[:, RN:RN + 64], stats[:, TT_:TT_ + 64]), waits=[ev_cl], inc="ve")
        if S1_PART == "rope":
            for tt in range(NTT):
                prev_kst_done[tt] = ev_ks[tt]
            continue
        ev_am_prev = None
        for tt in range(NTT):
            ksl = slice(tt * D, (tt + 1) * D)
            ev_f = P.op("vector", lambda h, ksl=ksl, tt=tt: h.scalar_tensor_tensor(
                fmkv[:], kst[:, ksl], stats[:, RN + tt:RN + tt + 1], vst[:, ksl],
                ALU.mult, ALU.mult), waits=[ev_rn, ev_am_prev], inc="ve")
            tile_no[0] += 1
            tn = tile_no[0]

            def amm(h):
                ins = None
                for dc in range(NDC):
                    ins = h.matmul(PA[:, dc:dc + 1], fmkv[:, dc * 128:(dc + 1) * 128],
                                   g["ones_col"][:, 0:1],
                                   start=(dc == 0), stop=(dc == NDC - 1),
                                   skip_group_check=True)
                return ins
            ev_am = P.op("tensor", amm, waits=[ev_f, a_rd_prev[0]], inc="pe")
            ev_aa = P.op("vector", lambda h, tn=tn: h.tensor_add(
                g["A_acc"][tn % 2][:, 0:16], g["A_acc"][(tn - 1) % 2][:, 0:16],
                PA[:, 0:16]), waits=[ev_am], inc="ve")
            a_rd_prev[0] = ev_aa
            ev_am_prev = ev_am
            prev_kst_done[tt] = ev_am


def emit_sweep2(nc, P, ps, inp, scr, out, g):
    PMM = [ps.enter_context(nc.psum_tensor(_nm("pmm"), [128, 512], F32)) for i in range(4)]
    PTR = [ps.enter_context(nc.psum_tensor(_nm("ptrf"), [128, 512], F32)) for i in range(2)]
    PTB = [ps.enter_context(nc.psum_tensor(_nm("ptrb"), [128, 1024], BF16)) for i in range(2)]
    xst = ps.enter_context(nc.sbuf_tensor("s2_xst", [128, NTT * D], F32))
    att = ps.enter_context(nc.sbuf_tensor("s2_att", [128, NTT * D], F32))
    tmpA = ps.enter_context(nc.sbuf_tensor("s2_tmpA", [128, D], F32))
    tmpB = ps.enter_context(nc.sbuf_tensor("s2_tmpB", [128, D], F32))
    ghold = ps.enter_context(nc.sbuf_tensor("s2_gh", [128, NTT * 512], F32))
    uh = ps.enter_context(nc.sbuf_tensor("s2_uh", [128, 512], F32))
    sqg = ps.enter_context(nc.sbuf_tensor("s2_sqg", [128, 512], F32))
    hbuf = [ps.enter_context(nc.sbuf_tensor(f"s2_hb{i}", [128, 512], F32)) for i in range(2)]
    cs = [ps.enter_context(nc.sbuf_tensor("s2_cs0", [128, 512], F32))]
    sn = [ps.enter_context(nc.sbuf_tensor("s2_sn0", [128, 512], F32))]
    bout = [ps.enter_context(nc.sbuf_tensor(f"s2_bo{i}", [128, 512], F32)) for i in range(2)]
    nwb = ps.enter_context(nc.sbuf_tensor("s2_nw", [128, D], F32))
    stats = ps.enter_context(nc.sbuf_tensor("s2_st", [128, 768], F32))
    qT_hi = ps.enter_context(nc.sbuf_tensor("s2_qTh", [128, NDC * TS], BF16))
    qT_lo = ps.enter_context(nc.sbuf_tensor("s2_qTl", [128, NDC * TS], BF16))
    hqA = ps.enter_context(nc.sbuf_tensor("s2_hqA", [128, 32 * TS], BF16))
    hq = ps.enter_context(nc.sbuf_tensor("s2_hq", [128, FF], BF16))
    vvb = ps.enter_context(nc.sbuf_tensor("s2_vv", [128, D], BF16))
    sgnb = ps.enter_context(nc.sbuf_tensor("s2_sgn", [128, D], BF16))
    wt = [ps.enter_context(nc.sbuf_tensor(f"s2_wt{i}", [128, 512], BF16)) for i in range(4)]

    sejunk = tmpB[:]
    gx4 = ghold[:, 0:D]
    lad = {"a": tmpB, "vv": vvb, "s1": xst[:, 0:D], "b": xst[:, D:2 * D],
           "r0": xst[:, 2 * D:3 * D], "r1": xst[:, 3 * D:4 * D], "sgn": sgnb}

    def hqT_ap(fc, c0, c1):
        if fc < 32:
            base = fc * TS
            return hqA[:, base + c0:base + c1]
        base = (fc - 32) * TS
        return qT_lo[:, base + c0:base + c1]

    ffn = xst

    SUM1, SQS1, M1, T1_, RS1 = 0, 64, 128, 192, 256
    SUM2, SQS2, M2, RS2 = 320, 384, 448, 512
    AMX, SX, SCG = 576, 608, 640
    HM = 704
    _ev1 = P.mset("vector", stats[:, 0:512], 1.0)
    _ev2 = P.mset("vector", stats[:, 512:768], 1.0)
    P.wait("scalar", [_ev1, _ev2])
    ev_nw = P.op("sync", lambda h: h.dma_start(nwb[:], inp["nw_bc"]), inc="d1", inc_k=16)

    prev_stage_done = [None] * NTT
    prev_cs_done = [None]
    prev_qT_done = [None]
    prev_bo_done = [None, None]
    prev_e2 = [None]
    ev_bank_free = [None] * NTT
    ev_wu = {}
    wno = [0]
    tg = {}
    gno = [0]

    def wload(src_ap, r0, c0, waits=()):
        wi = wno[0]
        ridx = wi % 4
        ev = P.op("sync", lambda h, ridx=ridx: h.dma_start(
            wt[ridx][:], src_ap[r0:r0 + 128, c0:c0 + 512]),
            waits=[ev_wu.get(wi - 4)] + list(waits), inc="d1", inc_k=16)
        wno[0] += 1
        return wi, ridx, ev

    for s in range(NST):
        tok0 = s * TS
        ev_x, ev_q = [], []
        for tt in range(NTT):
            r0 = tok0 + tt * 128
            ev_x.append(P.op("sync", lambda h, tt=tt, r0=r0: h.dma_start(
                xst[:, tt * D:(tt + 1) * D], inp["x"][r0:r0 + 128, :]),
                waits=[prev_stage_done[tt]], inc="d1", inc_k=16))
            ev_q.append(P.op("sync", lambda h, tt=tt, r0=r0: h.dma_start(
                att[:, tt * D:(tt + 1) * D], scr["q_sp"][r0:r0 + 128, :]),
                waits=[prev_stage_done[tt]], inc="d1", inc_k=16))
        ev_red, ev_sq = [], []
        for tt in range(NTT):
            ev_red.append(P.op("vector", lambda h, tt=tt: h.reduce_sum(
                stats[:, SUM1 + tt:SUM1 + tt + 1], xst[:, tt * D:(tt + 1) * D], axis=AX.X),
                waits=[ev_x[tt]], inc="ve"))
            ev_sq.append(P.op("scalar", lambda h, tt=tt: h.activation(
                sejunk, xst[:, tt * D:(tt + 1) * D], AF.Square,
                accum_out=stats[:, SQS1 + tt:SQS1 + tt + 1]), waits=[ev_x[tt]], inc="se",
                chain="s2_tmpB"))
        ev_ch1 = emit_ln_chain(P, stats, SUM1, SQS1, M1, T1_, RS1, waits=ev_red + ev_sq)
        ev_qT = []
        for tt in range(NTT):
            base = tt * D
            pp = 0
            r0 = tok0 + tt * 128
            evc = P.op("sync", lambda h, pp=pp, r0=r0: h.dma_start(
                cs[pp][:], inp["q_cos"][r0:r0 + 128, :]),
                waits=[prev_cs_done[pp]], inc="d1", inc_k=16)
            evs_ = P.op("sync", lambda h, pp=pp, r0=r0: h.dma_start(
                sn[pp][:], inp["q_sin"][r0:r0 + 128, :]), inc="d1", inc_k=16)
            ev_r = emit_rope(P, att, base, cs[pp][:], sn[pp][:], tmpA,
                             waits=[ev_q[tt], evc, evs_], tag="s2_tmpA")
            prev_cs_done[pp] = ev_r
            ev_last = None
            for gi in range(4):
                bank = PTR[gno[0] % 2]

                def tr(h, gi=gi, bank=bank, base=base):
                    ins = None
                    for k in range(4):
                        dc = gi * 4 + k
                        ins = h.transpose(bank[:, k * 128:(k + 1) * 128],
                                          att[:, base + dc * 128:base + (dc + 1) * 128],
                                          g["identf"][:])
                    return ins
                ev_t = P.op("tensor", tr, waits=[ev_r, tg.get(gno[0] - 2)], inc="pe")
                hi3 = qT_hi[:].rearrange("p (dc t) -> p dc t", t=TS)[
                    :, gi * 4:(gi + 1) * 4, tt * 128:(tt + 1) * 128]
                lo3 = qT_lo[:].rearrange("p (dc t) -> p dc t", t=TS)[
                    :, gi * 4:(gi + 1) * 4, tt * 128:(tt + 1) * 128]
                bk3 = bank[:].rearrange("p (k c) -> p k c", c=128)
                ev_hi = P.op("scalar", lambda h, hi3=hi3, bk3=bk3: h.copy(hi3, bk3),
                             waits=[ev_t], inc="se")
                ev_lo = P.op("vector", lambda h, lo3=lo3, bk3=bk3, hi3=hi3: h.tensor_sub(
                    lo3, bk3, hi3), waits=[ev_hi], inc="ve")
                tg[gno[0]] = ev_lo
                gno[0] += 1
                ev_last = ev_lo
            ev_qT.append(ev_last)
        ev_mm = None
        for oc2 in range(4):
            bb = oc2 % 2
            ev_bo = P.op("sync", lambda h, oc2=oc2, bb=bb: h.dma_start(
                bout[bb][:], inp["b_out_bc"][:, oc2 * 512:(oc2 + 1) * 512]),
                waits=[prev_bo_done[bb]], inc="d1", inc_k=16)
            for dc in range(NDC):
                wi1, r1_, ev_w1 = wload(scr["woutT_hi"], dc * 128, oc2 * 512)
                wi2, r2_, ev_w2 = wload(scr["woutT_lo"], dc * 128, oc2 * 512)

                def mm(h, dc=dc, r1_=r1_, r2_=r2_):
                    ins = None
                    for tt in range(NTT):
                        stsl = slice(dc * TS + tt * 128, dc * TS + (tt + 1) * 128)
                        ins = h.matmul(PMM[tt][:], qT_hi[:, stsl], wt[r1_][:],
                                       start=(dc == 0), stop=False)
                        ins = h.matmul(PMM[tt][:], qT_hi[:, stsl], wt[r2_][:],
                                       start=False, stop=False)
                        ins = h.matmul(PMM[tt][:], qT_lo[:, stsl], wt[r1_][:],
                                       start=False, stop=(dc == NDC - 1))
                    return ins
                wts = [ev_w1, ev_w2]
                if dc == 0:
                    wts += ev_qT + ev_bank_free
                ev_mm = P.op("tensor", mm, waits=wts, inc="pe")
                ev_wu[wi1] = ev_mm
                ev_wu[wi2] = ev_mm
            for tt in range(NTT):
                csl = slice(oc2 * 512, (oc2 + 1) * 512)
                ev_e1 = P.op("vector", lambda h, tt=tt, bb=bb: h.tensor_add(
                    uh[:], PMM[tt][:], bout[bb][:]),
                    waits=[ev_mm, ev_bo, prev_e2[0]], inc="ve", chain="s2_uh")
                ev_t1 = P.op("vector", lambda h, tt=tt, csl=csl: h.tensor_scalar(
                    tmpA[:, 0:512], xst[:, tt * D + csl.start:tt * D + csl.stop],
                    stats[:, M1 + tt:M1 + tt + 1], stats[:, RS1 + tt:RS1 + tt + 1],
                    ALU.subtract, ALU.mult), waits=[ev_ch1], inc="ve", chain="s2_tmpA")
                ev_e2 = P.op("vector", lambda h, tt=tt, csl=csl: h.tensor_add(
                    att[:, tt * D + csl.start:tt * D + csl.stop], uh[:], tmpA[:, 0:512]),
                    waits=[ev_e1, ev_t1], inc="ve", chain="s2_tmpA")
                prev_e2[0] = ev_e2
                ev_bank_free[tt] = ev_e2
            prev_bo_done[bb] = ev_bank_free[NTT - 1]
        prev_qT_done[0] = ev_mm
        ev_red2, ev_sq2 = [], []
        for tt in range(NTT):
            ev_red2.append(P.op("vector", lambda h, tt=tt: h.reduce_sum(
                stats[:, SUM2 + tt:SUM2 + tt + 1], att[:, tt * D:(tt + 1) * D], axis=AX.X),
                waits=[ev_bank_free[tt]], inc="ve"))
            ev_sq2.append(P.op("scalar", lambda h, tt=tt: h.activation(
                sejunk, att[:, tt * D:(tt + 1) * D], AF.Square,
                accum_out=stats[:, SQS2 + tt:SQS2 + tt + 1]),
                waits=[ev_bank_free[tt]], inc="se", chain="s2_tmpB"))
        ev_ch2 = emit_ln_chain(P, stats, SUM2, SQS2, M2, T1_, RS2, waits=ev_red2 + ev_sq2)
        ev_x2 = []
        ev_sq3 = []
        for tt in range(NTT):
            tsl = slice(tt * D, (tt + 1) * D)
            ev_a = P.op("vector", lambda h, tsl=tsl, tt=tt: h.tensor_scalar(
                tmpA[:], att[:, tsl], stats[:, M2 + tt:M2 + tt + 1],
                stats[:, RS2 + tt:RS2 + tt + 1], ALU.subtract, ALU.mult),
                waits=[ev_ch2], inc="ve", chain="s2_tmpA")
            ev_b = P.op("vector", lambda h, tsl=tsl, tt=tt: h.scalar_tensor_tensor(
                att[:, tsl], tmpA[:], 1.0, xst[:, tsl], ALU.mult, ALU.add,
                accum_out=stats[:, SUM1 + tt:SUM1 + tt + 1]), waits=[ev_a], inc="ve",
                chain="s2_tmpA")
            ev_x2.append(ev_b)
            ev_sq3.append(P.op("scalar", lambda h, tsl=tsl, tt=tt: h.activation(
                sejunk, att[:, tsl], AF.Square,
                accum_out=stats[:, SQS1 + tt:SQS1 + tt + 1]), waits=[ev_b], inc="se",
                chain="s2_tmpB"))
        ev_ch3 = emit_ln_chain(P, stats, SUM1, SQS1, M1, T1_, RS1, waits=ev_x2 + ev_sq3)
        ev = P.op("vector", lambda h: h.tensor_mul(
            stats[:, T1_:T1_ + 64], stats[:, M1:M1 + 64], stats[:, M1:M1 + 64]),
            waits=[ev_ch3], inc="ve")
        ev = P.op("vector", lambda h: h.scalar_tensor_tensor(
            stats[:, T1_:T1_ + 64], stats[:, SQS1:SQS1 + 64], 1.0 / D, stats[:, T1_:T1_ + 64],
            ALU.mult, ALU.subtract), waits=[ev], inc="ve")
        ev = P.op("vector", lambda h: h.tensor_mul(
            stats[:, SUM2:SUM2 + 64], stats[:, RS1:RS1 + 64], stats[:, RS1:RS1 + 64]),
            waits=[ev], inc="ve")
        ev = P.op("vector", lambda h: h.tensor_mul(
            stats[:, T1_:T1_ + 64], stats[:, T1_:T1_ + 64], stats[:, SUM2:SUM2 + 64]),
            waits=[ev], inc="ve")
        ev = P.op("vector", lambda h: h.tensor_scalar(
            stats[:, SQS2:SQS2 + 64], stats[:, T1_:T1_ + 64], EPS, None, ALU.add),
            waits=[ev], inc="ve")
        ev = P.op("scalar", lambda h: h.sqrt(
            stats[:, T1_:T1_ + 64], stats[:, SQS2:SQS2 + 64]), waits=[ev], inc="se")
        ev = P.op("vector", lambda h: h.reciprocal(
            stats[:, SQS2:SQS2 + 64], stats[:, T1_:T1_ + 64]), waits=[ev], inc="ve")
        ev_c3 = P.op("vector", lambda h: h.tensor_mul(
            stats[:, RS2:RS2 + 64], stats[:, RS1:RS1 + 64], stats[:, SQS2:SQS2 + 64]),
            waits=[ev], inc="ve")
        ev_xqT = []
        ev_lad_prev = None
        for tt in range(NTT):
            tsl = slice(tt * D, (tt + 1) * D)
            ev_p1 = P.op("vector", lambda h, tsl=tsl, tt=tt: h.tensor_scalar(
                tmpA[:], att[:, tsl], stats[:, M1 + tt:M1 + tt + 1],
                stats[:, RS2 + tt:RS2 + tt + 1], ALU.subtract, ALU.mult),
                waits=[ev_c3, ev_lad_prev], inc="ve", chain="s2_tmpA")
            ev_x4 = P.op("vector", lambda h: h.tensor_mul(
                gx4, tmpA[:], nwb[:]), waits=[ev_p1, ev_nw], inc="ve", chain="s2_tmpA")
            ev_mx = P.op("vector", lambda h, tt=tt: h.reduce_max(
                stats[:, AMX + tt:AMX + tt + 1], gx4, axis=AX.X,
                apply_absolute_value=True), waits=[ev_x4], inc="ve")
            P.mset("vector", stats[:, T1_:T1_ + 64], 1.0)
            ev = P.op("vector", lambda h: h.tensor_single_scalar(
                stats[:, T1_:T1_ + 64], stats[:, AMX:AMX + 64], 1e-5, ALU.max),
                waits=[ev_mx], inc="ve")
            ev = P.op("vector", lambda h: h.reciprocal(
                stats[:, SUM2:SUM2 + 64], stats[:, T1_:T1_ + 64]), waits=[ev], inc="ve")
            ev_sx = P.op("vector", lambda h: h.tensor_scalar(
                stats[:, SX:SX + 64], stats[:, SUM2:SUM2 + 64], 127.0, None, ALU.mult),
                waits=[ev], inc="ve")
            ev_sc = P.op("vector", lambda h: h.tensor_scalar(
                stats[:, SCG:SCG + 64], stats[:, T1_:T1_ + 64], g["wconst"][:, 2:3],
                0.5 / 127.0, ALU.mult, ALU.mult), waits=[ev_sx], inc="ve")
            ev_lq = emit_quant_ladder(P, gx4, D, stats[:, SX + tt:SX + tt + 1],
                                      hq[:, 0:D], lad, waits=[ev_sx, ev_lad_prev],
                                      a_chain="s2_tmpB")
            ev_last = None
            for gi in range(4):
                bank = PTB[gno[0] % 2]

                def tr(h, gi=gi, bank=bank):
                    ins = None
                    for k in range(4):
                        dc = gi * 4 + k
                        ins = h.transpose(bank[:, k * 128:(k + 1) * 128],
                                          hq[:, dc * 128:(dc + 1) * 128], g["identb"][:])
                    return ins
                ev_t = P.op("tensor", tr, waits=[ev_lq, tg.get(gno[0] - 2)], inc="pe")
                dst3 = qT_hi[:].rearrange("p (dc t) -> p dc t", t=TS)[
                    :, gi * 4:(gi + 1) * 4, tt * 128:(tt + 1) * 128]
                bk3 = bank[:].rearrange("p (k c) -> p k c", c=128)[:, 0:4, :]
                ev_e = P.op("vector", lambda h, dst3=dst3, bk3=bk3: h.tensor_copy(dst3, bk3),
                            waits=[ev_t], inc="ve")
                tg[gno[0]] = ev_e
                gno[0] += 1
                ev_last = ev_e
            ev_lad_prev = ev_last
            ev_xqT.append(ev_last)
        ev_h_sp = {}
        ev_hb_guard = [None, None]
        ev_hmax = []
        for j in range(11):
            for phase, oc3 in (("g", j), ("u", 11 + j)):
                for dc in range(NDC):
                    wi1, r1_, ev_w1 = wload(scr["guT_q"], dc * 128, oc3 * 512)

                    def mm(h, dc=dc, r1_=r1_):
                        ins = None
                        for tt in range(NTT):
                            stsl = slice(dc * TS + tt * 128, dc * TS + (tt + 1) * 128)
                            ins = h.matmul(PMM[tt][:], qT_hi[:, stsl], wt[r1_][:],
                                           start=(dc == 0), stop=(dc == NDC - 1))
                        return ins
                    wts = [ev_w1]
                    if dc == 0:
                        wts += ev_xqT + ev_bank_free
                    ev_mm = P.op("tensor", mm, waits=wts, inc="pe")
                    ev_wu[wi1] = ev_mm
                for tt in range(NTT):
                    if phase == "g":
                        ev_e = P.op("vector", lambda h, tt=tt: h.tensor_scalar(
                            ghold[:, tt * 512:(tt + 1) * 512], PMM[tt][:],
                            stats[:, SCG + tt:SCG + tt + 1], 0.0, ALU.mult, ALU.max),
                            waits=[ev_mm], inc="ve", chain="s2_gx4")
                        ev_bank_free[tt] = ev_e
                    else:
                        ev_u = P.op("vector", lambda h, tt=tt: h.tensor_scalar(
                            uh[:], PMM[tt][:], stats[:, SCG + tt:SCG + tt + 1], None,
                            ALU.mult), waits=[ev_mm], inc="ve", chain="s2_uh")
                        ev_s = P.op("scalar", lambda h, tt=tt: h.square(
                            sqg[:], ghold[:, tt * 512:(tt + 1) * 512]),
                            waits=[ev_bank_free[tt], ev_h_sp.get(("hh_prev",))], inc="se",
                            chain="sqg")
                        hb = hbuf[tt % 2]
                        ev_hh = P.op("vector", lambda h, hb=hb: h.tensor_mul(
                            hb[:], sqg[:], uh[:]),
                            waits=[ev_u, ev_s, ev_hb_guard[tt % 2]], inc="ve",
                            chain="s2_uh")
                        ev_h_sp[("hh_prev",)] = ev_hh
                        ev_hm = P.op("vector", lambda h, tt=tt, j=j, hb=hb: h.reduce_max(
                            stats[:, HM + tt * 16 + j:HM + tt * 16 + j + 1], hb[:],
                            axis=AX.X, apply_absolute_value=True), waits=[ev_hh], inc="ve")
                        ev_hmax.append(ev_hm)
                        ev_d = P.op("gpsimd", lambda h, tt=tt, j=j, hb=hb: h.dma_start(
                            scr["h_sp"][tt * 128:(tt + 1) * 128, j * 512:(j + 1) * 512], hb[:]),
                            waits=[ev_hh], inc="d2", inc_k=16)
                        ev_h_sp[(j, tt)] = ev_d
                        ev_hb_guard[tt % 2] = ev_d
                        ev_bank_free[tt] = ev_hh
        ev_hqT = [None] * NTT
        ev_lad2 = None
        for tt in range(NTT):
            ev_am = P.op("vector", lambda h, tt=tt: h.reduce_max(
                stats[:, AMX + tt:AMX + tt + 1],
                stats[:, HM + tt * 16:HM + tt * 16 + 11], axis=AX.X), waits=ev_hmax, inc="ve")
            P.mset("vector", stats[:, T1_:T1_ + 64], 1.0)
            ev = P.op("vector", lambda h: h.tensor_single_scalar(
                stats[:, T1_:T1_ + 64], stats[:, AMX:AMX + 64], 1e-5, ALU.max),
                waits=[ev_am], inc="ve")
            ev = P.op("vector", lambda h: h.reciprocal(
                stats[:, SUM2:SUM2 + 64], stats[:, T1_:T1_ + 64]), waits=[ev], inc="ve")
            ev_sx2 = P.op("vector", lambda h: h.tensor_scalar(
                stats[:, SX:SX + 64], stats[:, SUM2:SUM2 + 64], 127.0, None, ALU.mult),
                waits=[ev], inc="ve")
            ev_sc2 = P.op("vector", lambda h: h.tensor_scalar(
                stats[:, SCG:SCG + 64], stats[:, T1_:T1_ + 64], g["wconst"][:, 5:6],
                0.5 / 127.0, ALU.mult, ALU.mult), waits=[ev_sx2], inc="ve")
            for ci, (c0, cw) in enumerate([(0, 2048), (2048, 2048), (4096, 1536)]):
                ev_ld = P.op("sync", lambda h, tt=tt, c0=c0, cw=cw: h.dma_start(
                    tmpA[:, 0:cw], scr["h_sp"][tt * 128:(tt + 1) * 128, c0:c0 + cw]),
                    waits=[ev_h_sp.get((10, tt)), ev_lad2], inc="d1", inc_k=16,
                    chain="s2_tmpA")
                ev_lad2 = emit_quant_ladder(
                    P, tmpA[:, 0:cw], cw, stats[:, SX + tt:SX + tt + 1],
                    hq[:, c0:c0 + cw], lad,
                    waits=[ev_ld, ev_sx2] + (ev_xqT if (tt == 0 and ci == 0) else []),
                    a_chain="s2_tmpB")
            ev_last = None
            for gi in range(6):
                n_in = min(8, NFC - gi * 8)
                bank = PTB[gno[0] % 2]

                def tr(h, gi=gi, n_in=n_in, bank=bank):
                    ins = None
                    for k in range(n_in):
                        fc = gi * 8 + k
                        ins = h.transpose(bank[:, k * 128:(k + 1) * 128],
                                          hq[:, fc * 128:(fc + 1) * 128], g["identb"][:])
                    return ins
                ev_t = P.op("tensor", tr, waits=[ev_lad2, tg.get(gno[0] - 2)], inc="pe")
                arena = hqA[:] if gi < 4 else qT_lo[:]
                fc0 = gi * 8 if gi < 4 else gi * 8 - 32
                dst3 = arena.rearrange("p (fc t) -> p fc t", t=TS)[
                    :, fc0:fc0 + n_in, tt * 128:(tt + 1) * 128]
                bk3 = bank[:].rearrange("p (k c) -> p k c", c=128)[:, 0:n_in, :]
                ev_e = P.op("vector", lambda h, dst3=dst3, bk3=bk3: h.tensor_copy(dst3, bk3),
                            waits=[ev_t], inc="ve")
                tg[gno[0]] = ev_e
                gno[0] += 1
                ev_last = ev_e
            ev_lad2 = ev_last
            ev_hqT[tt] = ev_last
        for oc4 in range(4):
            for fc in range(NFC):
                wi1, r1_, ev_w1 = wload(scr["dnT_q"], fc * 128, oc4 * 512)

                def mm(h, fc=fc, r1_=r1_):
                    ins = None
                    for tt in range(NTT):
                        ins = h.matmul(PMM[tt][:], hqT_ap(fc, tt * 128, (tt + 1) * 128),
                                       wt[r1_][:], start=(fc == 0), stop=(fc == NFC - 1))
                    return ins
                wts = [ev_w1]
                if fc == 0:
                    wts += ev_hqT + ev_bank_free
                ev_mm = P.op("tensor", mm, waits=wts, inc="pe")
                ev_wu[wi1] = ev_mm
            for tt in range(NTT):
                csl = slice(oc4 * 512, (oc4 + 1) * 512)
                ev_e = P.op("vector", lambda h, tt=tt, csl=csl: h.tensor_scalar(
                    ffn[:, tt * D + csl.start: tt * D + csl.stop], PMM[tt][:],
                    stats[:, SCG + tt:SCG + tt + 1], None, ALU.mult),
                    waits=[ev_mm], inc="ve")
                ev_bank_free[tt] = ev_e
        ev_red4, ev_sq4 = [], []
        for tt in range(NTT):
            ev_red4.append(P.op("vector", lambda h, tt=tt: h.reduce_sum(
                stats[:, SUM1 + tt:SUM1 + tt + 1], ffn[:, tt * D:(tt + 1) * D], axis=AX.X),
                waits=[ev_bank_free[tt]], inc="ve"))
            ev_sq4.append(P.op("scalar", lambda h, tt=tt: h.activation(
                sejunk, ffn[:, tt * D:(tt + 1) * D], AF.Square,
                accum_out=stats[:, SQS1 + tt:SQS1 + tt + 1]),
                waits=[ev_bank_free[tt]], inc="se", chain="s2_tmpB"))
        ev_ch4 = emit_ln_chain(P, stats, SUM1, SQS1, M1, T1_, RS1, waits=ev_red4 + ev_sq4)
        prev_st = None
        prev_o = None
        for tt in range(NTT):
            tsl = slice(tt * D, (tt + 1) * D)
            ev_a = P.op("vector", lambda h, tsl=tsl, tt=tt: h.tensor_scalar(
                tmpA[:], ffn[:, tsl], stats[:, M1 + tt:M1 + tt + 1],
                stats[:, RS1 + tt:RS1 + tt + 1], ALU.subtract, ALU.mult),
                waits=[ev_ch4, prev_o], inc="ve", chain="s2_tmpA")
            ev_o = P.op("vector", lambda h, tsl=tsl: h.tensor_add(
                tmpB[:], tmpA[:], att[:, tsl]), waits=[ev_a, prev_st], inc="ve",
                chain="s2_tmpB")
            prev_o = ev_o
            r0 = tok0 + tt * 128
            ev_st = P.op("gpsimd", lambda h, r0=r0: h.dma_start(
                out[r0:r0 + 128, :], tmpB[:]), waits=[ev_o], inc="d2", inc_k=16,
                chain="s2_tmpB")
            prev_st = ev_st
            prev_stage_done[tt] = ev_st
    P.wait("gpsimd", [prev_stage_done[tt] for tt in range(NTT)])


def kernel(L0_attn_X, L0_attn_W_in, L0_attn_b_in, L0_attn_W_out, L0_attn_b_out,
           L0_ffn_w_gate_up, L0_ffn_w_down, L0_ffn_norm_weight):
    X = np.ascontiguousarray(np.asarray(L0_attn_X, dtype=np.float32))
    W_in = np.ascontiguousarray(np.asarray(L0_attn_W_in, dtype=np.float32))
    b_in = np.asarray(L0_attn_b_in, dtype=np.float32)
    W_out = np.ascontiguousarray(np.asarray(L0_attn_W_out, dtype=np.float32))
    b_out = np.asarray(L0_attn_b_out, dtype=np.float32)
    w_gu = np.ascontiguousarray(np.asarray(L0_ffn_w_gate_up, dtype=np.float32))
    w_dn = np.ascontiguousarray(np.asarray(L0_ffn_w_down, dtype=np.float32))
    nw = np.asarray(L0_ffn_norm_weight, dtype=np.float32)
    return _run(X, W_in, b_in, W_out, b_out, w_gu, w_dn, nw, build_program())


def _run(X, W_in, b_in, W_out, b_out, w_gu, w_dn, nw, nc, extra_keys=()):
    b_in_bc = np.ascontiguousarray(np.broadcast_to(b_in[None, :], (128, E)))
    b_out_bc = np.ascontiguousarray(np.broadcast_to(b_out[None, :], (128, D)))
    nw_bc = np.ascontiguousarray(np.broadcast_to(nw[None, :], (128, D)))
    in_maps = []
    for c in range(8):
        b, half = c // 2, c % 2
        qc, qs = _rope_tables(np.arange(half * T, (half + 1) * T))
        oc_, os_ = _rope_tables(np.arange((1 - half) * T, (2 - half) * T))
        in_maps.append({
            "x": np.ascontiguousarray(X[b, half * T:(half + 1) * T, :]),
            "xo": np.ascontiguousarray(X[b, (1 - half) * T:(2 - half) * T, :]),
            "q_cos": qc, "q_sin": qs, "o_cos": oc_, "o_sin": os_,
            "w_in": W_in, "w_out": W_out, "w_gu": w_gu, "w_dn": w_dn,
            "b_in_bc": b_in_bc, "b_out_bc": b_out_bc, "nw_bc": nw_bc,
        })
    res = run_bass_kernel_spmd(nc, in_maps, core_ids=list(range(8)))
    full = np.empty((4, 4096, D), dtype=np.float32)
    for c in range(8):
        b, half = c // 2, c % 2
        full[b, half * T:(half + 1) * T, :] = res.results[c]["out"]
    if extra_keys:
        return full, res
    return full


# version 06
# speedup vs baseline: 1.1870x; 1.1434x over previous
import contextlib
import numpy as np

import concourse.bass as bass
import concourse.mybir as mybir
from concourse.bass_utils import run_bass_kernel_spmd

F32 = mybir.dt.float32
BF16 = mybir.dt.bfloat16
ALU = mybir.AluOpType
AF = mybir.ActivationFunctionType
AX = mybir.AxisListType

D = 2048
E = 6144
FF = 5632
GU = 11264
T = 2048
S1_PART = "full"
TS = 512
NTT = TS // 128
NST = T // TS
NDC = D // 128
NFC = FF // 128
ROPE_THETA = 10000.0
EPS = 1e-5


class Ev:
    __slots__ = ("sem", "val")

    def __init__(self, sem, val):
        self.sem = sem
        self.val = val


class SemCtr:
    def __init__(self, handle):
        self.h = handle
        self.n = 0

    def next(self, k):
        self.n += k
        return Ev(self, self.n)


class Prog:
    ENGS = ("sync", "gpsimd", "vector", "scalar", "tensor")

    def __init__(self, nc, st):
        self.nc = nc
        self.items = {e: [] for e in self.ENGS}
        self.sems = {}
        for name in ("pe", "ve", "se", "bar"):
            self.sems[name] = SemCtr(st.enter_context(nc.semaphore("s_" + name)))
        self.pools = {"d1": [], "d2": []}
        for name, n in (("d1", 6), ("d2", 4)):
            for i in range(n):
                s = SemCtr(st.enter_context(nc.semaphore(f"s_{name}_{i}")))
                self.sems[f"{name}_{i}"] = s
                self.pools[name].append(s)
        self.rr = {"d1": 0, "d2": 0}
        self.chains = {}

    def op(self, eng, fn, waits=(), inc=None, inc_k=1, chain=None):
        ws = [w for w in waits if w is not None]
        if chain is not None and chain in self.chains:
            ws.append(self.chains[chain])
        if inc in ("d1", "d2"):
            sem = self.pools[inc][self.rr[inc] % len(self.pools[inc])]
            self.rr[inc] += 1
            ws.append(Ev(sem, sem.n))
            ev = sem.next(inc_k)
        else:
            ev = self.sems[inc].next(inc_k) if inc else None

        def run(h, ws=ws, ev=ev, fn=fn, inc_k=inc_k):
            for w in ws:
                h.wait_ge(w.sem.h, w.val)
            ins = fn(h)
            if ev is not None:
                ins.then_inc(ev.sem.h, inc_k)
        self.items[eng].append(run)
        if chain is not None:
            assert ev is not None
            self.chains[chain] = ev
        return ev

    ESEM = {"vector": "ve", "scalar": "se", "tensor": "pe", "sync": "ve", "gpsimd": "ve"}

    def mset(self, eng, ap, val, waits=()):
        ev = self.op(eng, lambda h: h.memset(ap, val), waits=waits, inc=self.ESEM[eng])
        self.wait(eng, [ev])
        return ev

    def wait(self, eng, waits):
        ws = [w for w in waits if w is not None]

        def run(h, ws=ws):
            for w in ws:
                h.wait_ge(w.sem.h, w.val)
        self.items[eng].append(run)

    def barrier(self):
        bar = self.sems["bar"]
        pre = [Ev(s, s.n) for k, s in self.sems.items() if k != "bar"]
        for e in self.ENGS:
            self.wait(e, pre)
            self.op(e, lambda h: h.nop(), inc="bar")
        tgt = Ev(bar, bar.n)
        for e in self.ENGS:
            self.wait(e, [tgt])

    def install(self, block):
        for name in self.ENGS:
            prog = self.items[name]
            section = getattr(block, name)

            def body(h, prog=prog):
                for fn in prog:
                    fn(h)
            section(body)


_NAME_N = [0]


def _nm(base):
    _NAME_N[0] += 1
    return f"{base}_{_NAME_N[0]}"


def _rope_tables(pos):
    import jax
    import jax.numpy as jnp
    cpu = jax.devices("cpu")[0]
    with jax.default_device(cpu):
        inv = 1.0 / ROPE_THETA ** (jnp.arange(0, 1024, 2, dtype=jnp.float32) / 1024)
        fr = jnp.asarray(pos, dtype=jnp.float32)[:, None] * inv[None, :]
        c, s = np.asarray(jnp.cos(fr)), np.asarray(jnp.sin(fr))
    return np.ascontiguousarray(c), np.ascontiguousarray(s)


def build_program(debug=(), upto="all", s1_jobs=None, skip_p0=False):
    debug = set(debug)
    nc = bass.Bass("TRN2", target_bir_lowering=False)

    inp = {}
    for name, shape, dt in [
        ("x", [T, D], F32), ("xo", [T, D], F32),
        ("q_cos", [T, 512], F32), ("q_sin", [T, 512], F32),
        ("o_cos", [T, 512], F32), ("o_sin", [T, 512], F32),
        ("w_in", [E, D], F32), ("w_out", [D, D], F32),
        ("w_gu", [GU, D], F32), ("w_dn", [D, FF], F32),
        ("b_in_bc", [128, E], F32), ("b_out_bc", [128, D], F32),
        ("nw_bc", [128, D], F32),
    ]:
        inp[name] = nc.dram_tensor(name, shape, dt, kind="ExternalInput").ap()
    out = nc.dram_tensor("out", [T, D], F32, kind="ExternalOutput").ap()

    scr = {}
    for name, shape, dt in [
        ("winT_hi", [D, E], BF16), ("winT_lo", [D, E], BF16),
        ("woutT_hi", [D, D], BF16), ("woutT_lo", [D, D], BF16),
        ("guT_q", [D, GU], BF16), ("dnT_q", [FF, D], BF16),
        ("q_sp", [T, D], F32), ("h_sp", [TS, FF], F32),
    ]:
        scr[name] = nc.dram_tensor(name, shape, dt).ap()

    dbg_out = {}

    def dbg(name, shape, dt=F32):
        dbg_out[name] = nc.dram_tensor(name, shape, dt, kind="ExternalOutput").ap()
        return dbg_out[name]

    with contextlib.ExitStack() as st:
        P = Prog(nc, st)

        g = {}
        g["identf"] = st.enter_context(nc.sbuf_tensor("identf", [128, 128], F32))
        g["identb"] = st.enter_context(nc.sbuf_tensor("identb", [128, 128], BF16))
        g["iot"] = st.enter_context(nc.sbuf_tensor("iot", [128, 128], F32))
        g["ones_col"] = st.enter_context(nc.sbuf_tensor("ones_col", [128, 8], F32))
        g["ones_row"] = st.enter_context(nc.sbuf_tensor("ones_row", [1, 128], F32))
        g["wconst"] = st.enter_context(nc.sbuf_tensor("wconst", [128, 64], F32))
        g["A_acc"] = [st.enter_context(nc.sbuf_tensor(f"A_acc{i}", [128, 64], F32))
                      for i in range(2)]

        ev_iota = P.op("gpsimd", lambda h: h.iota(
            g["iot"][:], [[1, 128]], channel_multiplier=-1,
            allow_small_or_imprecise_dtypes=True), inc="ve")
        P.op("vector", lambda h: h.tensor_single_scalar(
            g["identf"][:], g["iot"][:], 0.0, ALU.is_equal), waits=[ev_iota], inc="ve")
        P.op("vector", lambda h: h.tensor_single_scalar(
            g["identb"][:], g["iot"][:], 0.0, ALU.is_equal), inc="ve")
        P.mset("vector", g["ones_col"][:], 1.0)
        P.mset("vector", g["ones_row"][:], 1.0)
        P.mset("vector", g["wconst"][:], 1.0)
        P.mset("vector", g["A_acc"][0][:], 0.0)
        P.mset("vector", g["A_acc"][1][:], 0.0)
        P.barrier()

        if not skip_p0:
            with contextlib.ExitStack() as ps:
                emit_wprep_hilo(nc, P, ps, inp["w_in"], scr["winT_hi"], scr["winT_lo"],
                                rows=E, g=g, scale_col=None)
            P.barrier()
        if "winT" in debug:
            emit_scratch_dump(nc, P, dbg("dbg_winT_hi", [D, E], BF16), scr["winT_hi"], D, E, BF16)
            emit_scratch_dump(nc, P, dbg("dbg_winT_lo", [D, E], BF16), scr["winT_lo"], D, E, BF16)
            P.barrier()
        if upto == "p0a":
            emit_dummy_out(nc, P, out)
            with nc.Block() as block:
                P.install(block)
            return nc

        if not skip_p0:
            with contextlib.ExitStack() as ps:
                emit_ffn_means(nc, P, ps, inp["w_gu"], inp["w_dn"], g)
            P.barrier()

        if not skip_p0:
            with contextlib.ExitStack() as ps:
                emit_wprep_quant(nc, P, ps, inp["w_gu"], scr["guT_q"], rows=GU,
                                 thr_pos=g["wconst"][:, 0:1], thr_neg=g["wconst"][:, 1:2],
                                 g=g, cols=D)
            P.barrier()
            with contextlib.ExitStack() as ps:
                emit_wprep_quant(nc, P, ps, inp["w_dn"], scr["dnT_q"], rows=D,
                                 thr_pos=g["wconst"][:, 3:4], thr_neg=g["wconst"][:, 4:5],
                                 g=g, cols=FF)
            P.barrier()
        if "wq" in debug:
            emit_scratch_dump(nc, P, dbg("dbg_guT", [D, GU], BF16), scr["guT_q"], D, GU, BF16)
            emit_scratch_dump(nc, P, dbg("dbg_dnT", [FF, D], BF16), scr["dnT_q"], FF, D, BF16)
            o5 = dbg("dbg_wconst", [128, 64], F32)
            evd = P.op("gpsimd", lambda h: h.dma_start(o5, g["wconst"][:]), inc="d2", inc_k=16)
            P.wait("gpsimd", [evd])
            P.barrier()
        if upto == "p0":
            emit_dummy_out(nc, P, out)
            with nc.Block() as block:
                P.install(block)
            return nc

        with contextlib.ExitStack() as ps:
            emit_sweep1(nc, P, ps, inp, scr, g, s1_jobs=s1_jobs)
        P.barrier()

        P.barrier()
        if "s1" in debug:
            emit_scratch_dump(nc, P, dbg("dbg_qsp", [T, D], F32), scr["q_sp"], T, D, F32)
            o7 = dbg("dbg_A", [128, 64], F32)
            n_t = (s1_jobs if s1_jobs is not None else 2 * NST) * NTT
            evd = P.op("gpsimd", lambda h: h.dma_start(o7, g["A_acc"][n_t % 2][:]),
                       inc="d2", inc_k=16)
            P.wait("gpsimd", [evd])
            P.barrier()
        if upto == "p1":
            emit_dummy_out(nc, P, out)
            with nc.Block() as block:
                P.install(block)
            return nc

        n_s1_tiles = (s1_jobs if s1_jobs is not None else 2 * NST) * NTT
        with contextlib.ExitStack() as ps:
            emit_wprep_hilo(nc, P, ps, inp["w_out"], scr["woutT_hi"], scr["woutT_lo"],
                            rows=D, g=g, scale_col=g["A_acc"][n_s1_tiles % 2])
        P.barrier()

        with contextlib.ExitStack() as ps:
            emit_sweep2(nc, P, ps, inp, scr, out, g)
        P.barrier()

        with nc.Block() as block:
            P.install(block)
    return nc


def emit_dummy_out(nc, P, out):
    with contextlib.ExitStack() as ps:
        z = ps.enter_context(nc.sbuf_tensor(_nm("zz"), [128, D], F32))
        ev = P.mset("vector", z[:], 0.0)
        evs = []
        for r in range(T // 128):
            evs.append(P.op("gpsimd", lambda h, r=r: h.dma_start(
                out[r * 128:(r + 1) * 128, :], z[:]), waits=[ev], inc="d2", inc_k=16))
        P.wait("gpsimd", [evs[-1]])
        P.barrier()


def emit_scratch_dump(nc, P, dst, src, rows, cols, dt):
    with contextlib.ExitStack() as ps:
        buf = ps.enter_context(nc.sbuf_tensor(_nm("dump_buf"), [128, cols], dt))
        ev_st = None
        for r in range(rows // 128):
            ev_ld = P.op("sync", lambda h, r=r: h.dma_start(
                buf[:], src[r * 128:(r + 1) * 128, :]), waits=[ev_st], inc="d1", inc_k=16)
            ev_st = P.op("sync", lambda h, r=r: h.dma_start(
                dst[r * 128:(r + 1) * 128, :], buf[:]), waits=[ev_ld], inc="d1", inc_k=16)
        P.wait("sync", [ev_st])
    P.barrier()


def emit_wprep_hilo(nc, P, ps, w_ap, out_hi, out_lo, rows, g, scale_col):
    PTR = [ps.enter_context(nc.psum_tensor(_nm("ptrf"), [128, 512], F32)) for i in range(2)]
    """Stream w [rows, D] fp32 column-slab-wise; PE-transpose; optionally scale
    per-partition by scale_col[:, dc]; split to bf16 hi/lo; store [D, rows].
    raw is ping-ponged and each slab load is split over parallel DMA queues so
    the next load overlaps this slab's transposes/stores."""
    nec = rows // 128
    raw = [ps.enter_context(nc.sbuf_tensor(_nm("wp_raw"), [128, rows], F32))
           for _ in range(2)]
    hi = ps.enter_context(nc.sbuf_tensor(_nm("wp_hi"), [128, rows], BF16))
    lo = ps.enter_context(nc.sbuf_tensor(_nm("wp_lo"), [128, rows], BF16))
    wsc = ps.enter_context(nc.sbuf_tensor(_nm("wp_sc"), [128, 512], F32))
    ev_out = []
    tg = {}
    gno = [0]
    tr_last = {}
    nsp = 4 if nec >= 8 else 2
    for dc in range(NDC):
        rb = raw[dc % 2]
        st_w = list(ev_out)
        ev_lds = []
        for k in range(nsp):
            a0, a1 = k * nec // nsp, (k + 1) * nec // nsp
            src = w_ap[a0 * 128:a1 * 128, dc * 128:(dc + 1) * 128].rearrange(
                "(a p) d -> p a d", p=128)
            dst = rb[:, a0 * 128:a1 * 128].rearrange("p (a d) -> p a d", d=128)
            ev_lds.append(P.op("sync", lambda h, src=src, dst=dst: h.dma_start(dst, src),
                               waits=[tr_last.get(dc - 2)], inc="d1", inc_k=16))
        ngr = (nec + 3) // 4
        ev_sp = None
        for gi in range(ngr):
            n_in = min(4, nec - gi * 4)
            bank = PTR[gno[0] % 2]

            def tr(h, gi=gi, n_in=n_in, bank=bank, rb=rb):
                ins = None
                for k in range(n_in):
                    ec = gi * 4 + k
                    ins = h.transpose(bank[:, k * 128:(k + 1) * 128],
                                      rb[:, ec * 128:(ec + 1) * 128], g["identf"][:])
                return ins
            ev_t = P.op("tensor", tr, waits=ev_lds + [tg.get(gno[0] - 2)], inc="pe")
            tr_last[dc] = ev_t
            w = n_in * 128
            colsl = slice(gi * 512, gi * 512 + w)
            if scale_col is not None:
                ev_s = P.op("vector", lambda h, bank=bank, w=w, dc=dc: h.tensor_scalar(
                    wsc[:, 0:w], bank[:, 0:w], scale_col[:, dc:dc + 1], None, ALU.mult),
                    waits=[ev_t], inc="ve", chain="wprep_sc")
                ev_hi = P.op("scalar", lambda h, w=w, colsl=colsl: h.copy(
                    hi[:, colsl], wsc[:, 0:w]), waits=[ev_s] + st_w, inc="se")
                ev_lo = P.op("vector", lambda h, w=w, colsl=colsl: h.tensor_sub(
                    lo[:, colsl], wsc[:, 0:w], hi[:, colsl]), waits=[ev_hi], inc="ve",
                    chain="wprep_sc")
            else:
                ev_hi = P.op("scalar", lambda h, bank=bank, w=w, colsl=colsl: h.copy(
                    hi[:, colsl], bank[:, 0:w]), waits=[ev_t] + st_w, inc="se")
                ev_lo = P.op("vector", lambda h, bank=bank, w=w, colsl=colsl: h.tensor_sub(
                    lo[:, colsl], bank[:, 0:w], hi[:, colsl]), waits=[ev_hi], inc="ve")
            tg[gno[0]] = ev_lo
            gno[0] += 1
            ev_sp = ev_lo
        ev_o1 = P.op("gpsimd", lambda h, dc=dc: h.dma_start(
            out_hi[dc * 128:(dc + 1) * 128, :], hi[:, 0:rows]), waits=[ev_sp], inc="d2", inc_k=16)
        ev_o2 = P.op("gpsimd", lambda h, dc=dc: h.dma_start(
            out_lo[dc * 128:(dc + 1) * 128, :], lo[:, 0:rows]), waits=[ev_sp], inc="d2", inc_k=16)
        ev_out = [ev_o1, ev_o2]
    P.wait("gpsimd", ev_out)


def emit_ffn_means(nc, P, ps, w_gu, w_dn, g):
    PMISC = ps.enter_context(nc.psum_tensor(_nm("pmisc"), [128, 512], F32))
    """mean|w| for gu and dn -> wconst cols (thr, -thr, m)."""
    slabs = [ps.enter_context(nc.sbuf_tensor(f"mn_slab{j}", [128, 2048], F32))
             for j in range(4)]
    parts = ps.enter_context(nc.sbuf_tensor("mn_parts", [128, 192], F32))
    rowb = ps.enter_context(nc.sbuf_tensor("mn_rows", [1, 256], F32))
    acc = ps.enter_context(nc.sbuf_tensor("mn_acc", [1, 512], F32))
    ch1 = ps.enter_context(nc.sbuf_tensor("mn_c1", [1, 512], F32))
    ch2 = ps.enter_context(nc.sbuf_tensor("mn_c2", [1, 512], F32))
    P.mset("vector", parts[:], 0.0)
    P.mset("vector", acc[:], 1.0)
    P.mset("vector", rowb[:], 0.0)

    specs = [(w_gu, GU * D, 0), (w_dn, D * FF, 1)]
    ev_reds = {}
    col = 0
    idx = 0
    cols_used = [[], []]
    for w_ap, total, mi in specs:
        flat = w_ap.rearrange("a b -> (a b)")
        n_sl = total // (128 * 2048)
        for i in range(n_sl):
            src = flat[i * 128 * 2048:(i + 1) * 128 * 2048].rearrange("(p m) -> p m", p=128)
            j = idx % 4
            ev_ld = P.op("sync", lambda h, src=src, j=j: h.dma_start(slabs[j][:], src),
                         waits=[ev_reds.get(idx - 4)], inc="d1", inc_k=16)
            ev_reds[idx] = P.op("vector", lambda h, col=col, j=j: h.reduce_sum(
                parts[:, col:col + 1], slabs[j][:], axis=AX.X, apply_absolute_value=True),
                waits=[ev_ld], inc="ve")
            cols_used[mi].append(col)
            col += 1
            idx += 1
    ev_red = ev_reds[idx - 1]
    ev_p = []
    for mi in range(2):
        c0, c1 = cols_used[mi][0], cols_used[mi][-1] + 1
        ev_p.append(P.op("vector", lambda h, mi=mi, c0=c0, c1=c1: h.reduce_sum(
            parts[:, 184 + mi:185 + mi], parts[:, c0:c1], axis=AX.X), waits=[ev_red], inc="ve"))
    ev_t, ev_r = [], []
    for mi in range(2):
        ev_t.append(P.op("tensor", lambda h, mi=mi: h.transpose(
            PMISC[0:1, mi * 128:(mi + 1) * 128],
            parts[:, 184 + mi:185 + mi], g["identf"][:]),
            waits=[ev_p[mi]] + (ev_r[-1:] if ev_r else []), inc="pe"))
        ev_r.append(P.op("vector", lambda h, mi=mi: h.tensor_copy(
            rowb[0:1, mi * 128:(mi + 1) * 128], PMISC[0:1, mi * 128:(mi + 1) * 128]),
            waits=[ev_t[mi]], inc="ve"))
    ev_s0 = P.op("vector", lambda h: h.reduce_sum(
        acc[0:1, 0:1], rowb[0:1, 0:128], axis=AX.X), waits=ev_r, inc="ve")
    ev_s1 = P.op("vector", lambda h: h.reduce_sum(
        acc[0:1, 256:257], rowb[0:1, 128:256], axis=AX.X), inc="ve")
    P.mset("vector", ch1[:], 1.0)
    P.mset("vector", ch2[:], 1.0)
    Ninv = [1.0 / (GU * D), 1.0 / (D * FF)]
    ev = P.op("vector", lambda h: h.tensor_scalar(
        ch1[0:1, 0:256], acc[0:1, 0:256], Ninv[0], None, ALU.mult),
        waits=[ev_s0, ev_s1], inc="ve")
    ev = P.op("vector", lambda h: h.tensor_scalar(
        ch1[0:1, 256:512], acc[0:1, 256:512], Ninv[1], None, ALU.mult), waits=[ev], inc="ve")
    ev = P.op("vector", lambda h: h.tensor_single_scalar(
        ch2[:], ch1[:], 1e-5, ALU.max), waits=[ev], inc="ve")
    ev = P.op("vector", lambda h: h.tensor_scalar(
        ch1[:], ch2[:], 0.5, None, ALU.mult), waits=[ev], inc="ve")
    items = [(0, ch1, 0), (2, ch2, 0), (3, ch1, 256), (5, ch2, 256)]
    ev_b = []
    for k, (colw, srcbuf, sc) in enumerate(items):
        evm = P.op("tensor", lambda h, k=k, srcbuf=srcbuf, sc=sc: h.matmul(
            PMISC[:, 256 + k:257 + k], g["ones_row"][:], srcbuf[0:1, sc:sc + 1],
            start=True, stop=True),
            waits=[ev] + (ev_b[-1:] if ev_b else []), inc="pe")
        ev_b.append(P.op("vector", lambda h, k=k, colw=colw: h.tensor_copy(
            g["wconst"][:, colw:colw + 1], PMISC[:, 256 + k:257 + k]), waits=[evm], inc="ve"))
    ev = P.op("vector", lambda h: h.tensor_scalar(
        g["wconst"][:, 1:2], g["wconst"][:, 0:1], -1.0, None, ALU.mult), waits=ev_b, inc="ve")
    P.op("vector", lambda h: h.tensor_scalar(
        g["wconst"][:, 4:5], g["wconst"][:, 3:4], -1.0, None, ALU.mult), waits=[ev], inc="ve")


def emit_wprep_quant(nc, P, ps, w_ap, out_q, rows, thr_pos, thr_neg, g, cols):
    PTR = [ps.enter_context(nc.psum_tensor(_nm("ptrb"), [128, 1024], BF16)) for i in range(2)]
    """wq' = sign(w-thr)+sign(w+thr) (natural layout) then bf16 transpose to
    out_q [cols, rows]."""
    nec = rows // 128
    ncc = cols // 128
    raw = [ps.enter_context(nc.sbuf_tensor(_nm("wq_raw"), [128, rows], F32))
           for _ in range(2)]
    sa = ps.enter_context(nc.sbuf_tensor(_nm("wq_sa"), [128, rows], BF16))
    sb2 = ps.enter_context(nc.sbuf_tensor(_nm("wq_sb"), [128, rows], BF16))
    wq = ps.enter_context(nc.sbuf_tensor(_nm("wq_wq"), [128, rows], BF16))
    wqT = ps.enter_context(nc.sbuf_tensor(_nm("wq_wqT"), [128, rows], BF16))
    ev_out = None
    sign_done = {}
    tr_done = {}
    qtg = {}
    qgno = [0]
    nsp = 4 if nec >= 8 else 2
    for dc in range(ncc):
        rb = raw[dc % 2]
        st_prev = ev_out
        ev_lds = []
        for k in range(nsp):
            a0, a1 = k * nec // nsp, (k + 1) * nec // nsp
            src = w_ap[a0 * 128:a1 * 128, dc * 128:(dc + 1) * 128].rearrange(
                "(a p) d -> p a d", p=128)
            dst = rb[:, a0 * 128:a1 * 128].rearrange("p (a d) -> p a d", d=128)
            ev_lds.append(P.op("sync", lambda h, src=src, dst=dst: h.dma_start(dst, src),
                               waits=[sign_done.get(dc - 2)], inc="d1", inc_k=16))
        ev_a = P.op("scalar", lambda h, rb=rb: h.activation(
            sa[:, 0:rows], rb[:, 0:rows], AF.Sign, bias=thr_neg),
            waits=ev_lds + [sign_done.get(dc - 1)], inc="se")
        ev_b = P.op("scalar", lambda h, rb=rb: h.activation(
            sb2[:, 0:rows], rb[:, 0:rows], AF.Sign, bias=thr_pos),
            waits=[ev_a], inc="se")
        ev_w = P.op("vector", lambda h: h.tensor_add(
            wq[:, 0:rows], sa[:, 0:rows], sb2[:, 0:rows]),
            waits=[ev_a, ev_b, tr_done.get(dc - 1)], inc="ve")
        sign_done[dc] = ev_w
        ngr = (nec + 7) // 8
        ev_last = None
        for gi in range(ngr):
            n_in = min(8, nec - gi * 8)
            bank = PTR[(qgno[0] + gi) % 2][:]

            def tr(h, gi=gi, n_in=n_in, bank=bank):
                ins = None
                for k in range(n_in):
                    ec = gi * 8 + k
                    ins = h.transpose(bank[:, k * 128:(k + 1) * 128],
                                      wq[:, ec * 128:(ec + 1) * 128], g["identb"][:])
                return ins
            ev_t = P.op("tensor", tr, waits=[ev_w, qtg.get(qgno[0] + gi - 2)], inc="pe")
            tr_done[dc] = ev_t
            w = n_in * 128
            ev_e = P.op("vector", lambda h, gi=gi, w=w, bank=bank: h.tensor_copy(
                wqT[:, gi * 1024:gi * 1024 + w], bank[:, 0:w]),
                waits=[ev_t, st_prev], inc="ve")
            qtg[qgno[0] + gi] = ev_e
            ev_last = ev_e
        qgno[0] += ngr
        ev_out = P.op("gpsimd", lambda h, dc=dc: h.dma_start(
            out_q[dc * 128:(dc + 1) * 128, :], wqT[:, 0:rows]), waits=[ev_last], inc="d2", inc_k=16)
    P.wait("gpsimd", [ev_out])


def emit_ln_chain(P, stats, C_SUM, C_SQS, C_M, C_T, C_RS, waits):
    P.mset("vector", stats[:, C_T:C_T + 64], 1.0)
    P.mset("vector", stats[:, C_T:C_T + 64], 1.0)
    ev = P.op("vector", lambda h: h.tensor_scalar(
        stats[:, C_M:C_M + 64], stats[:, C_SUM:C_SUM + 64], 1.0 / D, None, ALU.mult),
        waits=waits, inc="ve")
    ev = P.op("vector", lambda h: h.tensor_mul(
        stats[:, C_T:C_T + 64], stats[:, C_M:C_M + 64], stats[:, C_M:C_M + 64]),
        waits=[ev], inc="ve")
    ev = P.op("vector", lambda h: h.scalar_tensor_tensor(
        stats[:, C_T:C_T + 64], stats[:, C_SQS:C_SQS + 64], 1.0 / D, stats[:, C_T:C_T + 64],
        ALU.mult, ALU.subtract), waits=[ev], inc="ve")
    ev = P.op("vector", lambda h: h.tensor_scalar(
        stats[:, C_RS:C_RS + 64], stats[:, C_T:C_T + 64], EPS, None, ALU.add),
        waits=[ev], inc="ve")
    ev = P.op("scalar", lambda h: h.sqrt(stats[:, C_T:C_T + 64], stats[:, C_RS:C_RS + 64]),
              waits=[ev], inc="se")
    ev = P.op("vector", lambda h: h.reciprocal(stats[:, C_RS:C_RS + 64], stats[:, C_T:C_T + 64]),
              waits=[ev], inc="ve")
    return ev


def emit_rope(P, buf, base, cs_ap, sn_ap, rtmp, waits, tag="rope"):
    EVN = buf[:, base:base + 1024:2]
    ODD = buf[:, base + 1:base + 1024:2]
    P.op("vector", lambda h: h.tensor_mul(rtmp[:, 0:512], EVN, cs_ap), waits=waits,
         inc="ve", chain=tag)
    P.op("vector", lambda h: h.tensor_mul(rtmp[:, 512:1024], ODD, cs_ap), inc="ve", chain=tag)
    P.op("vector", lambda h: h.scalar_tensor_tensor(
        rtmp[:, 1024:1536], ODD, -1.0, sn_ap, ALU.mult, ALU.mult), inc="ve", chain=tag)
    P.op("vector", lambda h: h.tensor_mul(rtmp[:, 1536:2048], EVN, sn_ap), inc="ve", chain=tag)
    P.op("vector", lambda h: h.tensor_add(EVN, rtmp[:, 0:512], rtmp[:, 1024:1536]),
         inc="ve", chain=tag)
    return P.op("vector", lambda h: h.tensor_add(ODD, rtmp[:, 512:1024], rtmp[:, 1536:2048]),
                inc="ve", chain=tag)


def emit_quant_ladder(P, x_ap, w, sx_col, xq_ap, lad, waits, a_chain=None):
    a, vv, s1, b, r0, r1, sgn = (lad[k] for k in ("a", "vv", "s1", "b", "r0", "r1", "sgn"))
    ev_g = P.op("scalar", lambda h: h.activation(sgn[:, 0:w], x_ap, AF.Sign, scale=sx_col),
                waits=waits, inc="se")
    ev_a = P.op("scalar", lambda h: h.activation(a[:, 0:w], x_ap, AF.Abs, scale=sx_col),
                waits=waits, inc="se", chain=a_chain)
    ev = P.op("vector", lambda h: h.tensor_scalar(
        vv[:, 0:w], a[:, 0:w], 128.0, None, ALU.add), waits=[ev_a], inc="ve")
    ev = P.op("vector", lambda h: h.tensor_scalar(
        s1[:, 0:w], vv[:, 0:w], -128.0, None, ALU.add), waits=[ev], inc="ve")
    r = s1
    rbuf = [r0, r1]
    for i, k in enumerate([64, 32, 16, 8, 4, 2, 1]):
        nxt = rbuf[i % 2]
        ev = P.op("vector", lambda h, r=r, k=k, w=w: h.tensor_scalar(
            b[:, 0:w], r[:, 0:w], float(k) - 0.5, float(-k), ALU.is_ge, ALU.mult),
            waits=[ev], inc="ve")
        ev = P.op("vector", lambda h, r=r, nxt=nxt, w=w: h.tensor_add(
            nxt[:, 0:w], r[:, 0:w], b[:, 0:w]), waits=[ev], inc="ve")
        r = nxt
    ev = P.op("vector", lambda h, r=r, w=w: h.tensor_sub(
        b[:, 0:w], s1[:, 0:w], r[:, 0:w]), waits=[ev], inc="ve")
    return P.op("vector", lambda h, w=w: h.tensor_mul(
        xq_ap, b[:, 0:w], sgn[:, 0:w]), waits=[ev, ev_g], inc="ve")


def emit_sweep1(nc, P, ps, inp, scr, g, s1_jobs=None):
    PMM = [ps.enter_context(nc.psum_tensor(_nm("pmm"), [128, 512], F32)) for i in range(4)]
    PTR = [ps.enter_context(nc.psum_tensor(_nm("ptrf"), [128, 512], F32)) for i in range(2)]
    PA = ps.enter_context(nc.psum_tensor(_nm("papart"), [128, 512], F32))
    xT_hi = ps.enter_context(nc.sbuf_tensor("s1_xT_hi", [128, NDC * TS], BF16))
    xT_lo = ps.enter_context(nc.sbuf_tensor("s1_xT_lo", [128, NDC * TS], BF16))
    kst = ps.enter_context(nc.sbuf_tensor("s1_k", [128, NTT * D], F32))
    vst = ps.enter_context(nc.sbuf_tensor("s1_v", [128, NTT * D], F32))
    xln = ps.enter_context(nc.sbuf_tensor("s1_xln", [128, D], F32))
    qev = [ps.enter_context(nc.sbuf_tensor(f"s1_qe{i}", [128, 512], F32)) for i in range(2)]
    sejunk = ps.enter_context(nc.sbuf_tensor("s1_sejunk", [128, D], F32))
    fmkv = ps.enter_context(nc.sbuf_tensor("s1_fmkv", [128, D], F32))
    cs = ps.enter_context(nc.sbuf_tensor("s1_cs", [128, NTT * 512], F32))
    sn = ps.enter_context(nc.sbuf_tensor("s1_sn", [128, NTT * 512], F32))
    rtmp = ps.enter_context(nc.sbuf_tensor("s1_rt", [128, D], F32))
    binb = [ps.enter_context(nc.sbuf_tensor(f"s1_bin{i}", [128, 512], F32)) for i in range(2)]
    wt = [ps.enter_context(nc.sbuf_tensor(f"s1_wt{i}", [128, 512], BF16)) for i in range(4)]
    stats = ps.enter_context(nc.sbuf_tensor("s1_st", [128, 512], F32))
    SUM, SQS, M, TT_, RS, KS, T2, RN = 0, 64, 128, 192, 256, 320, 384, 448
    _ev = P.mset("vector", stats[:], 1.0)
    P.wait("scalar", [_ev])

    n_tiles_total = (NST + NST) * NTT
    tile_no = [0]
    a_rd_prev = [None]
    tg = {}
    gno = [0]
    prev_xln = [None]
    prev_kst_done = [None] * NTT
    prev_cs_done = [None]
    prev_xT_done = [None]
    prev_bias_done = [None, None]
    prev_qe_done = [None, None]
    ev_bank_free = [None] * NTT

    jobs = [("own", s) for s in range(NST)] + [("dup", s) for s in range(NST)]
    if s1_jobs is not None:
        jobs = jobs[:s1_jobs]
    n_tiles_total = len(jobs) * NTT
    for kind, s in jobs:
        x_ap = inp["x"] if kind == "own" else inp["xo"]
        cos_ap = inp["q_cos"] if kind == "own" else inp["o_cos"]
        sin_ap = inp["q_sin"] if kind == "own" else inp["o_sin"]
        oc_list = list(range(12)) if kind == "own" else list(range(4, 12))
        tok0 = s * TS

        ev_x, ev_cs = [], []
        for tt in range(NTT):
            r0 = tok0 + tt * 128
            ev_x.append(P.op("sync", lambda h, tt=tt, r0=r0, x_ap=x_ap: h.dma_start(
                kst[:, tt * D:(tt + 1) * D], x_ap[r0:r0 + 128, :]),
                waits=[prev_kst_done[tt]], inc="d1", inc_k=16))
            ev_cs.append(P.op("sync", lambda h, tt=tt, r0=r0, cos_ap=cos_ap: h.dma_start(
                cs[:, tt * 512:(tt + 1) * 512], cos_ap[r0:r0 + 128, :]),
                waits=[prev_cs_done[0]], inc="d1", inc_k=16))
            ev_cs.append(P.op("sync", lambda h, tt=tt, r0=r0, sin_ap=sin_ap: h.dma_start(
                sn[:, tt * 512:(tt + 1) * 512], sin_ap[r0:r0 + 128, :]), inc="d1", inc_k=16))
        if S1_PART == "dma":
            for tt in range(NTT):
                prev_kst_done[tt] = ev_x[tt]
            continue
        ev_red, ev_sq = [], []
        for tt in range(NTT):
            ev_red.append(P.op("vector", lambda h, tt=tt: h.reduce_sum(
                stats[:, SUM + tt:SUM + tt + 1], kst[:, tt * D:(tt + 1) * D], axis=AX.X),
                waits=[ev_x[tt]], inc="ve"))
            if S1_PART != "nosq":
                ev_sq.append(P.op("scalar", lambda h, tt=tt: h.activation(
                    sejunk[:], kst[:, tt * D:(tt + 1) * D], AF.Square,
                    accum_out=stats[:, SQS + tt:SQS + tt + 1]), waits=[ev_x[tt]], inc="se",
                    chain="s1jnk"))
        if S1_PART == "stats" or S1_PART == "nosq":
            for tt in range(NTT):
                prev_kst_done[tt] = ev_red[tt]
            continue
        ev_ch = emit_ln_chain(P, stats, SUM, SQS, M, TT_, RS, waits=ev_red + ev_sq)

        if S1_PART == "chain":
            for tt in range(NTT):
                prev_kst_done[tt] = ev_red[tt]
            continue
        ev_xT = []
        for tt in range(NTT):
            ev_ap = P.op("vector", lambda h, tt=tt: h.tensor_scalar(
                xln[:], kst[:, tt * D:(tt + 1) * D],
                stats[:, M + tt:M + tt + 1], stats[:, RS + tt:RS + tt + 1],
                ALU.subtract, ALU.mult), waits=[ev_ch, prev_xln[0]], inc="ve")
            if S1_PART == "apply":
                prev_xln[0] = ev_ap
                continue
            ev_last = None
            for gi in range(4):
                bank = PTR[gno[0] % 2]

                def tr(h, gi=gi, bank=bank):
                    ins = None
                    for k in range(4):
                        dc = gi * 4 + k
                        ins = h.transpose(bank[:, k * 128:(k + 1) * 128],
                                          xln[:, dc * 128:(dc + 1) * 128], g["identf"][:])
                    return ins
                wts = [ev_ap, tg.get(gno[0] - 2)]
                if tt == 0 and gi == 0:
                    wts.append(prev_xT_done[0])
                ev_t = P.op("tensor", tr, waits=wts, inc="pe")
                if S1_PART == "trnoev":
                    tg[gno[0]] = ev_t
                    gno[0] += 1
                    prev_xln[0] = ev_t
                    ev_last = ev_t
                    continue
                hi3 = xT_hi[:].rearrange("p (dc t) -> p dc t", t=TS)[
                    :, gi * 4:(gi + 1) * 4, tt * 128:(tt + 1) * 128]
                lo3 = xT_lo[:].rearrange("p (dc t) -> p dc t", t=TS)[
                    :, gi * 4:(gi + 1) * 4, tt * 128:(tt + 1) * 128]
                bk3 = bank[:].rearrange("p (k c) -> p k c", c=128)
                ev_hi = P.op("scalar", lambda h, hi3=hi3, bk3=bk3: h.copy(hi3, bk3),
                             waits=[ev_t], inc="se")
                ev_lo = P.op("vector", lambda h, lo3=lo3, bk3=bk3, hi3=hi3: h.tensor_sub(
                    lo3, bk3, hi3), waits=[ev_hi], inc="ve")
                tg[gno[0]] = ev_lo
                gno[0] += 1
                ev_last = ev_lo
            prev_xln[0] = ev_last
            ev_xT.append(ev_last)
        if S1_PART in ("ln", "apply", "trnoev"):
            for tt in range(NTT):
                prev_kst_done[tt] = ev_red[tt]
            continue

        ev_wu = {}
        wno = [0]
        ev_mm = None
        for oi, oc in enumerate(oc_list):
            bb = oi % 2
            ev_bias = P.op("sync", lambda h, oc=oc, bb=bb: h.dma_start(
                binb[bb][:], inp["b_in_bc"][:, oc * 512:(oc + 1) * 512]),
                waits=[prev_bias_done[bb]], inc="d1", inc_k=16)
            for dc in range(NDC):
                wi = wno[0]
                ridx = (wi % 2) * 2
                ev_w1 = P.op("sync", lambda h, oc=oc, dc=dc, ridx=ridx: h.dma_start(
                    wt[ridx][:], scr["winT_hi"][dc * 128:(dc + 1) * 128, oc * 512:(oc + 1) * 512]),
                    waits=[ev_wu.get(wi - 2)], inc="d1", inc_k=16)
                ev_w2 = P.op("sync", lambda h, oc=oc, dc=dc, ridx=ridx: h.dma_start(
                    wt[ridx + 1][:], scr["winT_lo"][dc * 128:(dc + 1) * 128, oc * 512:(oc + 1) * 512]),
                    inc="d1", inc_k=16)

                def mm(h, dc=dc, ridx=ridx):
                    ins = None
                    for tt in range(NTT):
                        stsl = slice(dc * TS + tt * 128, dc * TS + (tt + 1) * 128)
                        ins = h.matmul(PMM[tt][:], xT_hi[:, stsl], wt[ridx][:],
                                       start=(dc == 0), stop=False)
                        ins = h.matmul(PMM[tt][:], xT_hi[:, stsl], wt[ridx + 1][:],
                                       start=False, stop=False)
                        ins = h.matmul(PMM[tt][:], xT_lo[:, stsl], wt[ridx][:],
                                       start=False, stop=(dc == NDC - 1))
                    return ins
                wts = [ev_w1, ev_w2]
                if dc == 0:
                    wts += ev_xT + ev_bank_free
                ev_mm = P.op("tensor", mm, waits=wts, inc="pe")
                ev_wu[wi] = ev_mm
                wno[0] += 1
            for tt in range(NTT):
                if oc < 4:
                    qb_i = tt % 2
                    ev_e = P.op("vector", lambda h, tt=tt, qb_i=qb_i, bb=bb: h.tensor_add(
                        qev[qb_i][:], PMM[tt][:], binb[bb][:]),
                        waits=[ev_mm, ev_bias, prev_qe_done[qb_i]], inc="ve")
                    r0 = tok0 + tt * 128
                    ev_sp = P.op("gpsimd", lambda h, r0=r0, oc=oc, qb_i=qb_i: h.dma_start(
                        scr["q_sp"][r0:r0 + 128, oc * 512:(oc + 1) * 512], qev[qb_i][:]),
                        waits=[ev_e], inc="d2", inc_k=16)
                    prev_qe_done[qb_i] = ev_sp
                    ev_bank_free[tt] = ev_e
                elif oc < 8:
                    ev_e = P.op("vector", lambda h, tt=tt, oc=oc, bb=bb: h.tensor_add(
                        kst[:, tt * D + (oc - 4) * 512: tt * D + (oc - 3) * 512],
                        PMM[tt][:], binb[bb][:]), waits=[ev_mm, ev_bias], inc="ve")
                    ev_bank_free[tt] = ev_e
                else:
                    ev_e = P.op("vector", lambda h, tt=tt, oc=oc, bb=bb: h.tensor_add(
                        vst[:, tt * D + (oc - 8) * 512: tt * D + (oc - 7) * 512],
                        PMM[tt][:], binb[bb][:]), waits=[ev_mm, ev_bias], inc="ve")
                    ev_bank_free[tt] = ev_e
            prev_bias_done[bb] = ev_bank_free[NTT - 1]
        prev_xT_done[0] = ev_mm
        if S1_PART == "mm":
            for tt in range(NTT):
                prev_kst_done[tt] = ev_bank_free[tt]
            continue

        ev_ks = []
        ev_rope_last = None
        for tt in range(NTT):
            base = tt * D
            ev_r = emit_rope(P, kst, base, cs[:, tt * 512:(tt + 1) * 512],
                             sn[:, tt * 512:(tt + 1) * 512], rtmp,
                             waits=[ev_bank_free[tt]] + ev_cs)
            ev_rope_last = ev_r
            ev_ks.append(P.op("scalar", lambda h, base=base, tt=tt: h.activation(
                sejunk[:], kst[:, base:base + D], AF.Square,
                accum_out=stats[:, KS + tt:KS + tt + 1]), waits=[ev_r], inc="se",
                chain="s1jnk"))
        prev_cs_done[0] = ev_rope_last
        P.mset("vector", stats[:, T2:T2 + 64], 1.0)
        ev_msk = P.mset("vector", stats[:, T2:T2 + 64], 1.0)
        ev_sd = P.op("scalar", lambda h: h.sqrt(
            stats[:, T2:T2 + 64], stats[:, KS:KS + 64]), waits=ev_ks + [ev_msk], inc="se")
        ev_cl = P.op("vector", lambda h: h.tensor_single_scalar(
            stats[:, TT_:TT_ + 64], stats[:, T2:T2 + 64], 1e-12, ALU.max),
            waits=[ev_sd], inc="ve")
        ev_rn = P.op("vector", lambda h: h.reciprocal(
            stats[:, RN:RN + 64], stats[:, TT_:TT_ + 64]), waits=[ev_cl], inc="ve")
        if S1_PART == "rope":
            for tt in range(NTT):
                prev_kst_done[tt] = ev_ks[tt]
            continue
        ev_am_prev = None
        for tt in range(NTT):
            ksl = slice(tt * D, (tt + 1) * D)
            ev_f = P.op("vector", lambda h, ksl=ksl, tt=tt: h.scalar_tensor_tensor(
                fmkv[:], kst[:, ksl], stats[:, RN + tt:RN + tt + 1], vst[:, ksl],
                ALU.mult, ALU.mult), waits=[ev_rn, ev_am_prev], inc="ve")
            tile_no[0] += 1
            tn = tile_no[0]

            def amm(h):
                ins = None
                for dc in range(NDC):
                    ins = h.matmul(PA[:, dc:dc + 1], fmkv[:, dc * 128:(dc + 1) * 128],
                                   g["ones_col"][:, 0:1],
                                   start=(dc == 0), stop=(dc == NDC - 1),
                                   skip_group_check=True)
                return ins
            ev_am = P.op("tensor", amm, waits=[ev_f, a_rd_prev[0]], inc="pe")
            ev_aa = P.op("vector", lambda h, tn=tn: h.tensor_add(
                g["A_acc"][tn % 2][:, 0:16], g["A_acc"][(tn - 1) % 2][:, 0:16],
                PA[:, 0:16]), waits=[ev_am], inc="ve")
            a_rd_prev[0] = ev_aa
            ev_am_prev = ev_am
            prev_kst_done[tt] = ev_am


def emit_sweep2(nc, P, ps, inp, scr, out, g):
    PMM = [ps.enter_context(nc.psum_tensor(_nm("pmm"), [128, 512], F32)) for i in range(4)]
    PTR = [ps.enter_context(nc.psum_tensor(_nm("ptrf"), [128, 512], F32)) for i in range(2)]
    PTB = [ps.enter_context(nc.psum_tensor(_nm("ptrb"), [128, 1024], BF16)) for i in range(2)]
    xst = ps.enter_context(nc.sbuf_tensor("s2_xst", [128, NTT * D], F32))
    att = ps.enter_context(nc.sbuf_tensor("s2_att", [128, NTT * D], F32))
    tmpA = ps.enter_context(nc.sbuf_tensor("s2_tmpA", [128, D], F32))
    tmpB = ps.enter_context(nc.sbuf_tensor("s2_tmpB", [128, D], F32))
    ghold = ps.enter_context(nc.sbuf_tensor("s2_gh", [128, NTT * 512], F32))
    uh = ps.enter_context(nc.sbuf_tensor("s2_uh", [128, 512], F32))
    sqg = ps.enter_context(nc.sbuf_tensor("s2_sqg", [128, 512], F32))
    hbuf = [ps.enter_context(nc.sbuf_tensor(f"s2_hb{i}", [128, 512], F32)) for i in range(2)]
    cs = [ps.enter_context(nc.sbuf_tensor("s2_cs0", [128, 512], F32))]
    sn = [ps.enter_context(nc.sbuf_tensor("s2_sn0", [128, 512], F32))]
    bout = [ps.enter_context(nc.sbuf_tensor(f"s2_bo{i}", [128, 512], F32)) for i in range(2)]
    nwb = ps.enter_context(nc.sbuf_tensor("s2_nw", [128, D], F32))
    stats = ps.enter_context(nc.sbuf_tensor("s2_st", [128, 768], F32))
    qT_hi = ps.enter_context(nc.sbuf_tensor("s2_qTh", [128, NDC * TS], BF16))
    qT_lo = ps.enter_context(nc.sbuf_tensor("s2_qTl", [128, NDC * TS], BF16))
    hqA = ps.enter_context(nc.sbuf_tensor("s2_hqA", [128, 32 * TS], BF16))
    hq = ps.enter_context(nc.sbuf_tensor("s2_hq", [128, FF], BF16))
    vvb = ps.enter_context(nc.sbuf_tensor("s2_vv", [128, D], BF16))
    sgnb = ps.enter_context(nc.sbuf_tensor("s2_sgn", [128, D], BF16))
    wt = [ps.enter_context(nc.sbuf_tensor(f"s2_wt{i}", [128, 512], BF16)) for i in range(4)]

    sejunk = tmpB[:]
    gx4 = ghold[:, 0:D]
    lad = {"a": tmpB, "vv": vvb, "s1": xst[:, 0:D], "b": xst[:, D:2 * D],
           "r0": xst[:, 2 * D:3 * D], "r1": xst[:, 3 * D:4 * D], "sgn": sgnb}

    def hqT_ap(fc, c0, c1):
        if fc < 32:
            base = fc * TS
            return hqA[:, base + c0:base + c1]
        base = (fc - 32) * TS
        return qT_lo[:, base + c0:base + c1]

    ffn = xst

    SUM1, SQS1, M1, T1_, RS1 = 0, 64, 128, 192, 256
    SUM2, SQS2, M2, RS2 = 320, 384, 448, 512
    AMX, SX, SCG = 576, 608, 640
    HM = 704
    _ev1 = P.mset("vector", stats[:, 0:512], 1.0)
    _ev2 = P.mset("vector", stats[:, 512:768], 1.0)
    P.wait("scalar", [_ev1, _ev2])
    ev_nw = P.op("sync", lambda h: h.dma_start(nwb[:], inp["nw_bc"]), inc="d1", inc_k=16)

    prev_stage_done = [None] * NTT
    prev_cs_done = [None]
    prev_qT_done = [None]
    prev_bo_done = [None, None]
    prev_e2 = [None]
    ev_bank_free = [None] * NTT
    ev_wu = {}
    wno = [0]
    tg = {}
    gno = [0]

    def wload(src_ap, r0, c0, waits=()):
        wi = wno[0]
        ridx = wi % 4
        ev = P.op("sync", lambda h, ridx=ridx: h.dma_start(
            wt[ridx][:], src_ap[r0:r0 + 128, c0:c0 + 512]),
            waits=[ev_wu.get(wi - 4)] + list(waits), inc="d1", inc_k=16)
        wno[0] += 1
        return wi, ridx, ev

    for s in range(NST):
        tok0 = s * TS
        ev_x, ev_q = [], []
        for tt in range(NTT):
            r0 = tok0 + tt * 128
            ev_x.append(P.op("sync", lambda h, tt=tt, r0=r0: h.dma_start(
                xst[:, tt * D:(tt + 1) * D], inp["x"][r0:r0 + 128, :]),
                waits=[prev_stage_done[tt]], inc="d1", inc_k=16))
            ev_q.append(P.op("sync", lambda h, tt=tt, r0=r0: h.dma_start(
                att[:, tt * D:(tt + 1) * D], scr["q_sp"][r0:r0 + 128, :]),
                waits=[prev_stage_done[tt]], inc="d1", inc_k=16))
        ev_red, ev_sq = [], []
        for tt in range(NTT):
            ev_red.append(P.op("vector", lambda h, tt=tt: h.reduce_sum(
                stats[:, SUM1 + tt:SUM1 + tt + 1], xst[:, tt * D:(tt + 1) * D], axis=AX.X),
                waits=[ev_x[tt]], inc="ve"))
            ev_sq.append(P.op("scalar", lambda h, tt=tt: h.activation(
                sejunk, xst[:, tt * D:(tt + 1) * D], AF.Square,
                accum_out=stats[:, SQS1 + tt:SQS1 + tt + 1]), waits=[ev_x[tt]], inc="se",
                chain="s2_tmpB"))
        ev_ch1 = emit_ln_chain(P, stats, SUM1, SQS1, M1, T1_, RS1, waits=ev_red + ev_sq)
        ev_qT = []
        for tt in range(NTT):
            base = tt * D
            pp = 0
            r0 = tok0 + tt * 128
            evc = P.op("sync", lambda h, pp=pp, r0=r0: h.dma_start(
                cs[pp][:], inp["q_cos"][r0:r0 + 128, :]),
                waits=[prev_cs_done[pp]], inc="d1", inc_k=16)
            evs_ = P.op("sync", lambda h, pp=pp, r0=r0: h.dma_start(
                sn[pp][:], inp["q_sin"][r0:r0 + 128, :]), inc="d1", inc_k=16)
            ev_r = emit_rope(P, att, base, cs[pp][:], sn[pp][:], tmpA,
                             waits=[ev_q[tt], evc, evs_], tag="s2_tmpA")
            prev_cs_done[pp] = ev_r
            ev_last = None
            for gi in range(4):
                bank = PTR[gno[0] % 2]

                def tr(h, gi=gi, bank=bank, base=base):
                    ins = None
                    for k in range(4):
                        dc = gi * 4 + k
                        ins = h.transpose(bank[:, k * 128:(k + 1) * 128],
                                          att[:, base + dc * 128:base + (dc + 1) * 128],
                                          g["identf"][:])
                    return ins
                ev_t = P.op("tensor", tr, waits=[ev_r, tg.get(gno[0] - 2)], inc="pe")
                hi3 = qT_hi[:].rearrange("p (dc t) -> p dc t", t=TS)[
                    :, gi * 4:(gi + 1) * 4, tt * 128:(tt + 1) * 128]
                lo3 = qT_lo[:].rearrange("p (dc t) -> p dc t", t=TS)[
                    :, gi * 4:(gi + 1) * 4, tt * 128:(tt + 1) * 128]
                bk3 = bank[:].rearrange("p (k c) -> p k c", c=128)
                ev_hi = P.op("scalar", lambda h, hi3=hi3, bk3=bk3: h.copy(hi3, bk3),
                             waits=[ev_t], inc="se")
                ev_lo = P.op("vector", lambda h, lo3=lo3, bk3=bk3, hi3=hi3: h.tensor_sub(
                    lo3, bk3, hi3), waits=[ev_hi], inc="ve")
                tg[gno[0]] = ev_lo
                gno[0] += 1
                ev_last = ev_lo
            ev_qT.append(ev_last)
        ev_mm = None
        for oc2 in range(4):
            bb = oc2 % 2
            ev_bo = P.op("sync", lambda h, oc2=oc2, bb=bb: h.dma_start(
                bout[bb][:], inp["b_out_bc"][:, oc2 * 512:(oc2 + 1) * 512]),
                waits=[prev_bo_done[bb]], inc="d1", inc_k=16)
            for dc in range(NDC):
                wi1, r1_, ev_w1 = wload(scr["woutT_hi"], dc * 128, oc2 * 512)
                wi2, r2_, ev_w2 = wload(scr["woutT_lo"], dc * 128, oc2 * 512)

                def mm(h, dc=dc, r1_=r1_, r2_=r2_):
                    ins = None
                    for tt in range(NTT):
                        stsl = slice(dc * TS + tt * 128, dc * TS + (tt + 1) * 128)
                        ins = h.matmul(PMM[tt][:], qT_hi[:, stsl], wt[r1_][:],
                                       start=(dc == 0), stop=False)
                        ins = h.matmul(PMM[tt][:], qT_hi[:, stsl], wt[r2_][:],
                                       start=False, stop=False)
                        ins = h.matmul(PMM[tt][:], qT_lo[:, stsl], wt[r1_][:],
                                       start=False, stop=(dc == NDC - 1))
                    return ins
                wts = [ev_w1, ev_w2]
                if dc == 0:
                    wts += ev_qT + ev_bank_free
                ev_mm = P.op("tensor", mm, waits=wts, inc="pe")
                ev_wu[wi1] = ev_mm
                ev_wu[wi2] = ev_mm
            for tt in range(NTT):
                csl = slice(oc2 * 512, (oc2 + 1) * 512)
                ev_e1 = P.op("vector", lambda h, tt=tt, bb=bb: h.tensor_add(
                    uh[:], PMM[tt][:], bout[bb][:]),
                    waits=[ev_mm, ev_bo, prev_e2[0]], inc="ve", chain="s2_uh")
                ev_t1 = P.op("vector", lambda h, tt=tt, csl=csl: h.tensor_scalar(
                    tmpA[:, 0:512], xst[:, tt * D + csl.start:tt * D + csl.stop],
                    stats[:, M1 + tt:M1 + tt + 1], stats[:, RS1 + tt:RS1 + tt + 1],
                    ALU.subtract, ALU.mult), waits=[ev_ch1], inc="ve", chain="s2_tmpA")
                ev_e2 = P.op("vector", lambda h, tt=tt, csl=csl: h.tensor_add(
                    att[:, tt * D + csl.start:tt * D + csl.stop], uh[:], tmpA[:, 0:512]),
                    waits=[ev_e1, ev_t1], inc="ve", chain="s2_tmpA")
                prev_e2[0] = ev_e2
                ev_bank_free[tt] = ev_e2
            prev_bo_done[bb] = ev_bank_free[NTT - 1]
        prev_qT_done[0] = ev_mm
        ev_red2, ev_sq2 = [], []
        for tt in range(NTT):
            ev_red2.append(P.op("vector", lambda h, tt=tt: h.reduce_sum(
                stats[:, SUM2 + tt:SUM2 + tt + 1], att[:, tt * D:(tt + 1) * D], axis=AX.X),
                waits=[ev_bank_free[tt]], inc="ve"))
            ev_sq2.append(P.op("scalar", lambda h, tt=tt: h.activation(
                sejunk, att[:, tt * D:(tt + 1) * D], AF.Square,
                accum_out=stats[:, SQS2 + tt:SQS2 + tt + 1]),
                waits=[ev_bank_free[tt]], inc="se", chain="s2_tmpB"))
        ev_ch2 = emit_ln_chain(P, stats, SUM2, SQS2, M2, T1_, RS2, waits=ev_red2 + ev_sq2)
        ev_x2 = []
        ev_sq3 = []
        for tt in range(NTT):
            tsl = slice(tt * D, (tt + 1) * D)
            ev_a = P.op("vector", lambda h, tsl=tsl, tt=tt: h.tensor_scalar(
                tmpA[:], att[:, tsl], stats[:, M2 + tt:M2 + tt + 1],
                stats[:, RS2 + tt:RS2 + tt + 1], ALU.subtract, ALU.mult),
                waits=[ev_ch2], inc="ve", chain="s2_tmpA")
            ev_b = P.op("vector", lambda h, tsl=tsl, tt=tt: h.scalar_tensor_tensor(
                att[:, tsl], tmpA[:], 1.0, xst[:, tsl], ALU.mult, ALU.add,
                accum_out=stats[:, SUM1 + tt:SUM1 + tt + 1]), waits=[ev_a], inc="ve",
                chain="s2_tmpA")
            ev_x2.append(ev_b)
            ev_sq3.append(P.op("scalar", lambda h, tsl=tsl, tt=tt: h.activation(
                sejunk, att[:, tsl], AF.Square,
                accum_out=stats[:, SQS1 + tt:SQS1 + tt + 1]), waits=[ev_b], inc="se",
                chain="s2_tmpB"))
        ev_ch3 = emit_ln_chain(P, stats, SUM1, SQS1, M1, T1_, RS1, waits=ev_x2 + ev_sq3)
        ev = P.op("vector", lambda h: h.tensor_mul(
            stats[:, T1_:T1_ + 64], stats[:, M1:M1 + 64], stats[:, M1:M1 + 64]),
            waits=[ev_ch3], inc="ve")
        ev = P.op("vector", lambda h: h.scalar_tensor_tensor(
            stats[:, T1_:T1_ + 64], stats[:, SQS1:SQS1 + 64], 1.0 / D, stats[:, T1_:T1_ + 64],
            ALU.mult, ALU.subtract), waits=[ev], inc="ve")
        ev = P.op("vector", lambda h: h.tensor_mul(
            stats[:, SUM2:SUM2 + 64], stats[:, RS1:RS1 + 64], stats[:, RS1:RS1 + 64]),
            waits=[ev], inc="ve")
        ev = P.op("vector", lambda h: h.tensor_mul(
            stats[:, T1_:T1_ + 64], stats[:, T1_:T1_ + 64], stats[:, SUM2:SUM2 + 64]),
            waits=[ev], inc="ve")
        ev = P.op("vector", lambda h: h.tensor_scalar(
            stats[:, SQS2:SQS2 + 64], stats[:, T1_:T1_ + 64], EPS, None, ALU.add),
            waits=[ev], inc="ve")
        ev = P.op("scalar", lambda h: h.sqrt(
            stats[:, T1_:T1_ + 64], stats[:, SQS2:SQS2 + 64]), waits=[ev], inc="se")
        ev = P.op("vector", lambda h: h.reciprocal(
            stats[:, SQS2:SQS2 + 64], stats[:, T1_:T1_ + 64]), waits=[ev], inc="ve")
        ev_c3 = P.op("vector", lambda h: h.tensor_mul(
            stats[:, RS2:RS2 + 64], stats[:, RS1:RS1 + 64], stats[:, SQS2:SQS2 + 64]),
            waits=[ev], inc="ve")
        ev_xqT = []
        ev_lad_prev = None
        for tt in range(NTT):
            tsl = slice(tt * D, (tt + 1) * D)
            ev_p1 = P.op("vector", lambda h, tsl=tsl, tt=tt: h.tensor_scalar(
                tmpA[:], att[:, tsl], stats[:, M1 + tt:M1 + tt + 1],
                stats[:, RS2 + tt:RS2 + tt + 1], ALU.subtract, ALU.mult),
                waits=[ev_c3, ev_lad_prev], inc="ve", chain="s2_tmpA")
            ev_x4 = P.op("vector", lambda h: h.tensor_mul(
                gx4, tmpA[:], nwb[:]), waits=[ev_p1, ev_nw], inc="ve", chain="s2_tmpA")
            ev_mx = P.op("vector", lambda h, tt=tt: h.reduce_max(
                stats[:, AMX + tt:AMX + tt + 1], gx4, axis=AX.X,
                apply_absolute_value=True), waits=[ev_x4], inc="ve")
            P.mset("vector", stats[:, T1_:T1_ + 64], 1.0)
            ev = P.op("vector", lambda h: h.tensor_single_scalar(
                stats[:, T1_:T1_ + 64], stats[:, AMX:AMX + 64], 1e-5, ALU.max),
                waits=[ev_mx], inc="ve")
            ev = P.op("vector", lambda h: h.reciprocal(
                stats[:, SUM2:SUM2 + 64], stats[:, T1_:T1_ + 64]), waits=[ev], inc="ve")
            ev_sx = P.op("vector", lambda h: h.tensor_scalar(
                stats[:, SX:SX + 64], stats[:, SUM2:SUM2 + 64], 127.0, None, ALU.mult),
                waits=[ev], inc="ve")
            ev_sc = P.op("vector", lambda h: h.tensor_scalar(
                stats[:, SCG:SCG + 64], stats[:, T1_:T1_ + 64], g["wconst"][:, 2:3],
                0.5 / 127.0, ALU.mult, ALU.mult), waits=[ev_sx], inc="ve")
            ev_lq = emit_quant_ladder(P, gx4, D, stats[:, SX + tt:SX + tt + 1],
                                      hq[:, 0:D], lad, waits=[ev_sx, ev_lad_prev],
                                      a_chain="s2_tmpB")
            ev_last = None
            for gi in range(4):
                bank = PTB[gno[0] % 2]

                def tr(h, gi=gi, bank=bank):
                    ins = None
                    for k in range(4):
                        dc = gi * 4 + k
                        ins = h.transpose(bank[:, k * 128:(k + 1) * 128],
                                          hq[:, dc * 128:(dc + 1) * 128], g["identb"][:])
                    return ins
                ev_t = P.op("tensor", tr, waits=[ev_lq, tg.get(gno[0] - 2)], inc="pe")
                dst3 = qT_hi[:].rearrange("p (dc t) -> p dc t", t=TS)[
                    :, gi * 4:(gi + 1) * 4, tt * 128:(tt + 1) * 128]
                bk3 = bank[:].rearrange("p (k c) -> p k c", c=128)[:, 0:4, :]
                ev_e = P.op("vector", lambda h, dst3=dst3, bk3=bk3: h.tensor_copy(dst3, bk3),
                            waits=[ev_t], inc="ve")
                tg[gno[0]] = ev_e
                gno[0] += 1
                ev_last = ev_e
            ev_lad_prev = ev_last
            ev_xqT.append(ev_last)
        ev_h_sp = {}
        ev_hb_guard = [None, None]
        ev_hmax = []
        for j in range(11):
            for phase, oc3 in (("g", j), ("u", 11 + j)):
                for dc in range(NDC):
                    wi1, r1_, ev_w1 = wload(scr["guT_q"], dc * 128, oc3 * 512)

                    def mm(h, dc=dc, r1_=r1_):
                        ins = None
                        for tt in range(NTT):
                            stsl = slice(dc * TS + tt * 128, dc * TS + (tt + 1) * 128)
                            ins = h.matmul(PMM[tt][:], qT_hi[:, stsl], wt[r1_][:],
                                           start=(dc == 0), stop=(dc == NDC - 1))
                        return ins
                    wts = [ev_w1]
                    if dc == 0:
                        wts += ev_xqT + ev_bank_free
                    ev_mm = P.op("tensor", mm, waits=wts, inc="pe")
                    ev_wu[wi1] = ev_mm
                for tt in range(NTT):
                    if phase == "g":
                        ev_e = P.op("vector", lambda h, tt=tt: h.tensor_scalar(
                            ghold[:, tt * 512:(tt + 1) * 512], PMM[tt][:],
                            stats[:, SCG + tt:SCG + tt + 1], 0.0, ALU.mult, ALU.max),
                            waits=[ev_mm], inc="ve", chain="s2_gx4")
                        ev_bank_free[tt] = ev_e
                    else:
                        ev_u = P.op("vector", lambda h, tt=tt: h.tensor_scalar(
                            uh[:], PMM[tt][:], stats[:, SCG + tt:SCG + tt + 1], None,
                            ALU.mult), waits=[ev_mm], inc="ve", chain="s2_uh")
                        ev_s = P.op("scalar", lambda h, tt=tt: h.square(
                            sqg[:], ghold[:, tt * 512:(tt + 1) * 512]),
                            waits=[ev_bank_free[tt], ev_h_sp.get(("hh_prev",))], inc="se",
                            chain="sqg")
                        hb = hbuf[tt % 2]
                        ev_hh = P.op("vector", lambda h, hb=hb: h.tensor_mul(
                            hb[:], sqg[:], uh[:]),
                            waits=[ev_u, ev_s, ev_hb_guard[tt % 2]], inc="ve",
                            chain="s2_uh")
                        ev_h_sp[("hh_prev",)] = ev_hh
                        ev_hm = P.op("vector", lambda h, tt=tt, j=j, hb=hb: h.reduce_max(
                            stats[:, HM + tt * 16 + j:HM + tt * 16 + j + 1], hb[:],
                            axis=AX.X, apply_absolute_value=True), waits=[ev_hh], inc="ve")
                        ev_hmax.append(ev_hm)
                        ev_d = P.op("gpsimd", lambda h, tt=tt, j=j, hb=hb: h.dma_start(
                            scr["h_sp"][tt * 128:(tt + 1) * 128, j * 512:(j + 1) * 512], hb[:]),
                            waits=[ev_hh], inc="d2", inc_k=16)
                        ev_h_sp[(j, tt)] = ev_d
                        ev_hb_guard[tt % 2] = ev_d
                        ev_bank_free[tt] = ev_hh
        ev_hqT = [None] * NTT
        ev_lad2 = None
        for tt in range(NTT):
            ev_am = P.op("vector", lambda h, tt=tt: h.reduce_max(
                stats[:, AMX + tt:AMX + tt + 1],
                stats[:, HM + tt * 16:HM + tt * 16 + 11], axis=AX.X), waits=ev_hmax, inc="ve")
            P.mset("vector", stats[:, T1_:T1_ + 64], 1.0)
            ev = P.op("vector", lambda h: h.tensor_single_scalar(
                stats[:, T1_:T1_ + 64], stats[:, AMX:AMX + 64], 1e-5, ALU.max),
                waits=[ev_am], inc="ve")
            ev = P.op("vector", lambda h: h.reciprocal(
                stats[:, SUM2:SUM2 + 64], stats[:, T1_:T1_ + 64]), waits=[ev], inc="ve")
            ev_sx2 = P.op("vector", lambda h: h.tensor_scalar(
                stats[:, SX:SX + 64], stats[:, SUM2:SUM2 + 64], 127.0, None, ALU.mult),
                waits=[ev], inc="ve")
            ev_sc2 = P.op("vector", lambda h: h.tensor_scalar(
                stats[:, SCG:SCG + 64], stats[:, T1_:T1_ + 64], g["wconst"][:, 5:6],
                0.5 / 127.0, ALU.mult, ALU.mult), waits=[ev_sx2], inc="ve")
            for ci, (c0, cw) in enumerate([(0, 2048), (2048, 2048), (4096, 1536)]):
                ev_ld = P.op("sync", lambda h, tt=tt, c0=c0, cw=cw: h.dma_start(
                    tmpA[:, 0:cw], scr["h_sp"][tt * 128:(tt + 1) * 128, c0:c0 + cw]),
                    waits=[ev_h_sp.get((10, tt)), ev_lad2], inc="d1", inc_k=16,
                    chain="s2_tmpA")
                ev_lad2 = emit_quant_ladder(
                    P, tmpA[:, 0:cw], cw, stats[:, SX + tt:SX + tt + 1],
                    hq[:, c0:c0 + cw], lad,
                    waits=[ev_ld, ev_sx2] + (ev_xqT if (tt == 0 and ci == 0) else []),
                    a_chain="s2_tmpB")
            ev_last = None
            for gi in range(6):
                n_in = min(8, NFC - gi * 8)
                bank = PTB[gno[0] % 2]

                def tr(h, gi=gi, n_in=n_in, bank=bank):
                    ins = None
                    for k in range(n_in):
                        fc = gi * 8 + k
                        ins = h.transpose(bank[:, k * 128:(k + 1) * 128],
                                          hq[:, fc * 128:(fc + 1) * 128], g["identb"][:])
                    return ins
                ev_t = P.op("tensor", tr, waits=[ev_lad2, tg.get(gno[0] - 2)], inc="pe")
                arena = hqA[:] if gi < 4 else qT_lo[:]
                fc0 = gi * 8 if gi < 4 else gi * 8 - 32
                dst3 = arena.rearrange("p (fc t) -> p fc t", t=TS)[
                    :, fc0:fc0 + n_in, tt * 128:(tt + 1) * 128]
                bk3 = bank[:].rearrange("p (k c) -> p k c", c=128)[:, 0:n_in, :]
                ev_e = P.op("vector", lambda h, dst3=dst3, bk3=bk3: h.tensor_copy(dst3, bk3),
                            waits=[ev_t], inc="ve")
                tg[gno[0]] = ev_e
                gno[0] += 1
                ev_last = ev_e
            ev_lad2 = ev_last
            ev_hqT[tt] = ev_last
        for oc4 in range(4):
            for fc in range(NFC):
                wi1, r1_, ev_w1 = wload(scr["dnT_q"], fc * 128, oc4 * 512)

                def mm(h, fc=fc, r1_=r1_):
                    ins = None
                    for tt in range(NTT):
                        ins = h.matmul(PMM[tt][:], hqT_ap(fc, tt * 128, (tt + 1) * 128),
                                       wt[r1_][:], start=(fc == 0), stop=(fc == NFC - 1))
                    return ins
                wts = [ev_w1]
                if fc == 0:
                    wts += ev_hqT + ev_bank_free
                ev_mm = P.op("tensor", mm, waits=wts, inc="pe")
                ev_wu[wi1] = ev_mm
            for tt in range(NTT):
                csl = slice(oc4 * 512, (oc4 + 1) * 512)
                ev_e = P.op("vector", lambda h, tt=tt, csl=csl: h.tensor_scalar(
                    ffn[:, tt * D + csl.start: tt * D + csl.stop], PMM[tt][:],
                    stats[:, SCG + tt:SCG + tt + 1], None, ALU.mult),
                    waits=[ev_mm], inc="ve")
                ev_bank_free[tt] = ev_e
        ev_red4, ev_sq4 = [], []
        for tt in range(NTT):
            ev_red4.append(P.op("vector", lambda h, tt=tt: h.reduce_sum(
                stats[:, SUM1 + tt:SUM1 + tt + 1], ffn[:, tt * D:(tt + 1) * D], axis=AX.X),
                waits=[ev_bank_free[tt]], inc="ve"))
            ev_sq4.append(P.op("scalar", lambda h, tt=tt: h.activation(
                sejunk, ffn[:, tt * D:(tt + 1) * D], AF.Square,
                accum_out=stats[:, SQS1 + tt:SQS1 + tt + 1]),
                waits=[ev_bank_free[tt]], inc="se", chain="s2_tmpB"))
        ev_ch4 = emit_ln_chain(P, stats, SUM1, SQS1, M1, T1_, RS1, waits=ev_red4 + ev_sq4)
        prev_st = None
        prev_o = None
        for tt in range(NTT):
            tsl = slice(tt * D, (tt + 1) * D)
            ev_a = P.op("vector", lambda h, tsl=tsl, tt=tt: h.tensor_scalar(
                tmpA[:], ffn[:, tsl], stats[:, M1 + tt:M1 + tt + 1],
                stats[:, RS1 + tt:RS1 + tt + 1], ALU.subtract, ALU.mult),
                waits=[ev_ch4, prev_o], inc="ve", chain="s2_tmpA")
            ev_o = P.op("vector", lambda h, tsl=tsl: h.tensor_add(
                tmpB[:], tmpA[:], att[:, tsl]), waits=[ev_a, prev_st], inc="ve",
                chain="s2_tmpB")
            prev_o = ev_o
            r0 = tok0 + tt * 128
            ev_st = P.op("gpsimd", lambda h, r0=r0: h.dma_start(
                out[r0:r0 + 128, :], tmpB[:]), waits=[ev_o], inc="d2", inc_k=16,
                chain="s2_tmpB")
            prev_st = ev_st
            prev_stage_done[tt] = ev_st
    P.wait("gpsimd", [prev_stage_done[tt] for tt in range(NTT)])


def kernel(L0_attn_X, L0_attn_W_in, L0_attn_b_in, L0_attn_W_out, L0_attn_b_out,
           L0_ffn_w_gate_up, L0_ffn_w_down, L0_ffn_norm_weight):
    X = np.ascontiguousarray(np.asarray(L0_attn_X, dtype=np.float32))
    W_in = np.ascontiguousarray(np.asarray(L0_attn_W_in, dtype=np.float32))
    b_in = np.asarray(L0_attn_b_in, dtype=np.float32)
    W_out = np.ascontiguousarray(np.asarray(L0_attn_W_out, dtype=np.float32))
    b_out = np.asarray(L0_attn_b_out, dtype=np.float32)
    w_gu = np.ascontiguousarray(np.asarray(L0_ffn_w_gate_up, dtype=np.float32))
    w_dn = np.ascontiguousarray(np.asarray(L0_ffn_w_down, dtype=np.float32))
    nw = np.asarray(L0_ffn_norm_weight, dtype=np.float32)
    return _run(X, W_in, b_in, W_out, b_out, w_gu, w_dn, nw, build_program())


def _run(X, W_in, b_in, W_out, b_out, w_gu, w_dn, nw, nc, extra_keys=()):
    b_in_bc = np.ascontiguousarray(np.broadcast_to(b_in[None, :], (128, E)))
    b_out_bc = np.ascontiguousarray(np.broadcast_to(b_out[None, :], (128, D)))
    nw_bc = np.ascontiguousarray(np.broadcast_to(nw[None, :], (128, D)))
    in_maps = []
    for c in range(8):
        b, half = c // 2, c % 2
        qc, qs = _rope_tables(np.arange(half * T, (half + 1) * T))
        oc_, os_ = _rope_tables(np.arange((1 - half) * T, (2 - half) * T))
        in_maps.append({
            "x": np.ascontiguousarray(X[b, half * T:(half + 1) * T, :]),
            "xo": np.ascontiguousarray(X[b, (1 - half) * T:(2 - half) * T, :]),
            "q_cos": qc, "q_sin": qs, "o_cos": oc_, "o_sin": os_,
            "w_in": W_in, "w_out": W_out, "w_gu": w_gu, "w_dn": w_dn,
            "b_in_bc": b_in_bc, "b_out_bc": b_out_bc, "nw_bc": nw_bc,
        })
    res = run_bass_kernel_spmd(nc, in_maps, core_ids=list(range(8)))
    full = np.empty((4, 4096, D), dtype=np.float32)
    for c in range(8):
        b, half = c // 2, c % 2
        full[b, half * T:(half + 1) * T, :] = res.results[c]["out"]
    if extra_keys:
        return full, res
    return full


# version 07 (final)
# speedup vs baseline: 1.2618x; 1.0631x over previous
import contextlib
import numpy as np

import concourse.bass as bass
import concourse.mybir as mybir
from concourse.bass_utils import run_bass_kernel_spmd

F32 = mybir.dt.float32
BF16 = mybir.dt.bfloat16
ALU = mybir.AluOpType
AF = mybir.ActivationFunctionType
AX = mybir.AxisListType

D = 2048
E = 6144
FF = 5632
GU = 11264
T = 2048
S1_PART = "full"
TS = 512
NTT = TS // 128
NST = T // TS
NDC = D // 128
NFC = FF // 128
ROPE_THETA = 10000.0
EPS = 1e-5


class Ev:
    __slots__ = ("sem", "val")

    def __init__(self, sem, val):
        self.sem = sem
        self.val = val


class SemCtr:
    def __init__(self, handle):
        self.h = handle
        self.n = 0

    def next(self, k):
        self.n += k
        return Ev(self, self.n)


class Prog:
    ENGS = ("sync", "gpsimd", "vector", "scalar", "tensor")

    def __init__(self, nc, st):
        self.nc = nc
        self.items = {e: [] for e in self.ENGS}
        self.sems = {}
        for name in ("pe", "ve", "se", "bar"):
            self.sems[name] = SemCtr(st.enter_context(nc.semaphore("s_" + name)))
        self.pools = {"d1": [], "d2": []}
        for name, n in (("d1", 6), ("d2", 4)):
            for i in range(n):
                s = SemCtr(st.enter_context(nc.semaphore(f"s_{name}_{i}")))
                self.sems[f"{name}_{i}"] = s
                self.pools[name].append(s)
        self.rr = {"d1": 0, "d2": 0}
        self.chains = {}

    def op(self, eng, fn, waits=(), inc=None, inc_k=1, chain=None):
        ws = [w for w in waits if w is not None]
        if chain is not None and chain in self.chains:
            ws.append(self.chains[chain])
        if inc in ("d1", "d2"):
            sem = self.pools[inc][self.rr[inc] % len(self.pools[inc])]
            self.rr[inc] += 1
            ws.append(Ev(sem, sem.n))
            ev = sem.next(inc_k)
        else:
            ev = self.sems[inc].next(inc_k) if inc else None

        def run(h, ws=ws, ev=ev, fn=fn, inc_k=inc_k):
            for w in ws:
                h.wait_ge(w.sem.h, w.val)
            ins = fn(h)
            if ev is not None:
                ins.then_inc(ev.sem.h, inc_k)
        self.items[eng].append(run)
        if chain is not None:
            assert ev is not None
            self.chains[chain] = ev
        return ev

    ESEM = {"vector": "ve", "scalar": "se", "tensor": "pe", "sync": "ve", "gpsimd": "ve"}

    def mset(self, eng, ap, val, waits=()):
        ev = self.op(eng, lambda h: h.memset(ap, val), waits=waits, inc=self.ESEM[eng])
        self.wait(eng, [ev])
        return ev

    def wait(self, eng, waits):
        ws = [w for w in waits if w is not None]

        def run(h, ws=ws):
            for w in ws:
                h.wait_ge(w.sem.h, w.val)
        self.items[eng].append(run)

    def barrier(self):
        bar = self.sems["bar"]
        pre = [Ev(s, s.n) for k, s in self.sems.items() if k != "bar"]
        for e in self.ENGS:
            self.wait(e, pre)
            self.op(e, lambda h: h.nop(), inc="bar")
        tgt = Ev(bar, bar.n)
        for e in self.ENGS:
            self.wait(e, [tgt])

    def install(self, block):
        for name in self.ENGS:
            prog = self.items[name]
            section = getattr(block, name)

            def body(h, prog=prog):
                for fn in prog:
                    fn(h)
            section(body)


_NAME_N = [0]


def _nm(base):
    _NAME_N[0] += 1
    return f"{base}_{_NAME_N[0]}"


def _rope_tables(pos):
    import jax
    import jax.numpy as jnp
    cpu = jax.devices("cpu")[0]
    with jax.default_device(cpu):
        inv = 1.0 / ROPE_THETA ** (jnp.arange(0, 1024, 2, dtype=jnp.float32) / 1024)
        fr = jnp.asarray(pos, dtype=jnp.float32)[:, None] * inv[None, :]
        c, s = np.asarray(jnp.cos(fr)), np.asarray(jnp.sin(fr))
    return np.ascontiguousarray(c), np.ascontiguousarray(s)


def build_program(debug=(), upto="all", s1_jobs=None, skip_p0=False):
    debug = set(debug)
    nc = bass.Bass("TRN2", target_bir_lowering=False)

    inp = {}
    for name, shape, dt in [
        ("x", [T, D], F32), ("xo", [T, D], F32),
        ("q_cos", [T, 512], F32), ("q_sin", [T, 512], F32),
        ("o_cos", [T, 512], F32), ("o_sin", [T, 512], F32),
        ("w_in", [E, D], F32), ("w_out", [D, D], F32),
        ("w_gu", [GU, D], F32), ("w_dn", [D, FF], F32),
        ("b_in_bc", [128, E], F32), ("b_out_bc", [128, D], F32),
        ("nw_bc", [128, D], F32),
    ]:
        inp[name] = nc.dram_tensor(name, shape, dt, kind="ExternalInput").ap()
    out = nc.dram_tensor("out", [T, D], F32, kind="ExternalOutput").ap()

    scr = {}
    for name, shape, dt in [
        ("winT_hi", [D, E], BF16), ("winT_lo", [D, E], BF16),
        ("woutT_hi", [D, D], BF16), ("woutT_lo", [D, D], BF16),
        ("guT_q", [D, GU], BF16), ("dnT_q", [FF, D], BF16),
        ("q_sp", [T, D], F32), ("h_sp", [TS, FF], F32),
    ]:
        scr[name] = nc.dram_tensor(name, shape, dt).ap()

    dbg_out = {}

    def dbg(name, shape, dt=F32):
        dbg_out[name] = nc.dram_tensor(name, shape, dt, kind="ExternalOutput").ap()
        return dbg_out[name]

    with contextlib.ExitStack() as st:
        P = Prog(nc, st)

        g = {}
        g["identf"] = st.enter_context(nc.sbuf_tensor("identf", [128, 128], F32))
        g["identb"] = st.enter_context(nc.sbuf_tensor("identb", [128, 128], BF16))
        g["iot"] = st.enter_context(nc.sbuf_tensor("iot", [128, 128], F32))
        g["ones_col"] = st.enter_context(nc.sbuf_tensor("ones_col", [128, 8], F32))
        g["ones_row"] = st.enter_context(nc.sbuf_tensor("ones_row", [1, 128], F32))
        g["wconst"] = st.enter_context(nc.sbuf_tensor("wconst", [128, 64], F32))
        g["A_acc"] = [st.enter_context(nc.sbuf_tensor(f"A_acc{i}", [128, 64], F32))
                      for i in range(2)]

        ev_iota = P.op("gpsimd", lambda h: h.iota(
            g["iot"][:], [[1, 128]], channel_multiplier=-1,
            allow_small_or_imprecise_dtypes=True), inc="ve")
        P.op("vector", lambda h: h.tensor_single_scalar(
            g["identf"][:], g["iot"][:], 0.0, ALU.is_equal), waits=[ev_iota], inc="ve")
        P.op("vector", lambda h: h.tensor_single_scalar(
            g["identb"][:], g["iot"][:], 0.0, ALU.is_equal), inc="ve")
        P.mset("vector", g["ones_col"][:], 1.0)
        P.mset("vector", g["ones_row"][:], 1.0)
        P.mset("vector", g["wconst"][:], 1.0)
        P.mset("vector", g["A_acc"][0][:], 0.0)
        P.mset("vector", g["A_acc"][1][:], 0.0)
        P.barrier()

        if not skip_p0:
            with contextlib.ExitStack() as ps:
                emit_wprep_hilo(nc, P, ps, inp["w_in"], scr["winT_hi"], scr["winT_lo"],
                                rows=E, g=g, scale_col=None)
            P.barrier()
        if "winT" in debug:
            emit_scratch_dump(nc, P, dbg("dbg_winT_hi", [D, E], BF16), scr["winT_hi"], D, E, BF16)
            emit_scratch_dump(nc, P, dbg("dbg_winT_lo", [D, E], BF16), scr["winT_lo"], D, E, BF16)
            P.barrier()
        if upto == "p0a":
            emit_dummy_out(nc, P, out)
            with nc.Block() as block:
                P.install(block)
            return nc

        if not skip_p0:
            with contextlib.ExitStack() as ps:
                emit_ffn_means(nc, P, ps, inp["w_gu"], inp["w_dn"], g)
            P.barrier()

        if not skip_p0:
            with contextlib.ExitStack() as ps:
                emit_wprep_quant(nc, P, ps, inp["w_gu"], scr["guT_q"], rows=GU,
                                 thr_pos=g["wconst"][:, 0:1], thr_neg=g["wconst"][:, 1:2],
                                 g=g, cols=D)
            P.barrier()
            with contextlib.ExitStack() as ps:
                emit_wprep_quant(nc, P, ps, inp["w_dn"], scr["dnT_q"], rows=D,
                                 thr_pos=g["wconst"][:, 3:4], thr_neg=g["wconst"][:, 4:5],
                                 g=g, cols=FF)
            P.barrier()
        if "wq" in debug:
            emit_scratch_dump(nc, P, dbg("dbg_guT", [D, GU], BF16), scr["guT_q"], D, GU, BF16)
            emit_scratch_dump(nc, P, dbg("dbg_dnT", [FF, D], BF16), scr["dnT_q"], FF, D, BF16)
            o5 = dbg("dbg_wconst", [128, 64], F32)
            evd = P.op("gpsimd", lambda h: h.dma_start(o5, g["wconst"][:]), inc="d2", inc_k=16)
            P.wait("gpsimd", [evd])
            P.barrier()
        if upto == "p0":
            emit_dummy_out(nc, P, out)
            with nc.Block() as block:
                P.install(block)
            return nc

        with contextlib.ExitStack() as ps:
            emit_sweep1(nc, P, ps, inp, scr, g, s1_jobs=s1_jobs)
        P.barrier()

        P.barrier()
        if "s1" in debug:
            emit_scratch_dump(nc, P, dbg("dbg_qsp", [T, D], F32), scr["q_sp"], T, D, F32)
            o7 = dbg("dbg_A", [128, 64], F32)
            n_t = (s1_jobs if s1_jobs is not None else 2 * NST) * NTT
            evd = P.op("gpsimd", lambda h: h.dma_start(o7, g["A_acc"][n_t % 2][:]),
                       inc="d2", inc_k=16)
            P.wait("gpsimd", [evd])
            P.barrier()
        if upto == "p1":
            emit_dummy_out(nc, P, out)
            with nc.Block() as block:
                P.install(block)
            return nc

        n_s1_tiles = (s1_jobs if s1_jobs is not None else 2 * NST) * NTT
        with contextlib.ExitStack() as ps:
            emit_wprep_hilo(nc, P, ps, inp["w_out"], scr["woutT_hi"], scr["woutT_lo"],
                            rows=D, g=g, scale_col=g["A_acc"][n_s1_tiles % 2])
        P.barrier()

        with contextlib.ExitStack() as ps:
            emit_sweep2(nc, P, ps, inp, scr, out, g)
        P.barrier()

        with nc.Block() as block:
            P.install(block)
    return nc


def emit_dummy_out(nc, P, out):
    with contextlib.ExitStack() as ps:
        z = ps.enter_context(nc.sbuf_tensor(_nm("zz"), [128, D], F32))
        ev = P.mset("vector", z[:], 0.0)
        evs = []
        for r in range(T // 128):
            evs.append(P.op("gpsimd", lambda h, r=r: h.dma_start(
                out[r * 128:(r + 1) * 128, :], z[:]), waits=[ev], inc="d2", inc_k=16))
        P.wait("gpsimd", [evs[-1]])
        P.barrier()


def emit_scratch_dump(nc, P, dst, src, rows, cols, dt):
    with contextlib.ExitStack() as ps:
        buf = ps.enter_context(nc.sbuf_tensor(_nm("dump_buf"), [128, cols], dt))
        ev_st = None
        for r in range(rows // 128):
            ev_ld = P.op("sync", lambda h, r=r: h.dma_start(
                buf[:], src[r * 128:(r + 1) * 128, :]), waits=[ev_st], inc="d1", inc_k=16)
            ev_st = P.op("sync", lambda h, r=r: h.dma_start(
                dst[r * 128:(r + 1) * 128, :], buf[:]), waits=[ev_ld], inc="d1", inc_k=16)
        P.wait("sync", [ev_st])
    P.barrier()


def emit_wprep_hilo(nc, P, ps, w_ap, out_hi, out_lo, rows, g, scale_col):
    PTR = [ps.enter_context(nc.psum_tensor(_nm("ptrf"), [128, 512], F32)) for i in range(2)]
    """Stream w [rows, D] fp32 column-slab-wise; PE-transpose; optionally scale
    per-partition by scale_col[:, dc]; split to bf16 hi/lo; store [D, rows].
    raw is ping-ponged and each slab load is split over parallel DMA queues so
    the next load overlaps this slab's transposes/stores."""
    nec = rows // 128
    raw = [ps.enter_context(nc.sbuf_tensor(_nm("wp_raw"), [128, rows], F32))
           for _ in range(2)]
    hi = ps.enter_context(nc.sbuf_tensor(_nm("wp_hi"), [128, rows], BF16))
    lo = ps.enter_context(nc.sbuf_tensor(_nm("wp_lo"), [128, rows], BF16))
    wsc = ps.enter_context(nc.sbuf_tensor(_nm("wp_sc"), [128, 512], F32))
    ev_out = []
    tg = {}
    gno = [0]
    tr_last = {}
    nsp = 4 if nec >= 8 else 2
    for dc in range(NDC):
        rb = raw[dc % 2]
        st_w = list(ev_out)
        ev_lds = []
        for k in range(nsp):
            a0, a1 = k * nec // nsp, (k + 1) * nec // nsp
            src = w_ap[a0 * 128:a1 * 128, dc * 128:(dc + 1) * 128].rearrange(
                "(a p) d -> p a d", p=128)
            dst = rb[:, a0 * 128:a1 * 128].rearrange("p (a d) -> p a d", d=128)
            ev_lds.append(P.op("sync", lambda h, src=src, dst=dst: h.dma_start(dst, src),
                               waits=[tr_last.get(dc - 2)], inc="d1", inc_k=16))
        ngr = (nec + 3) // 4
        ev_sp = None
        for gi in range(ngr):
            n_in = min(4, nec - gi * 4)
            bank = PTR[gno[0] % 2]

            def tr(h, gi=gi, n_in=n_in, bank=bank, rb=rb):
                ins = None
                for k in range(n_in):
                    ec = gi * 4 + k
                    ins = h.transpose(bank[:, k * 128:(k + 1) * 128],
                                      rb[:, ec * 128:(ec + 1) * 128], g["identf"][:])
                return ins
            ev_t = P.op("tensor", tr, waits=ev_lds + [tg.get(gno[0] - 2)], inc="pe")
            tr_last[dc] = ev_t
            w = n_in * 128
            colsl = slice(gi * 512, gi * 512 + w)
            if scale_col is not None:
                ev_s = P.op("vector", lambda h, bank=bank, w=w, dc=dc: h.tensor_scalar(
                    wsc[:, 0:w], bank[:, 0:w], scale_col[:, dc:dc + 1], None, ALU.mult),
                    waits=[ev_t], inc="ve", chain="wprep_sc")
                ev_hi = P.op("scalar", lambda h, w=w, colsl=colsl: h.copy(
                    hi[:, colsl], wsc[:, 0:w]), waits=[ev_s] + st_w, inc="se")
                ev_lo = P.op("vector", lambda h, w=w, colsl=colsl: h.tensor_sub(
                    lo[:, colsl], wsc[:, 0:w], hi[:, colsl]), waits=[ev_hi], inc="ve",
                    chain="wprep_sc")
            else:
                ev_hi = P.op("scalar", lambda h, bank=bank, w=w, colsl=colsl: h.copy(
                    hi[:, colsl], bank[:, 0:w]), waits=[ev_t] + st_w, inc="se")
                ev_lo = P.op("vector", lambda h, bank=bank, w=w, colsl=colsl: h.tensor_sub(
                    lo[:, colsl], bank[:, 0:w], hi[:, colsl]), waits=[ev_hi], inc="ve")
            tg[gno[0]] = ev_lo
            gno[0] += 1
            ev_sp = ev_lo
        ev_o1 = P.op("gpsimd", lambda h, dc=dc: h.dma_start(
            out_hi[dc * 128:(dc + 1) * 128, :], hi[:, 0:rows]), waits=[ev_sp], inc="d2", inc_k=16)
        ev_o2 = P.op("gpsimd", lambda h, dc=dc: h.dma_start(
            out_lo[dc * 128:(dc + 1) * 128, :], lo[:, 0:rows]), waits=[ev_sp], inc="d2", inc_k=16)
        ev_out = [ev_o1, ev_o2]
    P.wait("gpsimd", ev_out)


def emit_ffn_means(nc, P, ps, w_gu, w_dn, g):
    PMISC = ps.enter_context(nc.psum_tensor(_nm("pmisc"), [128, 512], F32))
    """mean|w| for gu and dn -> wconst cols (thr, -thr, m)."""
    slabs = [ps.enter_context(nc.sbuf_tensor(f"mn_slab{j}", [128, 2048], F32))
             for j in range(4)]
    parts = ps.enter_context(nc.sbuf_tensor("mn_parts", [128, 192], F32))
    rowb = ps.enter_context(nc.sbuf_tensor("mn_rows", [1, 256], F32))
    acc = ps.enter_context(nc.sbuf_tensor("mn_acc", [1, 512], F32))
    ch1 = ps.enter_context(nc.sbuf_tensor("mn_c1", [1, 512], F32))
    ch2 = ps.enter_context(nc.sbuf_tensor("mn_c2", [1, 512], F32))
    P.mset("vector", parts[:], 0.0)
    P.mset("vector", acc[:], 1.0)
    P.mset("vector", rowb[:], 0.0)

    specs = [(w_gu, GU * D, 0), (w_dn, D * FF, 1)]
    ev_reds = {}
    col = 0
    idx = 0
    cols_used = [[], []]
    for w_ap, total, mi in specs:
        flat = w_ap.rearrange("a b -> (a b)")
        n_sl = total // (128 * 2048)
        for i in range(n_sl):
            src = flat[i * 128 * 2048:(i + 1) * 128 * 2048].rearrange("(p m) -> p m", p=128)
            j = idx % 4
            ev_ld = P.op("sync", lambda h, src=src, j=j: h.dma_start(slabs[j][:], src),
                         waits=[ev_reds.get(idx - 4)], inc="d1", inc_k=16)
            ev_reds[idx] = P.op("vector", lambda h, col=col, j=j: h.reduce_sum(
                parts[:, col:col + 1], slabs[j][:], axis=AX.X, apply_absolute_value=True),
                waits=[ev_ld], inc="ve")
            cols_used[mi].append(col)
            col += 1
            idx += 1
    ev_red = ev_reds[idx - 1]
    ev_p = []
    for mi in range(2):
        c0, c1 = cols_used[mi][0], cols_used[mi][-1] + 1
        ev_p.append(P.op("vector", lambda h, mi=mi, c0=c0, c1=c1: h.reduce_sum(
            parts[:, 184 + mi:185 + mi], parts[:, c0:c1], axis=AX.X), waits=[ev_red], inc="ve"))
    ev_t, ev_r = [], []
    for mi in range(2):
        ev_t.append(P.op("tensor", lambda h, mi=mi: h.transpose(
            PMISC[0:1, mi * 128:(mi + 1) * 128],
            parts[:, 184 + mi:185 + mi], g["identf"][:]),
            waits=[ev_p[mi]] + (ev_r[-1:] if ev_r else []), inc="pe"))
        ev_r.append(P.op("vector", lambda h, mi=mi: h.tensor_copy(
            rowb[0:1, mi * 128:(mi + 1) * 128], PMISC[0:1, mi * 128:(mi + 1) * 128]),
            waits=[ev_t[mi]], inc="ve"))
    ev_s0 = P.op("vector", lambda h: h.reduce_sum(
        acc[0:1, 0:1], rowb[0:1, 0:128], axis=AX.X), waits=ev_r, inc="ve")
    ev_s1 = P.op("vector", lambda h: h.reduce_sum(
        acc[0:1, 256:257], rowb[0:1, 128:256], axis=AX.X), inc="ve")
    P.mset("vector", ch1[:], 1.0)
    P.mset("vector", ch2[:], 1.0)
    Ninv = [1.0 / (GU * D), 1.0 / (D * FF)]
    ev = P.op("vector", lambda h: h.tensor_scalar(
        ch1[0:1, 0:256], acc[0:1, 0:256], Ninv[0], None, ALU.mult),
        waits=[ev_s0, ev_s1], inc="ve")
    ev = P.op("vector", lambda h: h.tensor_scalar(
        ch1[0:1, 256:512], acc[0:1, 256:512], Ninv[1], None, ALU.mult), waits=[ev], inc="ve")
    ev = P.op("vector", lambda h: h.tensor_single_scalar(
        ch2[:], ch1[:], 1e-5, ALU.max), waits=[ev], inc="ve")
    ev = P.op("vector", lambda h: h.tensor_scalar(
        ch1[:], ch2[:], 0.5, None, ALU.mult), waits=[ev], inc="ve")
    items = [(0, ch1, 0), (2, ch2, 0), (3, ch1, 256), (5, ch2, 256)]
    ev_b = []
    for k, (colw, srcbuf, sc) in enumerate(items):
        evm = P.op("tensor", lambda h, k=k, srcbuf=srcbuf, sc=sc: h.matmul(
            PMISC[:, 256 + k:257 + k], g["ones_row"][:], srcbuf[0:1, sc:sc + 1],
            start=True, stop=True),
            waits=[ev] + (ev_b[-1:] if ev_b else []), inc="pe")
        ev_b.append(P.op("vector", lambda h, k=k, colw=colw: h.tensor_copy(
            g["wconst"][:, colw:colw + 1], PMISC[:, 256 + k:257 + k]), waits=[evm], inc="ve"))
    ev = P.op("vector", lambda h: h.tensor_scalar(
        g["wconst"][:, 1:2], g["wconst"][:, 0:1], -1.0, None, ALU.mult), waits=ev_b, inc="ve")
    P.op("vector", lambda h: h.tensor_scalar(
        g["wconst"][:, 4:5], g["wconst"][:, 3:4], -1.0, None, ALU.mult), waits=[ev], inc="ve")


def emit_wprep_quant(nc, P, ps, w_ap, out_q, rows, thr_pos, thr_neg, g, cols):
    PTR = [ps.enter_context(nc.psum_tensor(_nm("ptrb"), [128, 1024], BF16)) for i in range(2)]
    """wq' = sign(w-thr)+sign(w+thr) (natural layout) then bf16 transpose to
    out_q [cols, rows]."""
    nec = rows // 128
    ncc = cols // 128
    raw = [ps.enter_context(nc.sbuf_tensor(_nm("wq_raw"), [128, rows], F32))
           for _ in range(2)]
    sa = ps.enter_context(nc.sbuf_tensor(_nm("wq_sa"), [128, rows], BF16))
    sb2 = ps.enter_context(nc.sbuf_tensor(_nm("wq_sb"), [128, rows], BF16))
    wq = ps.enter_context(nc.sbuf_tensor(_nm("wq_wq"), [128, rows], BF16))
    wqT = ps.enter_context(nc.sbuf_tensor(_nm("wq_wqT"), [128, rows], BF16))
    ev_out = None
    sign_done = {}
    tr_done = {}
    qtg = {}
    qgno = [0]
    nsp = 4 if nec >= 8 else 2
    for dc in range(ncc):
        rb = raw[dc % 2]
        st_prev = ev_out
        ev_lds = []
        for k in range(nsp):
            a0, a1 = k * nec // nsp, (k + 1) * nec // nsp
            src = w_ap[a0 * 128:a1 * 128, dc * 128:(dc + 1) * 128].rearrange(
                "(a p) d -> p a d", p=128)
            dst = rb[:, a0 * 128:a1 * 128].rearrange("p (a d) -> p a d", d=128)
            ev_lds.append(P.op("sync", lambda h, src=src, dst=dst: h.dma_start(dst, src),
                               waits=[sign_done.get(dc - 2)], inc="d1", inc_k=16))
        ev_a = P.op("scalar", lambda h, rb=rb: h.activation(
            sa[:, 0:rows], rb[:, 0:rows], AF.Sign, bias=thr_neg),
            waits=ev_lds + [sign_done.get(dc - 1)], inc="se")
        ev_b = P.op("scalar", lambda h, rb=rb: h.activation(
            sb2[:, 0:rows], rb[:, 0:rows], AF.Sign, bias=thr_pos),
            waits=[ev_a], inc="se")
        ev_w = P.op("vector", lambda h: h.tensor_add(
            wq[:, 0:rows], sa[:, 0:rows], sb2[:, 0:rows]),
            waits=[ev_a, ev_b, tr_done.get(dc - 1)], inc="ve")
        sign_done[dc] = ev_w
        ngr = (nec + 7) // 8
        ev_last = None
        for gi in range(ngr):
            n_in = min(8, nec - gi * 8)
            bank = PTR[(qgno[0] + gi) % 2][:]

            def tr(h, gi=gi, n_in=n_in, bank=bank):
                ins = None
                for k in range(n_in):
                    ec = gi * 8 + k
                    ins = h.transpose(bank[:, k * 128:(k + 1) * 128],
                                      wq[:, ec * 128:(ec + 1) * 128], g["identb"][:])
                return ins
            ev_t = P.op("tensor", tr, waits=[ev_w, qtg.get(qgno[0] + gi - 2)], inc="pe")
            tr_done[dc] = ev_t
            w = n_in * 128
            ev_e = P.op("vector", lambda h, gi=gi, w=w, bank=bank: h.tensor_copy(
                wqT[:, gi * 1024:gi * 1024 + w], bank[:, 0:w]),
                waits=[ev_t, st_prev], inc="ve")
            qtg[qgno[0] + gi] = ev_e
            ev_last = ev_e
        qgno[0] += ngr
        ev_out = P.op("gpsimd", lambda h, dc=dc: h.dma_start(
            out_q[dc * 128:(dc + 1) * 128, :], wqT[:, 0:rows]), waits=[ev_last], inc="d2", inc_k=16)
    P.wait("gpsimd", [ev_out])


def emit_ln_chain(P, stats, C_SUM, C_SQS, C_M, C_T, C_RS, waits):
    P.mset("vector", stats[:, C_T:C_T + 64], 1.0)
    P.mset("vector", stats[:, C_T:C_T + 64], 1.0)
    ev = P.op("vector", lambda h: h.tensor_scalar(
        stats[:, C_M:C_M + 64], stats[:, C_SUM:C_SUM + 64], 1.0 / D, None, ALU.mult),
        waits=waits, inc="ve")
    ev = P.op("vector", lambda h: h.tensor_mul(
        stats[:, C_T:C_T + 64], stats[:, C_M:C_M + 64], stats[:, C_M:C_M + 64]),
        waits=[ev], inc="ve")
    ev = P.op("vector", lambda h: h.scalar_tensor_tensor(
        stats[:, C_T:C_T + 64], stats[:, C_SQS:C_SQS + 64], 1.0 / D, stats[:, C_T:C_T + 64],
        ALU.mult, ALU.subtract), waits=[ev], inc="ve")
    ev = P.op("vector", lambda h: h.tensor_scalar(
        stats[:, C_RS:C_RS + 64], stats[:, C_T:C_T + 64], EPS, None, ALU.add),
        waits=[ev], inc="ve")
    ev = P.op("scalar", lambda h: h.sqrt(stats[:, C_T:C_T + 64], stats[:, C_RS:C_RS + 64]),
              waits=[ev], inc="se")
    ev = P.op("vector", lambda h: h.reciprocal(stats[:, C_RS:C_RS + 64], stats[:, C_T:C_T + 64]),
              waits=[ev], inc="ve")
    return ev


def emit_rope(P, buf, base, cs_ap, sn_ap, rtmp, waits, tag="rope"):
    EVN = buf[:, base:base + 1024:2]
    ODD = buf[:, base + 1:base + 1024:2]
    P.op("vector", lambda h: h.tensor_mul(rtmp[:, 0:512], EVN, cs_ap), waits=waits,
         inc="ve", chain=tag)
    P.op("vector", lambda h: h.tensor_mul(rtmp[:, 512:1024], ODD, cs_ap), inc="ve", chain=tag)
    P.op("vector", lambda h: h.scalar_tensor_tensor(
        rtmp[:, 1024:1536], ODD, -1.0, sn_ap, ALU.mult, ALU.mult), inc="ve", chain=tag)
    P.op("vector", lambda h: h.tensor_mul(rtmp[:, 1536:2048], EVN, sn_ap), inc="ve", chain=tag)
    P.op("vector", lambda h: h.tensor_add(EVN, rtmp[:, 0:512], rtmp[:, 1024:1536]),
         inc="ve", chain=tag)
    return P.op("vector", lambda h: h.tensor_add(ODD, rtmp[:, 512:1024], rtmp[:, 1536:2048]),
                inc="ve", chain=tag)


def emit_quant_ladder(P, x_ap, w, sx_col, xq_ap, lad, waits, a_chain=None):
    a, vv, r0, r1, sgn = (lad[k] for k in ("a", "vv", "r0", "r1", "sgn"))
    ev_g = P.op("scalar", lambda h: h.activation(sgn[:, 0:w], x_ap, AF.Sign, scale=sx_col),
                waits=waits, inc="se")
    ev_a = P.op("scalar", lambda h: h.activation(a[:, 0:w], x_ap, AF.Abs, scale=sx_col),
                waits=waits, inc="se", chain=a_chain)
    ev = P.op("vector", lambda h: h.tensor_scalar(
        vv[:, 0:w], a[:, 0:w], 128.0, None, ALU.add), waits=[ev_a], inc="ve")
    ev = P.op("vector", lambda h: h.tensor_scalar(
        xq_ap, vv[:, 0:w], -128.0, None, ALU.add), waits=[ev], inc="ve")
    r = None
    rbuf = [r0, r1]
    for i, k in enumerate([64, 32, 16, 8, 4, 2, 1]):
        rap = xq_ap if r is None else r[:, 0:w]
        nxt = rbuf[i % 2]
        ev = P.op("vector", lambda h, rap=rap, k=k, w=w: h.tensor_scalar(
            vv[:, 0:w], rap, float(k) - 0.5, float(-k), ALU.is_ge, ALU.mult),
            waits=[ev], inc="ve")
        ev = P.op("vector", lambda h, rap=rap, nxt=nxt, w=w: h.tensor_add(
            nxt[:, 0:w], rap, vv[:, 0:w]), waits=[ev], inc="ve")
        r = nxt
    ev = P.op("vector", lambda h, r=r, w=w: h.tensor_sub(
        vv[:, 0:w], xq_ap, r[:, 0:w]), waits=[ev], inc="ve")
    return P.op("vector", lambda h, w=w: h.tensor_mul(
        xq_ap, vv[:, 0:w], sgn[:, 0:w]), waits=[ev, ev_g], inc="ve")


def emit_sweep1(nc, P, ps, inp, scr, g, s1_jobs=None):
    PMM = [ps.enter_context(nc.psum_tensor(_nm("pmm"), [128, 512], F32)) for i in range(4)]
    PTR = [ps.enter_context(nc.psum_tensor(_nm("ptrf"), [128, 512], F32)) for i in range(2)]
    PA = ps.enter_context(nc.psum_tensor(_nm("papart"), [128, 512], F32))
    xT_hi = ps.enter_context(nc.sbuf_tensor("s1_xT_hi", [128, NDC * TS], BF16))
    xT_lo = ps.enter_context(nc.sbuf_tensor("s1_xT_lo", [128, NDC * TS], BF16))
    kst = ps.enter_context(nc.sbuf_tensor("s1_k", [128, NTT * D], F32))
    vst = ps.enter_context(nc.sbuf_tensor("s1_v", [128, NTT * D], F32))
    xln = ps.enter_context(nc.sbuf_tensor("s1_xln", [128, D], F32))
    qev = [ps.enter_context(nc.sbuf_tensor(f"s1_qe{i}", [128, 512], F32)) for i in range(2)]
    sejunk = ps.enter_context(nc.sbuf_tensor("s1_sejunk", [128, D], F32))
    fmkv = ps.enter_context(nc.sbuf_tensor("s1_fmkv", [128, D], F32))
    cs = ps.enter_context(nc.sbuf_tensor("s1_cs", [128, NTT * 512], F32))
    sn = ps.enter_context(nc.sbuf_tensor("s1_sn", [128, NTT * 512], F32))
    rtmp = ps.enter_context(nc.sbuf_tensor("s1_rt", [128, D], F32))
    binb = [ps.enter_context(nc.sbuf_tensor(f"s1_bin{i}", [128, 512], F32)) for i in range(2)]
    wt = [ps.enter_context(nc.sbuf_tensor(f"s1_wt{i}", [128, 512], BF16)) for i in range(4)]
    stats = ps.enter_context(nc.sbuf_tensor("s1_st", [128, 512], F32))
    SUM, SQS, M, TT_, RS, KS, T2, RN = 0, 64, 128, 192, 256, 320, 384, 448
    _ev = P.mset("vector", stats[:], 1.0)
    P.wait("scalar", [_ev])

    n_tiles_total = (NST + NST) * NTT
    tile_no = [0]
    a_rd_prev = [None]
    tg = {}
    gno = [0]
    prev_xln = [None]
    prev_kst_done = [None] * NTT
    prev_cs_done = [None]
    prev_xT_done = [None]
    prev_bias_done = [None, None]
    prev_qe_done = [None, None]
    ev_bank_free = [None] * NTT

    jobs = [("own", s) for s in range(NST)] + [("dup", s) for s in range(NST)]
    if s1_jobs is not None:
        jobs = jobs[:s1_jobs]
    n_tiles_total = len(jobs) * NTT
    for kind, s in jobs:
        x_ap = inp["x"] if kind == "own" else inp["xo"]
        cos_ap = inp["q_cos"] if kind == "own" else inp["o_cos"]
        sin_ap = inp["q_sin"] if kind == "own" else inp["o_sin"]
        oc_list = list(range(12)) if kind == "own" else list(range(4, 12))
        tok0 = s * TS

        ev_x, ev_cs = [], []
        for tt in range(NTT):
            r0 = tok0 + tt * 128
            ev_x.append(P.op("sync", lambda h, tt=tt, r0=r0, x_ap=x_ap: h.dma_start(
                kst[:, tt * D:(tt + 1) * D], x_ap[r0:r0 + 128, :]),
                waits=[prev_kst_done[tt]], inc="d1", inc_k=16))
            ev_cs.append(P.op("sync", lambda h, tt=tt, r0=r0, cos_ap=cos_ap: h.dma_start(
                cs[:, tt * 512:(tt + 1) * 512], cos_ap[r0:r0 + 128, :]),
                waits=[prev_cs_done[0]], inc="d1", inc_k=16))
            ev_cs.append(P.op("sync", lambda h, tt=tt, r0=r0, sin_ap=sin_ap: h.dma_start(
                sn[:, tt * 512:(tt + 1) * 512], sin_ap[r0:r0 + 128, :]), inc="d1", inc_k=16))
        if S1_PART == "dma":
            for tt in range(NTT):
                prev_kst_done[tt] = ev_x[tt]
            continue
        ev_red, ev_sq = [], []
        for tt in range(NTT):
            ev_red.append(P.op("vector", lambda h, tt=tt: h.reduce_sum(
                stats[:, SUM + tt:SUM + tt + 1], kst[:, tt * D:(tt + 1) * D], axis=AX.X),
                waits=[ev_x[tt]], inc="ve"))
            if S1_PART != "nosq":
                ev_sq.append(P.op("scalar", lambda h, tt=tt: h.activation(
                    sejunk[:], kst[:, tt * D:(tt + 1) * D], AF.Square,
                    accum_out=stats[:, SQS + tt:SQS + tt + 1]), waits=[ev_x[tt]], inc="se",
                    chain="s1jnk"))
        if S1_PART == "stats" or S1_PART == "nosq":
            for tt in range(NTT):
                prev_kst_done[tt] = ev_red[tt]
            continue
        ev_ch = emit_ln_chain(P, stats, SUM, SQS, M, TT_, RS, waits=ev_red + ev_sq)

        if S1_PART == "chain":
            for tt in range(NTT):
                prev_kst_done[tt] = ev_red[tt]
            continue
        ev_xT = []
        for tt in range(NTT):
            ev_ap = P.op("vector", lambda h, tt=tt: h.tensor_scalar(
                xln[:], kst[:, tt * D:(tt + 1) * D],
                stats[:, M + tt:M + tt + 1], stats[:, RS + tt:RS + tt + 1],
                ALU.subtract, ALU.mult), waits=[ev_ch, prev_xln[0]], inc="ve")
            if S1_PART == "apply":
                prev_xln[0] = ev_ap
                continue
            ev_last = None
            for gi in range(4):
                bank = PTR[gno[0] % 2]

                def tr(h, gi=gi, bank=bank):
                    ins = None
                    for k in range(4):
                        dc = gi * 4 + k
                        ins = h.transpose(bank[:, k * 128:(k + 1) * 128],
                                          xln[:, dc * 128:(dc + 1) * 128], g["identf"][:])
                    return ins
                wts = [ev_ap, tg.get(gno[0] - 2)]
                if tt == 0 and gi == 0:
                    wts.append(prev_xT_done[0])
                ev_t = P.op("tensor", tr, waits=wts, inc="pe")
                if S1_PART == "trnoev":
                    tg[gno[0]] = ev_t
                    gno[0] += 1
                    prev_xln[0] = ev_t
                    ev_last = ev_t
                    continue
                hi3 = xT_hi[:].rearrange("p (dc t) -> p dc t", t=TS)[
                    :, gi * 4:(gi + 1) * 4, tt * 128:(tt + 1) * 128]
                lo3 = xT_lo[:].rearrange("p (dc t) -> p dc t", t=TS)[
                    :, gi * 4:(gi + 1) * 4, tt * 128:(tt + 1) * 128]
                bk3 = bank[:].rearrange("p (k c) -> p k c", c=128)
                ev_hi = P.op("scalar", lambda h, hi3=hi3, bk3=bk3: h.copy(hi3, bk3),
                             waits=[ev_t], inc="se")
                ev_lo = P.op("vector", lambda h, lo3=lo3, bk3=bk3, hi3=hi3: h.tensor_sub(
                    lo3, bk3, hi3), waits=[ev_hi], inc="ve")
                tg[gno[0]] = ev_lo
                gno[0] += 1
                ev_last = ev_lo
            prev_xln[0] = ev_last
            ev_xT.append(ev_last)
        if S1_PART in ("ln", "apply", "trnoev"):
            for tt in range(NTT):
                prev_kst_done[tt] = ev_red[tt]
            continue

        ev_wu = {}
        wno = [0]
        ev_mm = None
        for oi, oc in enumerate(oc_list):
            bb = oi % 2
            ev_bias = P.op("sync", lambda h, oc=oc, bb=bb: h.dma_start(
                binb[bb][:], inp["b_in_bc"][:, oc * 512:(oc + 1) * 512]),
                waits=[prev_bias_done[bb]], inc="d1", inc_k=16)
            for dc in range(NDC):
                wi = wno[0]
                ridx = (wi % 2) * 2
                ev_w1 = P.op("sync", lambda h, oc=oc, dc=dc, ridx=ridx: h.dma_start(
                    wt[ridx][:], scr["winT_hi"][dc * 128:(dc + 1) * 128, oc * 512:(oc + 1) * 512]),
                    waits=[ev_wu.get(wi - 2)], inc="d1", inc_k=16)
                ev_w2 = P.op("sync", lambda h, oc=oc, dc=dc, ridx=ridx: h.dma_start(
                    wt[ridx + 1][:], scr["winT_lo"][dc * 128:(dc + 1) * 128, oc * 512:(oc + 1) * 512]),
                    inc="d1", inc_k=16)

                def mm(h, dc=dc, ridx=ridx):
                    ins = None
                    for tt in range(NTT):
                        stsl = slice(dc * TS + tt * 128, dc * TS + (tt + 1) * 128)
                        ins = h.matmul(PMM[tt][:], xT_hi[:, stsl], wt[ridx][:],
                                       start=(dc == 0), stop=False)
                        ins = h.matmul(PMM[tt][:], xT_hi[:, stsl], wt[ridx + 1][:],
                                       start=False, stop=False)
                        ins = h.matmul(PMM[tt][:], xT_lo[:, stsl], wt[ridx][:],
                                       start=False, stop=(dc == NDC - 1))
                    return ins
                wts = [ev_w1, ev_w2]
                if dc == 0:
                    wts += ev_xT + ev_bank_free
                ev_mm = P.op("tensor", mm, waits=wts, inc="pe")
                ev_wu[wi] = ev_mm
                wno[0] += 1
            for tt in range(NTT):
                if oc < 4:
                    qb_i = tt % 2
                    ev_e = P.op("vector", lambda h, tt=tt, qb_i=qb_i, bb=bb: h.tensor_add(
                        qev[qb_i][:], PMM[tt][:], binb[bb][:]),
                        waits=[ev_mm, ev_bias, prev_qe_done[qb_i]], inc="ve")
                    r0 = tok0 + tt * 128
                    ev_sp = P.op("gpsimd", lambda h, r0=r0, oc=oc, qb_i=qb_i: h.dma_start(
                        scr["q_sp"][r0:r0 + 128, oc * 512:(oc + 1) * 512], qev[qb_i][:]),
                        waits=[ev_e], inc="d2", inc_k=16)
                    prev_qe_done[qb_i] = ev_sp
                    ev_bank_free[tt] = ev_e
                elif oc < 8:
                    ev_e = P.op("vector", lambda h, tt=tt, oc=oc, bb=bb: h.tensor_add(
                        kst[:, tt * D + (oc - 4) * 512: tt * D + (oc - 3) * 512],
                        PMM[tt][:], binb[bb][:]), waits=[ev_mm, ev_bias], inc="ve")
                    ev_bank_free[tt] = ev_e
                else:
                    ev_e = P.op("vector", lambda h, tt=tt, oc=oc, bb=bb: h.tensor_add(
                        vst[:, tt * D + (oc - 8) * 512: tt * D + (oc - 7) * 512],
                        PMM[tt][:], binb[bb][:]), waits=[ev_mm, ev_bias], inc="ve")
                    ev_bank_free[tt] = ev_e
            prev_bias_done[bb] = ev_bank_free[NTT - 1]
        prev_xT_done[0] = ev_mm
        if S1_PART == "mm":
            for tt in range(NTT):
                prev_kst_done[tt] = ev_bank_free[tt]
            continue

        ev_ks = []
        ev_rope_last = None
        for tt in range(NTT):
            base = tt * D
            ev_r = emit_rope(P, kst, base, cs[:, tt * 512:(tt + 1) * 512],
                             sn[:, tt * 512:(tt + 1) * 512], rtmp,
                             waits=[ev_bank_free[tt]] + ev_cs)
            ev_rope_last = ev_r
            ev_ks.append(P.op("scalar", lambda h, base=base, tt=tt: h.activation(
                sejunk[:], kst[:, base:base + D], AF.Square,
                accum_out=stats[:, KS + tt:KS + tt + 1]), waits=[ev_r], inc="se",
                chain="s1jnk"))
        prev_cs_done[0] = ev_rope_last
        P.mset("vector", stats[:, T2:T2 + 64], 1.0)
        ev_msk = P.mset("vector", stats[:, T2:T2 + 64], 1.0)
        ev_sd = P.op("scalar", lambda h: h.sqrt(
            stats[:, T2:T2 + 64], stats[:, KS:KS + 64]), waits=ev_ks + [ev_msk], inc="se")
        ev_cl = P.op("vector", lambda h: h.tensor_single_scalar(
            stats[:, TT_:TT_ + 64], stats[:, T2:T2 + 64], 1e-12, ALU.max),
            waits=[ev_sd], inc="ve")
        ev_rn = P.op("vector", lambda h: h.reciprocal(
            stats[:, RN:RN + 64], stats[:, TT_:TT_ + 64]), waits=[ev_cl], inc="ve")
        if S1_PART == "rope":
            for tt in range(NTT):
                prev_kst_done[tt] = ev_ks[tt]
            continue
        ev_am_prev = None
        for tt in range(NTT):
            ksl = slice(tt * D, (tt + 1) * D)
            ev_f = P.op("vector", lambda h, ksl=ksl, tt=tt: h.scalar_tensor_tensor(
                fmkv[:], kst[:, ksl], stats[:, RN + tt:RN + tt + 1], vst[:, ksl],
                ALU.mult, ALU.mult), waits=[ev_rn, ev_am_prev], inc="ve")
            tile_no[0] += 1
            tn = tile_no[0]

            def amm(h):
                ins = None
                for dc in range(NDC):
                    ins = h.matmul(PA[:, dc:dc + 1], fmkv[:, dc * 128:(dc + 1) * 128],
                                   g["ones_col"][:, 0:1],
                                   start=(dc == 0), stop=(dc == NDC - 1),
                                   skip_group_check=True)
                return ins
            ev_am = P.op("tensor", amm, waits=[ev_f, a_rd_prev[0]], inc="pe")
            ev_aa = P.op("vector", lambda h, tn=tn: h.tensor_add(
                g["A_acc"][tn % 2][:, 0:16], g["A_acc"][(tn - 1) % 2][:, 0:16],
                PA[:, 0:16]), waits=[ev_am], inc="ve")
            a_rd_prev[0] = ev_aa
            ev_am_prev = ev_am
            prev_kst_done[tt] = ev_am


def emit_sweep2(nc, P, ps, inp, scr, out, g):
    PMM = [ps.enter_context(nc.psum_tensor(_nm("pmm"), [128, 512], F32)) for i in range(4)]
    PTR = [ps.enter_context(nc.psum_tensor(_nm("ptrf"), [128, 512], F32)) for i in range(2)]
    PTB = [ps.enter_context(nc.psum_tensor(_nm("ptrb"), [128, 1024], BF16)) for i in range(2)]
    xst = ps.enter_context(nc.sbuf_tensor("s2_xst", [128, NTT * D], F32))
    att = ps.enter_context(nc.sbuf_tensor("s2_att", [128, NTT * D], F32))
    tmpA = ps.enter_context(nc.sbuf_tensor("s2_tmpA", [128, D], F32))
    tmpB = ps.enter_context(nc.sbuf_tensor("s2_tmpB", [128, D], F32))
    ghold = ps.enter_context(nc.sbuf_tensor("s2_gh", [128, NTT * 512], F32))
    uh = ps.enter_context(nc.sbuf_tensor("s2_uh", [128, 512], F32))
    cs = [ps.enter_context(nc.sbuf_tensor("s2_cs0", [128, 512], F32))]
    sn = [ps.enter_context(nc.sbuf_tensor("s2_sn0", [128, 512], F32))]
    bout = [ps.enter_context(nc.sbuf_tensor(f"s2_bo{i}", [128, 512], F32)) for i in range(2)]
    nwb = ps.enter_context(nc.sbuf_tensor("s2_nw", [128, D], F32))
    stats = ps.enter_context(nc.sbuf_tensor("s2_st", [128, 768], F32))
    qT_hi = ps.enter_context(nc.sbuf_tensor("s2_qTh", [128, NDC * TS], BF16))
    qT_lo = ps.enter_context(nc.sbuf_tensor("s2_qTl", [128, NDC * TS], BF16))
    hqA = ps.enter_context(nc.sbuf_tensor("s2_hqA", [128, 32 * TS], BF16))
    hq = ps.enter_context(nc.sbuf_tensor("s2_hq", [128, FF], BF16))
    vvb = ps.enter_context(nc.sbuf_tensor("s2_vv", [128, D], BF16))
    sgnb = ps.enter_context(nc.sbuf_tensor("s2_sgn", [128, D], BF16))
    wt = [ps.enter_context(nc.sbuf_tensor(f"s2_wt{i}", [128, 512], BF16)) for i in range(4)]

    lr0 = ps.enter_context(nc.sbuf_tensor("s2_lr0", [128, D], BF16))
    lr1 = ps.enter_context(nc.sbuf_tensor("s2_lr1", [128, D], BF16))
    sejunk = tmpB[:]
    gx4 = ghold[:, 0:D]
    lad = {"a": tmpB, "vv": vvb, "r0": lr0, "r1": lr1, "sgn": sgnb}

    def hqT_ap(fc, c0, c1):
        if fc < 32:
            base = fc * TS
            return hqA[:, base + c0:base + c1]
        base = (fc - 32) * TS
        return qT_lo[:, base + c0:base + c1]

    ffn = xst

    SUM1, SQS1, M1, T1_, RS1 = 0, 64, 128, 192, 256
    SUM2, SQS2, M2, RS2 = 320, 384, 448, 512
    AMX, SX, SCG = 576, 608, 640
    HM = 704
    _ev1 = P.mset("vector", stats[:, 0:512], 1.0)
    _ev2 = P.mset("vector", stats[:, 512:768], 1.0)
    P.wait("scalar", [_ev1, _ev2])
    ev_nw = P.op("sync", lambda h: h.dma_start(nwb[:], inp["nw_bc"]), inc="d1", inc_k=16)

    prev_stage_done = [None] * NTT
    prev_cs_done = [None]
    prev_qT_done = [None]
    prev_bo_done = [None, None]
    prev_e2 = [None]
    ev_bank_free = [None] * NTT
    ev_wu = {}
    wno = [0]
    tg = {}
    gno = [0]

    def wload(src_ap, r0, c0, waits=()):
        wi = wno[0]
        ridx = wi % 4
        ev = P.op("sync", lambda h, ridx=ridx: h.dma_start(
            wt[ridx][:], src_ap[r0:r0 + 128, c0:c0 + 512]),
            waits=[ev_wu.get(wi - 4)] + list(waits), inc="d1", inc_k=16)
        wno[0] += 1
        return wi, ridx, ev

    for s in range(NST):
        tok0 = s * TS
        ev_x, ev_q = [], []
        for tt in range(NTT):
            r0 = tok0 + tt * 128
            ev_x.append(P.op("sync", lambda h, tt=tt, r0=r0: h.dma_start(
                xst[:, tt * D:(tt + 1) * D], inp["x"][r0:r0 + 128, :]),
                waits=[prev_stage_done[tt]], inc="d1", inc_k=16))
            ev_q.append(P.op("sync", lambda h, tt=tt, r0=r0: h.dma_start(
                att[:, tt * D:(tt + 1) * D], scr["q_sp"][r0:r0 + 128, :]),
                waits=[prev_stage_done[tt]], inc="d1", inc_k=16))
        ev_red, ev_sq = [], []
        for tt in range(NTT):
            ev_red.append(P.op("vector", lambda h, tt=tt: h.reduce_sum(
                stats[:, SUM1 + tt:SUM1 + tt + 1], xst[:, tt * D:(tt + 1) * D], axis=AX.X),
                waits=[ev_x[tt]], inc="ve"))
            ev_sq.append(P.op("scalar", lambda h, tt=tt: h.activation(
                sejunk, xst[:, tt * D:(tt + 1) * D], AF.Square,
                accum_out=stats[:, SQS1 + tt:SQS1 + tt + 1]), waits=[ev_x[tt]], inc="se",
                chain="s2_tmpB"))
        ev_ch1 = emit_ln_chain(P, stats, SUM1, SQS1, M1, T1_, RS1, waits=ev_red + ev_sq)
        ev_qT = []
        for tt in range(NTT):
            base = tt * D
            pp = 0
            r0 = tok0 + tt * 128
            evc = P.op("sync", lambda h, pp=pp, r0=r0: h.dma_start(
                cs[pp][:], inp["q_cos"][r0:r0 + 128, :]),
                waits=[prev_cs_done[pp]], inc="d1", inc_k=16)
            evs_ = P.op("sync", lambda h, pp=pp, r0=r0: h.dma_start(
                sn[pp][:], inp["q_sin"][r0:r0 + 128, :]), inc="d1", inc_k=16)
            ev_r = emit_rope(P, att, base, cs[pp][:], sn[pp][:], tmpA,
                             waits=[ev_q[tt], evc, evs_], tag="s2_tmpA")
            prev_cs_done[pp] = ev_r
            ev_last = None
            for gi in range(4):
                bank = PTR[gno[0] % 2]

                def tr(h, gi=gi, bank=bank, base=base):
                    ins = None
                    for k in range(4):
                        dc = gi * 4 + k
                        ins = h.transpose(bank[:, k * 128:(k + 1) * 128],
                                          att[:, base + dc * 128:base + (dc + 1) * 128],
                                          g["identf"][:])
                    return ins
                ev_t = P.op("tensor", tr, waits=[ev_r, tg.get(gno[0] - 2)], inc="pe")
                hi3 = qT_hi[:].rearrange("p (dc t) -> p dc t", t=TS)[
                    :, gi * 4:(gi + 1) * 4, tt * 128:(tt + 1) * 128]
                lo3 = qT_lo[:].rearrange("p (dc t) -> p dc t", t=TS)[
                    :, gi * 4:(gi + 1) * 4, tt * 128:(tt + 1) * 128]
                bk3 = bank[:].rearrange("p (k c) -> p k c", c=128)
                ev_hi = P.op("scalar", lambda h, hi3=hi3, bk3=bk3: h.copy(hi3, bk3),
                             waits=[ev_t], inc="se")
                ev_lo = P.op("vector", lambda h, lo3=lo3, bk3=bk3, hi3=hi3: h.tensor_sub(
                    lo3, bk3, hi3), waits=[ev_hi], inc="ve")
                tg[gno[0]] = ev_lo
                gno[0] += 1
                ev_last = ev_lo
            ev_qT.append(ev_last)
        ev_mm = None
        for oc2 in range(4):
            bb = oc2 % 2
            ev_bo = P.op("sync", lambda h, oc2=oc2, bb=bb: h.dma_start(
                bout[bb][:], inp["b_out_bc"][:, oc2 * 512:(oc2 + 1) * 512]),
                waits=[prev_bo_done[bb]], inc="d1", inc_k=16)
            for dc in range(NDC):
                wi1, r1_, ev_w1 = wload(scr["woutT_hi"], dc * 128, oc2 * 512)
                wi2, r2_, ev_w2 = wload(scr["woutT_lo"], dc * 128, oc2 * 512)

                def mm(h, dc=dc, r1_=r1_, r2_=r2_):
                    ins = None
                    for tt in range(NTT):
                        stsl = slice(dc * TS + tt * 128, dc * TS + (tt + 1) * 128)
                        ins = h.matmul(PMM[tt][:], qT_hi[:, stsl], wt[r1_][:],
                                       start=(dc == 0), stop=False)
                        ins = h.matmul(PMM[tt][:], qT_hi[:, stsl], wt[r2_][:],
                                       start=False, stop=False)
                        ins = h.matmul(PMM[tt][:], qT_lo[:, stsl], wt[r1_][:],
                                       start=False, stop=(dc == NDC - 1))
                    return ins
                wts = [ev_w1, ev_w2]
                if dc == 0:
                    wts += ev_qT + ev_bank_free
                ev_mm = P.op("tensor", mm, waits=wts, inc="pe")
                ev_wu[wi1] = ev_mm
                ev_wu[wi2] = ev_mm
            for tt in range(NTT):
                csl = slice(oc2 * 512, (oc2 + 1) * 512)
                ev_e1 = P.op("vector", lambda h, tt=tt, bb=bb: h.tensor_add(
                    uh[:], PMM[tt][:], bout[bb][:]),
                    waits=[ev_mm, ev_bo, prev_e2[0]], inc="ve", chain="s2_uh")
                ev_t1 = P.op("vector", lambda h, tt=tt, csl=csl: h.tensor_scalar(
                    tmpA[:, 0:512], xst[:, tt * D + csl.start:tt * D + csl.stop],
                    stats[:, M1 + tt:M1 + tt + 1], stats[:, RS1 + tt:RS1 + tt + 1],
                    ALU.subtract, ALU.mult), waits=[ev_ch1], inc="ve", chain="s2_tmpA")
                ev_e2 = P.op("vector", lambda h, tt=tt, csl=csl: h.tensor_add(
                    att[:, tt * D + csl.start:tt * D + csl.stop], uh[:], tmpA[:, 0:512]),
                    waits=[ev_e1, ev_t1], inc="ve", chain="s2_tmpA")
                prev_e2[0] = ev_e2
                ev_bank_free[tt] = ev_e2
            prev_bo_done[bb] = ev_bank_free[NTT - 1]
        prev_qT_done[0] = ev_mm
        ev_red2, ev_sq2 = [], []
        for tt in range(NTT):
            ev_red2.append(P.op("vector", lambda h, tt=tt: h.reduce_sum(
                stats[:, SUM2 + tt:SUM2 + tt + 1], att[:, tt * D:(tt + 1) * D], axis=AX.X),
                waits=[ev_bank_free[tt]], inc="ve"))
            ev_sq2.append(P.op("scalar", lambda h, tt=tt: h.activation(
                sejunk, att[:, tt * D:(tt + 1) * D], AF.Square,
                accum_out=stats[:, SQS2 + tt:SQS2 + tt + 1]),
                waits=[ev_bank_free[tt]], inc="se", chain="s2_tmpB"))
        ev_ch2 = emit_ln_chain(P, stats, SUM2, SQS2, M2, T1_, RS2, waits=ev_red2 + ev_sq2)
        ev_x2 = []
        ev_sq3 = []
        for tt in range(NTT):
            tsl = slice(tt * D, (tt + 1) * D)
            ev_a = P.op("vector", lambda h, tsl=tsl, tt=tt: h.tensor_scalar(
                tmpA[:], att[:, tsl], stats[:, M2 + tt:M2 + tt + 1],
                stats[:, RS2 + tt:RS2 + tt + 1], ALU.subtract, ALU.mult),
                waits=[ev_ch2], inc="ve", chain="s2_tmpA")
            ev_b = P.op("vector", lambda h, tsl=tsl, tt=tt: h.scalar_tensor_tensor(
                att[:, tsl], tmpA[:], 1.0, xst[:, tsl], ALU.mult, ALU.add,
                accum_out=stats[:, SUM1 + tt:SUM1 + tt + 1]), waits=[ev_a], inc="ve",
                chain="s2_tmpA")
            ev_x2.append(ev_b)
            ev_sq3.append(P.op("scalar", lambda h, tsl=tsl, tt=tt: h.activation(
                sejunk, att[:, tsl], AF.Square,
                accum_out=stats[:, SQS1 + tt:SQS1 + tt + 1]), waits=[ev_b], inc="se",
                chain="s2_tmpB"))
        ev_ch3 = emit_ln_chain(P, stats, SUM1, SQS1, M1, T1_, RS1, waits=ev_x2 + ev_sq3)
        ev = P.op("vector", lambda h: h.tensor_mul(
            stats[:, T1_:T1_ + 64], stats[:, M1:M1 + 64], stats[:, M1:M1 + 64]),
            waits=[ev_ch3], inc="ve")
        ev = P.op("vector", lambda h: h.scalar_tensor_tensor(
            stats[:, T1_:T1_ + 64], stats[:, SQS1:SQS1 + 64], 1.0 / D, stats[:, T1_:T1_ + 64],
            ALU.mult, ALU.subtract), waits=[ev], inc="ve")
        ev = P.op("vector", lambda h: h.tensor_mul(
            stats[:, SUM2:SUM2 + 64], stats[:, RS1:RS1 + 64], stats[:, RS1:RS1 + 64]),
            waits=[ev], inc="ve")
        ev = P.op("vector", lambda h: h.tensor_mul(
            stats[:, T1_:T1_ + 64], stats[:, T1_:T1_ + 64], stats[:, SUM2:SUM2 + 64]),
            waits=[ev], inc="ve")
        ev = P.op("vector", lambda h: h.tensor_scalar(
            stats[:, SQS2:SQS2 + 64], stats[:, T1_:T1_ + 64], EPS, None, ALU.add),
            waits=[ev], inc="ve")
        ev = P.op("scalar", lambda h: h.sqrt(
            stats[:, T1_:T1_ + 64], stats[:, SQS2:SQS2 + 64]), waits=[ev], inc="se")
        ev = P.op("vector", lambda h: h.reciprocal(
            stats[:, SQS2:SQS2 + 64], stats[:, T1_:T1_ + 64]), waits=[ev], inc="ve")
        ev_c3 = P.op("vector", lambda h: h.tensor_mul(
            stats[:, RS2:RS2 + 64], stats[:, RS1:RS1 + 64], stats[:, SQS2:SQS2 + 64]),
            waits=[ev], inc="ve")
        ev_xqT = []
        ev_lad_prev = None
        for tt in range(NTT):
            tsl = slice(tt * D, (tt + 1) * D)
            ev_p1 = P.op("vector", lambda h, tsl=tsl, tt=tt: h.tensor_scalar(
                tmpA[:], att[:, tsl], stats[:, M1 + tt:M1 + tt + 1],
                stats[:, RS2 + tt:RS2 + tt + 1], ALU.subtract, ALU.mult),
                waits=[ev_c3, ev_lad_prev], inc="ve", chain="s2_tmpA")
            ev_x4 = P.op("vector", lambda h: h.tensor_mul(
                gx4, tmpA[:], nwb[:]), waits=[ev_p1, ev_nw], inc="ve", chain="s2_tmpA")
            ev_mx = P.op("vector", lambda h, tt=tt: h.reduce_max(
                stats[:, AMX + tt:AMX + tt + 1], gx4, axis=AX.X,
                apply_absolute_value=True), waits=[ev_x4], inc="ve")
            P.mset("vector", stats[:, T1_:T1_ + 64], 1.0)
            ev = P.op("vector", lambda h: h.tensor_single_scalar(
                stats[:, T1_:T1_ + 64], stats[:, AMX:AMX + 64], 1e-5, ALU.max),
                waits=[ev_mx], inc="ve")
            ev = P.op("vector", lambda h: h.reciprocal(
                stats[:, SUM2:SUM2 + 64], stats[:, T1_:T1_ + 64]), waits=[ev], inc="ve")
            ev_sx = P.op("vector", lambda h: h.tensor_scalar(
                stats[:, SX:SX + 64], stats[:, SUM2:SUM2 + 64], 127.0, None, ALU.mult),
                waits=[ev], inc="ve")
            ev_sc = P.op("vector", lambda h: h.tensor_scalar(
                stats[:, SCG:SCG + 64], stats[:, T1_:T1_ + 64], g["wconst"][:, 2:3],
                0.5 / 127.0, ALU.mult, ALU.mult), waits=[ev_sx], inc="ve")
            ev_lq = emit_quant_ladder(P, gx4, D, stats[:, SX + tt:SX + tt + 1],
                                      hq[:, 0:D], lad, waits=[ev_sx, ev_lad_prev],
                                      a_chain="s2_tmpB")
            ev_last = None
            for gi in range(4):
                bank = PTB[gno[0] % 2]

                def tr(h, gi=gi, bank=bank):
                    ins = None
                    for k in range(4):
                        dc = gi * 4 + k
                        ins = h.transpose(bank[:, k * 128:(k + 1) * 128],
                                          hq[:, dc * 128:(dc + 1) * 128], g["identb"][:])
                    return ins
                ev_t = P.op("tensor", tr, waits=[ev_lq, tg.get(gno[0] - 2)], inc="pe")
                dst3 = qT_hi[:].rearrange("p (dc t) -> p dc t", t=TS)[
                    :, gi * 4:(gi + 1) * 4, tt * 128:(tt + 1) * 128]
                bk3 = bank[:].rearrange("p (k c) -> p k c", c=128)[:, 0:4, :]
                ev_e = P.op("vector", lambda h, dst3=dst3, bk3=bk3: h.tensor_copy(dst3, bk3),
                            waits=[ev_t], inc="ve")
                tg[gno[0]] = ev_e
                gno[0] += 1
                ev_last = ev_e
            ev_lad_prev = ev_last
            ev_xqT.append(ev_last)
        ev_h_sp = {}
        prev_gh = [[] for _ in range(NTT)]
        ev_hmax = []
        for j in range(11):
            for phase, oc3 in (("g", j), ("u", 11 + j)):
                for dc in range(NDC):
                    wi1, r1_, ev_w1 = wload(scr["guT_q"], dc * 128, oc3 * 512)

                    def mm(h, dc=dc, r1_=r1_):
                        ins = None
                        for tt in range(NTT):
                            stsl = slice(dc * TS + tt * 128, dc * TS + (tt + 1) * 128)
                            ins = h.matmul(PMM[tt][:], qT_hi[:, stsl], wt[r1_][:],
                                           start=(dc == 0), stop=(dc == NDC - 1))
                        return ins
                    wts = [ev_w1]
                    if dc == 0:
                        wts += ev_xqT + ev_bank_free
                    ev_mm = P.op("tensor", mm, waits=wts, inc="pe")
                    ev_wu[wi1] = ev_mm
                for tt in range(NTT):
                    if phase == "g":
                        ev_e = P.op("vector", lambda h, tt=tt: h.tensor_scalar(
                            ghold[:, tt * 512:(tt + 1) * 512], PMM[tt][:],
                            stats[:, SCG + tt:SCG + tt + 1], 0.0, ALU.mult, ALU.max),
                            waits=[ev_mm] + prev_gh[tt], inc="ve", chain="s2_gx4")
                        ev_bank_free[tt] = ev_e
                    else:
                        ev_u = P.op("vector", lambda h, tt=tt: h.tensor_scalar(
                            uh[:], PMM[tt][:], stats[:, SCG + tt:SCG + tt + 1], None,
                            ALU.mult), waits=[ev_mm], inc="ve", chain="s2_uh")
                        ev_s = P.op("scalar", lambda h, tt=tt: h.square(
                            tmpA[:, 0:512], ghold[:, tt * 512:(tt + 1) * 512]),
                            waits=[ev_bank_free[tt], ev_h_sp.get(("hh_prev",))], inc="se",
                            chain="s2_tmpA")
                        hb = ghold[:, tt * 512:(tt + 1) * 512]
                        ev_hh = P.op("vector", lambda h, hb=hb: h.tensor_mul(
                            hb, tmpA[:, 0:512], uh[:]),
                            waits=[ev_u, ev_s], inc="ve", chain="s2_uh")
                        ev_h_sp[("hh_prev",)] = ev_hh
                        ev_hm = P.op("vector", lambda h, tt=tt, j=j, hb=hb: h.reduce_max(
                            stats[:, HM + tt * 16 + j:HM + tt * 16 + j + 1], hb,
                            axis=AX.X, apply_absolute_value=True), waits=[ev_hh], inc="ve")
                        ev_hmax.append(ev_hm)
                        ev_d = P.op("gpsimd", lambda h, tt=tt, j=j, hb=hb: h.dma_start(
                            scr["h_sp"][tt * 128:(tt + 1) * 128, j * 512:(j + 1) * 512], hb),
                            waits=[ev_hh], inc="d2", inc_k=16)
                        ev_h_sp[(j, tt)] = ev_d
                        prev_gh[tt] = [ev_d, ev_hm]
                        ev_bank_free[tt] = ev_hh
        ev_hqT = [None] * NTT
        ev_lad2 = None
        for tt in range(NTT):
            ev_am = P.op("vector", lambda h, tt=tt: h.reduce_max(
                stats[:, AMX + tt:AMX + tt + 1],
                stats[:, HM + tt * 16:HM + tt * 16 + 11], axis=AX.X), waits=ev_hmax, inc="ve")
            P.mset("vector", stats[:, T1_:T1_ + 64], 1.0)
            ev = P.op("vector", lambda h: h.tensor_single_scalar(
                stats[:, T1_:T1_ + 64], stats[:, AMX:AMX + 64], 1e-5, ALU.max),
                waits=[ev_am], inc="ve")
            ev = P.op("vector", lambda h: h.reciprocal(
                stats[:, SUM2:SUM2 + 64], stats[:, T1_:T1_ + 64]), waits=[ev], inc="ve")
            ev_sx2 = P.op("vector", lambda h: h.tensor_scalar(
                stats[:, SX:SX + 64], stats[:, SUM2:SUM2 + 64], 127.0, None, ALU.mult),
                waits=[ev], inc="ve")
            ev_sc2 = P.op("vector", lambda h: h.tensor_scalar(
                stats[:, SCG:SCG + 64], stats[:, T1_:T1_ + 64], g["wconst"][:, 5:6],
                0.5 / 127.0, ALU.mult, ALU.mult), waits=[ev_sx2], inc="ve")
            ev_lads = []
            cbuf = [tmpA, ghold]
            for ci, (c0, cw) in enumerate([(0, 2048), (2048, 2048), (4096, 1536)]):
                cb = cbuf[ci % 2]
                chn = "s2_tmpA" if ci % 2 == 0 else "s2_gx4"
                ev_ld = P.op("sync", lambda h, tt=tt, c0=c0, cw=cw, cb=cb: h.dma_start(
                    cb[:, 0:cw], scr["h_sp"][tt * 128:(tt + 1) * 128, c0:c0 + cw]),
                    waits=sum(prev_gh, [])
                    + [ev_lads[-2] if len(ev_lads) >= 2 else ev_lad2],
                    inc="d1", inc_k=16, chain=chn)
                ev_lad2 = emit_quant_ladder(
                    P, cb[:, 0:cw], cw, stats[:, SX + tt:SX + tt + 1],
                    hq[:, c0:c0 + cw], lad,
                    waits=[ev_ld, ev_sx2, ev_lad2] + (ev_xqT if (tt == 0 and ci == 0) else []),
                    a_chain="s2_tmpB")
                ev_lads.append(ev_lad2)
            ev_last = None
            chunk_of = [0, 0, 1, 1, 2, 2]
            for gi in range(6):
                n_in = min(8, NFC - gi * 8)
                bank = PTB[gno[0] % 2]

                def tr(h, gi=gi, n_in=n_in, bank=bank):
                    ins = None
                    for k in range(n_in):
                        fc = gi * 8 + k
                        ins = h.transpose(bank[:, k * 128:(k + 1) * 128],
                                          hq[:, fc * 128:(fc + 1) * 128], g["identb"][:])
                    return ins
                ev_t = P.op("tensor", tr, waits=[ev_lads[chunk_of[gi]],
                                                 tg.get(gno[0] - 2)], inc="pe")
                arena = hqA[:] if gi < 4 else qT_lo[:]
                fc0 = gi * 8 if gi < 4 else gi * 8 - 32
                dst3 = arena.rearrange("p (fc t) -> p fc t", t=TS)[
                    :, fc0:fc0 + n_in, tt * 128:(tt + 1) * 128]
                bk3 = bank[:].rearrange("p (k c) -> p k c", c=128)[:, 0:n_in, :]
                ev_e = P.op("vector", lambda h, dst3=dst3, bk3=bk3: h.tensor_copy(dst3, bk3),
                            waits=[ev_t], inc="ve")
                tg[gno[0]] = ev_e
                gno[0] += 1
                ev_last = ev_e
            ev_lad2 = ev_last
            ev_hqT[tt] = ev_last
        for oc4 in range(4):
            for fc in range(NFC):
                wi1, r1_, ev_w1 = wload(scr["dnT_q"], fc * 128, oc4 * 512)

                def mm(h, fc=fc, r1_=r1_):
                    ins = None
                    for tt in range(NTT):
                        ins = h.matmul(PMM[tt][:], hqT_ap(fc, tt * 128, (tt + 1) * 128),
                                       wt[r1_][:], start=(fc == 0), stop=(fc == NFC - 1))
                    return ins
                wts = [ev_w1]
                if fc == 0:
                    wts += ev_hqT + ev_bank_free
                ev_mm = P.op("tensor", mm, waits=wts, inc="pe")
                ev_wu[wi1] = ev_mm
            for tt in range(NTT):
                csl = slice(oc4 * 512, (oc4 + 1) * 512)
                ev_e = P.op("vector", lambda h, tt=tt, csl=csl: h.tensor_scalar(
                    ffn[:, tt * D + csl.start: tt * D + csl.stop], PMM[tt][:],
                    stats[:, SCG + tt:SCG + tt + 1], None, ALU.mult),
                    waits=[ev_mm], inc="ve")
                ev_bank_free[tt] = ev_e
        ev_red4, ev_sq4 = [], []
        for tt in range(NTT):
            ev_red4.append(P.op("vector", lambda h, tt=tt: h.reduce_sum(
                stats[:, SUM1 + tt:SUM1 + tt + 1], ffn[:, tt * D:(tt + 1) * D], axis=AX.X),
                waits=[ev_bank_free[tt]], inc="ve"))
            ev_sq4.append(P.op("scalar", lambda h, tt=tt: h.activation(
                sejunk, ffn[:, tt * D:(tt + 1) * D], AF.Square,
                accum_out=stats[:, SQS1 + tt:SQS1 + tt + 1]),
                waits=[ev_bank_free[tt]], inc="se", chain="s2_tmpB"))
        ev_ch4 = emit_ln_chain(P, stats, SUM1, SQS1, M1, T1_, RS1, waits=ev_red4 + ev_sq4)
        prev_st = None
        prev_o = None
        for tt in range(NTT):
            tsl = slice(tt * D, (tt + 1) * D)
            ev_a = P.op("vector", lambda h, tsl=tsl, tt=tt: h.tensor_scalar(
                tmpA[:], ffn[:, tsl], stats[:, M1 + tt:M1 + tt + 1],
                stats[:, RS1 + tt:RS1 + tt + 1], ALU.subtract, ALU.mult),
                waits=[ev_ch4, prev_o], inc="ve", chain="s2_tmpA")
            ev_o = P.op("vector", lambda h, tsl=tsl: h.tensor_add(
                tmpB[:], tmpA[:], att[:, tsl]), waits=[ev_a, prev_st], inc="ve",
                chain="s2_tmpB")
            prev_o = ev_o
            r0 = tok0 + tt * 128
            ev_st = P.op("gpsimd", lambda h, r0=r0: h.dma_start(
                out[r0:r0 + 128, :], tmpB[:]), waits=[ev_o], inc="d2", inc_k=16,
                chain="s2_tmpB")
            prev_st = ev_st
            prev_stage_done[tt] = ev_st
    P.wait("gpsimd", [prev_stage_done[tt] for tt in range(NTT)])


def kernel(L0_attn_X, L0_attn_W_in, L0_attn_b_in, L0_attn_W_out, L0_attn_b_out,
           L0_ffn_w_gate_up, L0_ffn_w_down, L0_ffn_norm_weight):
    X = np.ascontiguousarray(np.asarray(L0_attn_X, dtype=np.float32))
    W_in = np.ascontiguousarray(np.asarray(L0_attn_W_in, dtype=np.float32))
    b_in = np.asarray(L0_attn_b_in, dtype=np.float32)
    W_out = np.ascontiguousarray(np.asarray(L0_attn_W_out, dtype=np.float32))
    b_out = np.asarray(L0_attn_b_out, dtype=np.float32)
    w_gu = np.ascontiguousarray(np.asarray(L0_ffn_w_gate_up, dtype=np.float32))
    w_dn = np.ascontiguousarray(np.asarray(L0_ffn_w_down, dtype=np.float32))
    nw = np.asarray(L0_ffn_norm_weight, dtype=np.float32)
    return _run(X, W_in, b_in, W_out, b_out, w_gu, w_dn, nw, build_program())


def _run(X, W_in, b_in, W_out, b_out, w_gu, w_dn, nw, nc, extra_keys=()):
    b_in_bc = np.ascontiguousarray(np.broadcast_to(b_in[None, :], (128, E)))
    b_out_bc = np.ascontiguousarray(np.broadcast_to(b_out[None, :], (128, D)))
    nw_bc = np.ascontiguousarray(np.broadcast_to(nw[None, :], (128, D)))
    in_maps = []
    for c in range(8):
        b, half = c // 2, c % 2
        qc, qs = _rope_tables(np.arange(half * T, (half + 1) * T))
        oc_, os_ = _rope_tables(np.arange((1 - half) * T, (2 - half) * T))
        in_maps.append({
            "x": np.ascontiguousarray(X[b, half * T:(half + 1) * T, :]),
            "xo": np.ascontiguousarray(X[b, (1 - half) * T:(2 - half) * T, :]),
            "q_cos": qc, "q_sin": qs, "o_cos": oc_, "o_sin": os_,
            "w_in": W_in, "w_out": W_out, "w_gu": w_gu, "w_dn": w_dn,
            "b_in_bc": b_in_bc, "b_out_bc": b_out_bc, "nw_bc": nw_bc,
        })
    res = run_bass_kernel_spmd(nc, in_maps, core_ids=list(range(8)))
    full = np.empty((4, 4096, D), dtype=np.float32)
    for c in range(8):
        b, half = c // 2, c % 2
        full[b, half * T:(half + 1) * T, :] = res.results[c]["out"]
    if extra_keys:
        return full, res
    return full


# version 08
# speedup vs baseline: 1.2803x; 1.0147x over previous
import contextlib
import numpy as np

import concourse.bass as bass
import concourse.mybir as mybir
from concourse.bass_utils import run_bass_kernel_spmd

F32 = mybir.dt.float32
BF16 = mybir.dt.bfloat16
ALU = mybir.AluOpType
AF = mybir.ActivationFunctionType
AX = mybir.AxisListType

D = 2048
E = 6144
FF = 5632
GU = 11264
T = 2048
S1_PART = "full"
TS = 512
NTT = TS // 128
NST = T // TS
NDC = D // 128
NFC = FF // 128
ROPE_THETA = 10000.0
EPS = 1e-5


class Ev:
    __slots__ = ("sem", "val")

    def __init__(self, sem, val):
        self.sem = sem
        self.val = val


class SemCtr:
    def __init__(self, handle):
        self.h = handle
        self.n = 0

    def next(self, k):
        self.n += k
        return Ev(self, self.n)


class Prog:
    ENGS = ("sync", "gpsimd", "vector", "scalar", "tensor")

    def __init__(self, nc, st):
        self.nc = nc
        self.items = {e: [] for e in self.ENGS}
        self.sems = {}
        for name in ("pe", "ve", "se", "bar"):
            self.sems[name] = SemCtr(st.enter_context(nc.semaphore("s_" + name)))
        self.pools = {"d1": [], "d2": []}
        for name, n in (("d1", 6), ("d2", 4)):
            for i in range(n):
                s = SemCtr(st.enter_context(nc.semaphore(f"s_{name}_{i}")))
                self.sems[f"{name}_{i}"] = s
                self.pools[name].append(s)
        self.rr = {"d1": 0, "d2": 0}
        self.chains = {}

    def op(self, eng, fn, waits=(), inc=None, inc_k=1, chain=None):
        ws = [w for w in waits if w is not None]
        if chain is not None and chain in self.chains:
            ws.append(self.chains[chain])
        if inc in ("d1", "d2"):
            sem = self.pools[inc][self.rr[inc] % len(self.pools[inc])]
            self.rr[inc] += 1
            ws.append(Ev(sem, sem.n))
            ev = sem.next(inc_k)
        else:
            ev = self.sems[inc].next(inc_k) if inc else None

        def run(h, ws=ws, ev=ev, fn=fn, inc_k=inc_k):
            for w in ws:
                h.wait_ge(w.sem.h, w.val)
            ins = fn(h)
            if ev is not None:
                ins.then_inc(ev.sem.h, inc_k)
        self.items[eng].append(run)
        if chain is not None:
            assert ev is not None
            self.chains[chain] = ev
        return ev

    ESEM = {"vector": "ve", "scalar": "se", "tensor": "pe", "sync": "ve", "gpsimd": "ve"}

    def mset(self, eng, ap, val, waits=()):
        ev = self.op(eng, lambda h: h.memset(ap, val), waits=waits, inc=self.ESEM[eng])
        self.wait(eng, [ev])
        return ev

    def wait(self, eng, waits):
        ws = [w for w in waits if w is not None]

        def run(h, ws=ws):
            for w in ws:
                h.wait_ge(w.sem.h, w.val)
        self.items[eng].append(run)

    def barrier(self):
        bar = self.sems["bar"]
        pre = [Ev(s, s.n) for k, s in self.sems.items() if k != "bar"]
        for e in self.ENGS:
            self.wait(e, pre)
            self.op(e, lambda h: h.nop(), inc="bar")
        tgt = Ev(bar, bar.n)
        for e in self.ENGS:
            self.wait(e, [tgt])

    def install(self, block):
        for name in self.ENGS:
            prog = self.items[name]
            section = getattr(block, name)

            def body(h, prog=prog):
                for fn in prog:
                    fn(h)
            section(body)


_NAME_N = [0]


def _nm(base):
    _NAME_N[0] += 1
    return f"{base}_{_NAME_N[0]}"


def _rope_tables(pos):
    import jax
    import jax.numpy as jnp
    cpu = jax.devices("cpu")[0]
    with jax.default_device(cpu):
        inv = 1.0 / ROPE_THETA ** (jnp.arange(0, 1024, 2, dtype=jnp.float32) / 1024)
        fr = jnp.asarray(pos, dtype=jnp.float32)[:, None] * inv[None, :]
        c, s = np.asarray(jnp.cos(fr)), np.asarray(jnp.sin(fr))
    return np.ascontiguousarray(c), np.ascontiguousarray(s)


def build_program(debug=(), upto="all", s1_jobs=None, skip_p0=False):
    debug = set(debug)
    nc = bass.Bass("TRN2", target_bir_lowering=False)

    inp = {}
    for name, shape, dt in [
        ("x", [T, D], F32), ("xo", [T, D], F32),
        ("q_cos", [T, 512], F32), ("q_sin", [T, 512], F32),
        ("o_cos", [T, 512], F32), ("o_sin", [T, 512], F32),
        ("w_in", [E, D], F32), ("w_out", [D, D], F32),
        ("w_gu", [GU, D], F32), ("w_dn", [D, FF], F32),
        ("b_in_bc", [128, E], F32), ("b_out_bc", [128, D], F32),
        ("nw_bc", [128, D], F32),
    ]:
        inp[name] = nc.dram_tensor(name, shape, dt, kind="ExternalInput").ap()
    out = nc.dram_tensor("out", [T, D], F32, kind="ExternalOutput").ap()

    scr = {}
    for name, shape, dt in [
        ("winT_hi", [D, E], BF16), ("winT_lo", [D, E], BF16),
        ("woutT_hi", [D, D], BF16), ("woutT_lo", [D, D], BF16),
        ("guT_q", [D, GU], BF16), ("dnT_q", [FF, D], BF16),
        ("q_sp", [T, D], F32), ("h_sp", [TS, FF], F32),
    ]:
        scr[name] = nc.dram_tensor(name, shape, dt).ap()

    dbg_out = {}

    def dbg(name, shape, dt=F32):
        dbg_out[name] = nc.dram_tensor(name, shape, dt, kind="ExternalOutput").ap()
        return dbg_out[name]

    with contextlib.ExitStack() as st:
        P = Prog(nc, st)

        g = {}
        g["identf"] = st.enter_context(nc.sbuf_tensor("identf", [128, 128], F32))
        g["identb"] = st.enter_context(nc.sbuf_tensor("identb", [128, 128], BF16))
        g["iot"] = st.enter_context(nc.sbuf_tensor("iot", [128, 128], F32))
        g["ones_col"] = st.enter_context(nc.sbuf_tensor("ones_col", [128, 8], F32))
        g["ones_row"] = st.enter_context(nc.sbuf_tensor("ones_row", [1, 128], F32))
        g["wconst"] = st.enter_context(nc.sbuf_tensor("wconst", [128, 64], F32))
        g["A_acc"] = [st.enter_context(nc.sbuf_tensor(f"A_acc{i}", [128, 64], F32))
                      for i in range(2)]

        ev_iota = P.op("gpsimd", lambda h: h.iota(
            g["iot"][:], [[1, 128]], channel_multiplier=-1,
            allow_small_or_imprecise_dtypes=True), inc="ve")
        P.op("vector", lambda h: h.tensor_single_scalar(
            g["identf"][:], g["iot"][:], 0.0, ALU.is_equal), waits=[ev_iota], inc="ve")
        P.op("vector", lambda h: h.tensor_single_scalar(
            g["identb"][:], g["iot"][:], 0.0, ALU.is_equal), inc="ve")
        P.mset("vector", g["ones_col"][:], 1.0)
        P.mset("vector", g["ones_row"][:], 1.0)
        P.mset("vector", g["wconst"][:], 1.0)
        P.mset("vector", g["A_acc"][0][:], 0.0)
        P.mset("vector", g["A_acc"][1][:], 0.0)
        P.barrier()

        if not skip_p0:
            with contextlib.ExitStack() as ps:
                emit_wprep_hilo(nc, P, ps, inp["w_in"], scr["winT_hi"], scr["winT_lo"],
                                rows=E, g=g, scale_col=None)
            P.barrier()
        if "winT" in debug:
            emit_scratch_dump(nc, P, dbg("dbg_winT_hi", [D, E], BF16), scr["winT_hi"], D, E, BF16)
            emit_scratch_dump(nc, P, dbg("dbg_winT_lo", [D, E], BF16), scr["winT_lo"], D, E, BF16)
            P.barrier()
        if upto == "p0a":
            emit_dummy_out(nc, P, out)
            with nc.Block() as block:
                P.install(block)
            return nc

        if not skip_p0:
            with contextlib.ExitStack() as ps:
                emit_ffn_means(nc, P, ps, inp["w_gu"], inp["w_dn"], g)
            P.barrier()

        if not skip_p0:
            with contextlib.ExitStack() as ps:
                emit_wprep_quant(nc, P, ps, inp["w_gu"], scr["guT_q"], rows=GU,
                                 thr_pos=g["wconst"][:, 0:1], thr_neg=g["wconst"][:, 1:2],
                                 g=g, cols=D)
            P.barrier()
            with contextlib.ExitStack() as ps:
                emit_wprep_quant(nc, P, ps, inp["w_dn"], scr["dnT_q"], rows=D,
                                 thr_pos=g["wconst"][:, 3:4], thr_neg=g["wconst"][:, 4:5],
                                 g=g, cols=FF)
            P.barrier()
        if "wq" in debug:
            emit_scratch_dump(nc, P, dbg("dbg_guT", [D, GU], BF16), scr["guT_q"], D, GU, BF16)
            emit_scratch_dump(nc, P, dbg("dbg_dnT", [FF, D], BF16), scr["dnT_q"], FF, D, BF16)
            o5 = dbg("dbg_wconst", [128, 64], F32)
            evd = P.op("gpsimd", lambda h: h.dma_start(o5, g["wconst"][:]), inc="d2", inc_k=16)
            P.wait("gpsimd", [evd])
            P.barrier()
        if upto == "p0":
            emit_dummy_out(nc, P, out)
            with nc.Block() as block:
                P.install(block)
            return nc

        with contextlib.ExitStack() as ps:
            emit_sweep1(nc, P, ps, inp, scr, g, s1_jobs=s1_jobs)
        P.barrier()

        P.barrier()
        if "s1" in debug:
            emit_scratch_dump(nc, P, dbg("dbg_qsp", [T, D], F32), scr["q_sp"], T, D, F32)
            o7 = dbg("dbg_A", [128, 64], F32)
            n_t = (s1_jobs if s1_jobs is not None else 2 * NST) * NTT
            evd = P.op("gpsimd", lambda h: h.dma_start(o7, g["A_acc"][n_t % 2][:]),
                       inc="d2", inc_k=16)
            P.wait("gpsimd", [evd])
            P.barrier()
        if upto == "p1":
            emit_dummy_out(nc, P, out)
            with nc.Block() as block:
                P.install(block)
            return nc

        n_s1_tiles = (s1_jobs if s1_jobs is not None else 2 * NST) * NTT
        with contextlib.ExitStack() as ps:
            emit_wprep_hilo(nc, P, ps, inp["w_out"], scr["woutT_hi"], scr["woutT_lo"],
                            rows=D, g=g, scale_col=g["A_acc"][n_s1_tiles % 2])
        P.barrier()

        with contextlib.ExitStack() as ps:
            emit_sweep2(nc, P, ps, inp, scr, out, g)
        P.barrier()

        with nc.Block() as block:
            P.install(block)
    return nc


def emit_dummy_out(nc, P, out):
    with contextlib.ExitStack() as ps:
        z = ps.enter_context(nc.sbuf_tensor(_nm("zz"), [128, D], F32))
        ev = P.mset("vector", z[:], 0.0)
        evs = []
        for r in range(T // 128):
            evs.append(P.op("gpsimd", lambda h, r=r: h.dma_start(
                out[r * 128:(r + 1) * 128, :], z[:]), waits=[ev], inc="d2", inc_k=16))
        P.wait("gpsimd", [evs[-1]])
        P.barrier()


def emit_scratch_dump(nc, P, dst, src, rows, cols, dt):
    with contextlib.ExitStack() as ps:
        buf = ps.enter_context(nc.sbuf_tensor(_nm("dump_buf"), [128, cols], dt))
        ev_st = None
        for r in range(rows // 128):
            ev_ld = P.op("sync", lambda h, r=r: h.dma_start(
                buf[:], src[r * 128:(r + 1) * 128, :]), waits=[ev_st], inc="d1", inc_k=16)
            ev_st = P.op("sync", lambda h, r=r: h.dma_start(
                dst[r * 128:(r + 1) * 128, :], buf[:]), waits=[ev_ld], inc="d1", inc_k=16)
        P.wait("sync", [ev_st])
    P.barrier()


def emit_wprep_hilo(nc, P, ps, w_ap, out_hi, out_lo, rows, g, scale_col):
    PTR = [ps.enter_context(nc.psum_tensor(_nm("ptrf"), [128, 512], F32)) for i in range(2)]
    """Stream w [rows, D] fp32 column-slab-wise; PE-transpose; optionally scale
    per-partition by scale_col[:, dc]; split to bf16 hi/lo; store [D, rows].
    raw is ping-ponged and each slab load is split over parallel DMA queues so
    the next load overlaps this slab's transposes/stores."""
    nec = rows // 128
    raw = [ps.enter_context(nc.sbuf_tensor(_nm("wp_raw"), [128, rows], F32))
           for _ in range(2)]
    hi = ps.enter_context(nc.sbuf_tensor(_nm("wp_hi"), [128, rows], BF16))
    lo = ps.enter_context(nc.sbuf_tensor(_nm("wp_lo"), [128, rows], BF16))
    wsc = ps.enter_context(nc.sbuf_tensor(_nm("wp_sc"), [128, 512], F32))
    ev_out = []
    tg = {}
    gno = [0]
    tr_last = {}
    nsp = 4 if nec >= 8 else 2
    for dc in range(NDC):
        rb = raw[dc % 2]
        st_w = list(ev_out)
        ev_lds = []
        for k in range(nsp):
            a0, a1 = k * nec // nsp, (k + 1) * nec // nsp
            src = w_ap[a0 * 128:a1 * 128, dc * 128:(dc + 1) * 128].rearrange(
                "(a p) d -> p a d", p=128)
            dst = rb[:, a0 * 128:a1 * 128].rearrange("p (a d) -> p a d", d=128)
            ev_lds.append(P.op("sync", lambda h, src=src, dst=dst: h.dma_start(dst, src),
                               waits=[tr_last.get(dc - 2)], inc="d1", inc_k=16))
        ngr = (nec + 3) // 4
        ev_sp = None
        for gi in range(ngr):
            n_in = min(4, nec - gi * 4)
            bank = PTR[gno[0] % 2]

            def tr(h, gi=gi, n_in=n_in, bank=bank, rb=rb):
                ins = None
                for k in range(n_in):
                    ec = gi * 4 + k
                    ins = h.transpose(bank[:, k * 128:(k + 1) * 128],
                                      rb[:, ec * 128:(ec + 1) * 128], g["identf"][:])
                return ins
            ev_t = P.op("tensor", tr, waits=ev_lds + [tg.get(gno[0] - 2)], inc="pe")
            tr_last[dc] = ev_t
            w = n_in * 128
            colsl = slice(gi * 512, gi * 512 + w)
            if scale_col is not None:
                ev_s = P.op("vector", lambda h, bank=bank, w=w, dc=dc: h.tensor_scalar(
                    wsc[:, 0:w], bank[:, 0:w], scale_col[:, dc:dc + 1], None, ALU.mult),
                    waits=[ev_t], inc="ve", chain="wprep_sc")
                ev_hi = P.op("scalar", lambda h, w=w, colsl=colsl: h.copy(
                    hi[:, colsl], wsc[:, 0:w]), waits=[ev_s] + st_w, inc="se")
                ev_lo = P.op("vector", lambda h, w=w, colsl=colsl: h.tensor_sub(
                    lo[:, colsl], wsc[:, 0:w], hi[:, colsl]), waits=[ev_hi], inc="ve",
                    chain="wprep_sc")
            else:
                ev_hi = P.op("scalar", lambda h, bank=bank, w=w, colsl=colsl: h.copy(
                    hi[:, colsl], bank[:, 0:w]), waits=[ev_t] + st_w, inc="se")
                ev_lo = P.op("vector", lambda h, bank=bank, w=w, colsl=colsl: h.tensor_sub(
                    lo[:, colsl], bank[:, 0:w], hi[:, colsl]), waits=[ev_hi], inc="ve")
            tg[gno[0]] = ev_lo
            gno[0] += 1
            ev_sp = ev_lo
        ev_o1 = P.op("gpsimd", lambda h, dc=dc: h.dma_start(
            out_hi[dc * 128:(dc + 1) * 128, :], hi[:, 0:rows]), waits=[ev_sp], inc="d2", inc_k=16)
        ev_o2 = P.op("gpsimd", lambda h, dc=dc: h.dma_start(
            out_lo[dc * 128:(dc + 1) * 128, :], lo[:, 0:rows]), waits=[ev_sp], inc="d2", inc_k=16)
        ev_out = [ev_o1, ev_o2]
    P.wait("gpsimd", ev_out)


def emit_ffn_means(nc, P, ps, w_gu, w_dn, g):
    PMISC = ps.enter_context(nc.psum_tensor(_nm("pmisc"), [128, 512], F32))
    """mean|w| for gu and dn -> wconst cols (thr, -thr, m)."""
    slabs = [ps.enter_context(nc.sbuf_tensor(f"mn_slab{j}", [128, 2048], F32))
             for j in range(4)]
    parts = ps.enter_context(nc.sbuf_tensor("mn_parts", [128, 192], F32))
    rowb = ps.enter_context(nc.sbuf_tensor("mn_rows", [1, 256], F32))
    acc = ps.enter_context(nc.sbuf_tensor("mn_acc", [1, 512], F32))
    ch1 = ps.enter_context(nc.sbuf_tensor("mn_c1", [1, 512], F32))
    ch2 = ps.enter_context(nc.sbuf_tensor("mn_c2", [1, 512], F32))
    P.mset("vector", parts[:], 0.0)
    P.mset("vector", acc[:], 1.0)
    P.mset("vector", rowb[:], 0.0)

    specs = [(w_gu, GU * D, 0), (w_dn, D * FF, 1)]
    ev_reds = {}
    col = 0
    idx = 0
    cols_used = [[], []]
    for w_ap, total, mi in specs:
        flat = w_ap.rearrange("a b -> (a b)")
        n_sl = total // (128 * 2048)
        for i in range(n_sl):
            src = flat[i * 128 * 2048:(i + 1) * 128 * 2048].rearrange("(p m) -> p m", p=128)
            j = idx % 4
            ev_ld = P.op("sync", lambda h, src=src, j=j: h.dma_start(slabs[j][:], src),
                         waits=[ev_reds.get(idx - 4)], inc="d1", inc_k=16)
            ev_reds[idx] = P.op("vector", lambda h, col=col, j=j: h.reduce_sum(
                parts[:, col:col + 1], slabs[j][:], axis=AX.X, apply_absolute_value=True),
                waits=[ev_ld], inc="ve")
            cols_used[mi].append(col)
            col += 1
            idx += 1
    ev_red = ev_reds[idx - 1]
    ev_p = []
    for mi in range(2):
        c0, c1 = cols_used[mi][0], cols_used[mi][-1] + 1
        ev_p.append(P.op("vector", lambda h, mi=mi, c0=c0, c1=c1: h.reduce_sum(
            parts[:, 184 + mi:185 + mi], parts[:, c0:c1], axis=AX.X), waits=[ev_red], inc="ve"))
    ev_t, ev_r = [], []
    for mi in range(2):
        ev_t.append(P.op("tensor", lambda h, mi=mi: h.transpose(
            PMISC[0:1, mi * 128:(mi + 1) * 128],
            parts[:, 184 + mi:185 + mi], g["identf"][:]),
            waits=[ev_p[mi]] + (ev_r[-1:] if ev_r else []), inc="pe"))
        ev_r.append(P.op("vector", lambda h, mi=mi: h.tensor_copy(
            rowb[0:1, mi * 128:(mi + 1) * 128], PMISC[0:1, mi * 128:(mi + 1) * 128]),
            waits=[ev_t[mi]], inc="ve"))
    ev_s0 = P.op("vector", lambda h: h.reduce_sum(
        acc[0:1, 0:1], rowb[0:1, 0:128], axis=AX.X), waits=ev_r, inc="ve")
    ev_s1 = P.op("vector", lambda h: h.reduce_sum(
        acc[0:1, 256:257], rowb[0:1, 128:256], axis=AX.X), inc="ve")
    P.mset("vector", ch1[:], 1.0)
    P.mset("vector", ch2[:], 1.0)
    Ninv = [1.0 / (GU * D), 1.0 / (D * FF)]
    ev = P.op("vector", lambda h: h.tensor_scalar(
        ch1[0:1, 0:256], acc[0:1, 0:256], Ninv[0], None, ALU.mult),
        waits=[ev_s0, ev_s1], inc="ve")
    ev = P.op("vector", lambda h: h.tensor_scalar(
        ch1[0:1, 256:512], acc[0:1, 256:512], Ninv[1], None, ALU.mult), waits=[ev], inc="ve")
    ev = P.op("vector", lambda h: h.tensor_single_scalar(
        ch2[:], ch1[:], 1e-5, ALU.max), waits=[ev], inc="ve")
    ev = P.op("vector", lambda h: h.tensor_scalar(
        ch1[:], ch2[:], 0.5, None, ALU.mult), waits=[ev], inc="ve")
    items = [(0, ch1, 0), (2, ch2, 0), (3, ch1, 256), (5, ch2, 256)]
    ev_b = []
    for k, (colw, srcbuf, sc) in enumerate(items):
        evm = P.op("tensor", lambda h, k=k, srcbuf=srcbuf, sc=sc: h.matmul(
            PMISC[:, 256 + k:257 + k], g["ones_row"][:], srcbuf[0:1, sc:sc + 1],
            start=True, stop=True),
            waits=[ev] + (ev_b[-1:] if ev_b else []), inc="pe")
        ev_b.append(P.op("vector", lambda h, k=k, colw=colw: h.tensor_copy(
            g["wconst"][:, colw:colw + 1], PMISC[:, 256 + k:257 + k]), waits=[evm], inc="ve"))
    ev = P.op("vector", lambda h: h.tensor_scalar(
        g["wconst"][:, 1:2], g["wconst"][:, 0:1], -1.0, None, ALU.mult), waits=ev_b, inc="ve")
    P.op("vector", lambda h: h.tensor_scalar(
        g["wconst"][:, 4:5], g["wconst"][:, 3:4], -1.0, None, ALU.mult), waits=[ev], inc="ve")


def emit_wprep_quant(nc, P, ps, w_ap, out_q, rows, thr_pos, thr_neg, g, cols):
    PTR = [ps.enter_context(nc.psum_tensor(_nm("ptrb"), [128, 1024], BF16)) for i in range(2)]
    """wq' = sign(w-thr)+sign(w+thr) (natural layout) then bf16 transpose to
    out_q [cols, rows]."""
    nec = rows // 128
    ncc = cols // 128
    raw = [ps.enter_context(nc.sbuf_tensor(_nm("wq_raw"), [128, rows], F32))
           for _ in range(2)]
    sa = ps.enter_context(nc.sbuf_tensor(_nm("wq_sa"), [128, rows], BF16))
    sb2 = ps.enter_context(nc.sbuf_tensor(_nm("wq_sb"), [128, rows], BF16))
    wq = ps.enter_context(nc.sbuf_tensor(_nm("wq_wq"), [128, rows], BF16))
    wqT = ps.enter_context(nc.sbuf_tensor(_nm("wq_wqT"), [128, rows], BF16))
    ev_out = None
    sign_done = {}
    tr_done = {}
    qtg = {}
    qgno = [0]
    nsp = 4 if nec >= 8 else 2
    for dc in range(ncc):
        rb = raw[dc % 2]
        st_prev = ev_out
        ev_lds = []
        for k in range(nsp):
            a0, a1 = k * nec // nsp, (k + 1) * nec // nsp
            src = w_ap[a0 * 128:a1 * 128, dc * 128:(dc + 1) * 128].rearrange(
                "(a p) d -> p a d", p=128)
            dst = rb[:, a0 * 128:a1 * 128].rearrange("p (a d) -> p a d", d=128)
            ev_lds.append(P.op("sync", lambda h, src=src, dst=dst: h.dma_start(dst, src),
                               waits=[sign_done.get(dc - 2)], inc="d1", inc_k=16))
        ev_a = P.op("scalar", lambda h, rb=rb: h.activation(
            sa[:, 0:rows], rb[:, 0:rows], AF.Sign, bias=thr_neg),
            waits=ev_lds + [sign_done.get(dc - 1)], inc="se")
        ev_b = P.op("scalar", lambda h, rb=rb: h.activation(
            sb2[:, 0:rows], rb[:, 0:rows], AF.Sign, bias=thr_pos),
            waits=[ev_a], inc="se")
        ev_w = P.op("vector", lambda h: h.tensor_add(
            wq[:, 0:rows], sa[:, 0:rows], sb2[:, 0:rows]),
            waits=[ev_a, ev_b, tr_done.get(dc - 1)], inc="ve")
        sign_done[dc] = ev_w
        ngr = (nec + 7) // 8
        ev_last = None
        for gi in range(ngr):
            n_in = min(8, nec - gi * 8)
            bank = PTR[(qgno[0] + gi) % 2][:]

            def tr(h, gi=gi, n_in=n_in, bank=bank):
                ins = None
                for k in range(n_in):
                    ec = gi * 8 + k
                    ins = h.transpose(bank[:, k * 128:(k + 1) * 128],
                                      wq[:, ec * 128:(ec + 1) * 128], g["identb"][:])
                return ins
            ev_t = P.op("tensor", tr, waits=[ev_w, qtg.get(qgno[0] + gi - 2)], inc="pe")
            tr_done[dc] = ev_t
            w = n_in * 128
            ev_e = P.op("vector", lambda h, gi=gi, w=w, bank=bank: h.tensor_copy(
                wqT[:, gi * 1024:gi * 1024 + w], bank[:, 0:w]),
                waits=[ev_t, st_prev], inc="ve")
            qtg[qgno[0] + gi] = ev_e
            ev_last = ev_e
        qgno[0] += ngr
        ev_out = P.op("gpsimd", lambda h, dc=dc: h.dma_start(
            out_q[dc * 128:(dc + 1) * 128, :], wqT[:, 0:rows]), waits=[ev_last], inc="d2", inc_k=16)
    P.wait("gpsimd", [ev_out])


def emit_ln_chain(P, stats, C_SUM, C_SQS, C_M, C_T, C_RS, waits):
    P.mset("vector", stats[:, C_T:C_T + 64], 1.0)
    P.mset("vector", stats[:, C_T:C_T + 64], 1.0)
    ev = P.op("vector", lambda h: h.tensor_scalar(
        stats[:, C_M:C_M + 64], stats[:, C_SUM:C_SUM + 64], 1.0 / D, None, ALU.mult),
        waits=waits, inc="ve")
    ev = P.op("vector", lambda h: h.tensor_mul(
        stats[:, C_T:C_T + 64], stats[:, C_M:C_M + 64], stats[:, C_M:C_M + 64]),
        waits=[ev], inc="ve")
    ev = P.op("vector", lambda h: h.scalar_tensor_tensor(
        stats[:, C_T:C_T + 64], stats[:, C_SQS:C_SQS + 64], 1.0 / D, stats[:, C_T:C_T + 64],
        ALU.mult, ALU.subtract), waits=[ev], inc="ve")
    ev = P.op("vector", lambda h: h.tensor_scalar(
        stats[:, C_RS:C_RS + 64], stats[:, C_T:C_T + 64], EPS, None, ALU.add),
        waits=[ev], inc="ve")
    ev = P.op("scalar", lambda h: h.sqrt(stats[:, C_T:C_T + 64], stats[:, C_RS:C_RS + 64]),
              waits=[ev], inc="se")
    ev = P.op("vector", lambda h: h.reciprocal(stats[:, C_RS:C_RS + 64], stats[:, C_T:C_T + 64]),
              waits=[ev], inc="ve")
    return ev


def emit_rope(P, buf, base, cs_ap, sn_ap, rtmp, waits, tag="rope"):
    EVN = buf[:, base:base + 1024:2]
    ODD = buf[:, base + 1:base + 1024:2]
    P.op("vector", lambda h: h.tensor_mul(rtmp[:, 0:512], EVN, cs_ap), waits=waits,
         inc="ve", chain=tag)
    P.op("vector", lambda h: h.tensor_mul(rtmp[:, 512:1024], ODD, cs_ap), inc="ve", chain=tag)
    P.op("vector", lambda h: h.scalar_tensor_tensor(
        rtmp[:, 1024:1536], ODD, -1.0, sn_ap, ALU.mult, ALU.mult), inc="ve", chain=tag)
    P.op("vector", lambda h: h.tensor_mul(rtmp[:, 1536:2048], EVN, sn_ap), inc="ve", chain=tag)
    P.op("vector", lambda h: h.tensor_add(EVN, rtmp[:, 0:512], rtmp[:, 1024:1536]),
         inc="ve", chain=tag)
    return P.op("vector", lambda h: h.tensor_add(ODD, rtmp[:, 512:1024], rtmp[:, 1536:2048]),
                inc="ve", chain=tag)


def emit_quant_ladder(P, x_ap, w, sx_col, xq_ap, lad, waits, a_chain=None):
    a, vv, r0, r1, sgn = (lad[k] for k in ("a", "vv", "r0", "r1", "sgn"))
    ev_g = P.op("scalar", lambda h: h.activation(sgn[:, 0:w], x_ap, AF.Sign, scale=sx_col),
                waits=waits, inc="se")
    ev_a = P.op("scalar", lambda h: h.activation(a[:, 0:w], x_ap, AF.Abs, scale=sx_col),
                waits=waits, inc="se", chain=a_chain)
    ev = P.op("vector", lambda h: h.tensor_scalar(
        vv[:, 0:w], a[:, 0:w], 128.0, None, ALU.add), waits=[ev_a], inc="ve")
    ev = P.op("vector", lambda h: h.tensor_scalar(
        xq_ap, vv[:, 0:w], -128.0, None, ALU.add), waits=[ev], inc="ve")
    r = None
    rbuf = [r0, r1]
    for i, k in enumerate([64, 32, 16, 8, 4, 2, 1]):
        rap = xq_ap if r is None else r[:, 0:w]
        nxt = rbuf[i % 2]
        ev = P.op("vector", lambda h, rap=rap, k=k, w=w: h.tensor_scalar(
            vv[:, 0:w], rap, float(k) - 0.5, float(-k), ALU.is_ge, ALU.mult),
            waits=[ev], inc="ve")
        ev = P.op("vector", lambda h, rap=rap, nxt=nxt, w=w: h.tensor_add(
            nxt[:, 0:w], rap, vv[:, 0:w]), waits=[ev], inc="ve")
        r = nxt
    ev = P.op("vector", lambda h, r=r, w=w: h.tensor_sub(
        vv[:, 0:w], xq_ap, r[:, 0:w]), waits=[ev], inc="ve")
    return P.op("vector", lambda h, w=w: h.tensor_mul(
        xq_ap, vv[:, 0:w], sgn[:, 0:w]), waits=[ev, ev_g], inc="ve")


def emit_sweep1(nc, P, ps, inp, scr, g, s1_jobs=None):
    PMM = [ps.enter_context(nc.psum_tensor(_nm("pmm"), [128, 512], F32)) for i in range(4)]
    PTR = [ps.enter_context(nc.psum_tensor(_nm("ptrf"), [128, 512], F32)) for i in range(2)]
    PA = ps.enter_context(nc.psum_tensor(_nm("papart"), [128, 512], F32))
    xT_hi = ps.enter_context(nc.sbuf_tensor("s1_xT_hi", [128, NDC * TS], BF16))
    xT_lo = ps.enter_context(nc.sbuf_tensor("s1_xT_lo", [128, NDC * TS], BF16))
    kst = ps.enter_context(nc.sbuf_tensor("s1_k", [128, NTT * D], F32))
    vst = ps.enter_context(nc.sbuf_tensor("s1_v", [128, NTT * D], F32))
    xln = ps.enter_context(nc.sbuf_tensor("s1_xln", [128, D], F32))
    qev = [ps.enter_context(nc.sbuf_tensor(f"s1_qe{i}", [128, 512], F32)) for i in range(2)]
    sejunk = ps.enter_context(nc.sbuf_tensor("s1_sejunk", [128, D], F32))
    fmkv = ps.enter_context(nc.sbuf_tensor("s1_fmkv", [128, D], F32))
    cs = ps.enter_context(nc.sbuf_tensor("s1_cs", [128, NTT * 512], F32))
    sn = ps.enter_context(nc.sbuf_tensor("s1_sn", [128, NTT * 512], F32))
    rtmp = ps.enter_context(nc.sbuf_tensor("s1_rt", [128, D], F32))
    binb = [ps.enter_context(nc.sbuf_tensor(f"s1_bin{i}", [128, 512], F32)) for i in range(2)]
    wt = [ps.enter_context(nc.sbuf_tensor(f"s1_wt{i}", [128, 512], BF16)) for i in range(4)]
    stats = ps.enter_context(nc.sbuf_tensor("s1_st", [128, 512], F32))
    SUM, SQS, M, TT_, RS, KS, T2, RN = 0, 64, 128, 192, 256, 320, 384, 448
    _ev = P.mset("vector", stats[:], 1.0)
    P.wait("scalar", [_ev])

    n_tiles_total = (NST + NST) * NTT
    tile_no = [0]
    a_rd_prev = [None]
    tg = {}
    gno = [0]
    prev_xln = [None]
    prev_kst_done = [None] * NTT
    prev_cs_done = [None]
    prev_xT_done = [None]
    prev_bias_done = [None, None]
    prev_qe_done = [None, None]
    ev_bank_free = [None] * NTT

    jobs = [("own", s) for s in range(NST)] + [("dup", s) for s in range(NST)]
    if s1_jobs is not None:
        jobs = jobs[:s1_jobs]
    n_tiles_total = len(jobs) * NTT
    for kind, s in jobs:
        x_ap = inp["x"] if kind == "own" else inp["xo"]
        cos_ap = inp["q_cos"] if kind == "own" else inp["o_cos"]
        sin_ap = inp["q_sin"] if kind == "own" else inp["o_sin"]
        oc_list = list(range(12)) if kind == "own" else list(range(4, 12))
        tok0 = s * TS

        ev_x, ev_cs = [], []
        for tt in range(NTT):
            r0 = tok0 + tt * 128
            ev_x.append(P.op("sync", lambda h, tt=tt, r0=r0, x_ap=x_ap: h.dma_start(
                kst[:, tt * D:(tt + 1) * D], x_ap[r0:r0 + 128, :]),
                waits=[prev_kst_done[tt]], inc="d1", inc_k=16))
            ev_cs.append(P.op("sync", lambda h, tt=tt, r0=r0, cos_ap=cos_ap: h.dma_start(
                cs[:, tt * 512:(tt + 1) * 512], cos_ap[r0:r0 + 128, :]),
                waits=[prev_cs_done[0]], inc="d1", inc_k=16))
            ev_cs.append(P.op("sync", lambda h, tt=tt, r0=r0, sin_ap=sin_ap: h.dma_start(
                sn[:, tt * 512:(tt + 1) * 512], sin_ap[r0:r0 + 128, :]), inc="d1", inc_k=16))
        if S1_PART == "dma":
            for tt in range(NTT):
                prev_kst_done[tt] = ev_x[tt]
            continue
        ev_red, ev_sq = [], []
        for tt in range(NTT):
            ev_red.append(P.op("vector", lambda h, tt=tt: h.reduce_sum(
                stats[:, SUM + tt:SUM + tt + 1], kst[:, tt * D:(tt + 1) * D], axis=AX.X),
                waits=[ev_x[tt]], inc="ve"))
            if S1_PART != "nosq":
                ev_sq.append(P.op("scalar", lambda h, tt=tt: h.activation(
                    sejunk[:], kst[:, tt * D:(tt + 1) * D], AF.Square,
                    accum_out=stats[:, SQS + tt:SQS + tt + 1]), waits=[ev_x[tt]], inc="se",
                    chain="s1jnk"))
        if S1_PART == "stats" or S1_PART == "nosq":
            for tt in range(NTT):
                prev_kst_done[tt] = ev_red[tt]
            continue
        ev_ch = emit_ln_chain(P, stats, SUM, SQS, M, TT_, RS, waits=ev_red + ev_sq)

        if S1_PART == "chain":
            for tt in range(NTT):
                prev_kst_done[tt] = ev_red[tt]
            continue
        ev_xT = []
        for tt in range(NTT):
            ev_ap = P.op("vector", lambda h, tt=tt: h.tensor_scalar(
                xln[:], kst[:, tt * D:(tt + 1) * D],
                stats[:, M + tt:M + tt + 1], stats[:, RS + tt:RS + tt + 1],
                ALU.subtract, ALU.mult), waits=[ev_ch, prev_xln[0]], inc="ve")
            if S1_PART == "apply":
                prev_xln[0] = ev_ap
                continue
            ev_last = None
            for gi in range(4):
                bank = PTR[gno[0] % 2]

                def tr(h, gi=gi, bank=bank):
                    ins = None
                    for k in range(4):
                        dc = gi * 4 + k
                        ins = h.transpose(bank[:, k * 128:(k + 1) * 128],
                                          xln[:, dc * 128:(dc + 1) * 128], g["identf"][:])
                    return ins
                wts = [ev_ap, tg.get(gno[0] - 2)]
                if tt == 0 and gi == 0:
                    wts.append(prev_xT_done[0])
                ev_t = P.op("tensor", tr, waits=wts, inc="pe")
                if S1_PART == "trnoev":
                    tg[gno[0]] = ev_t
                    gno[0] += 1
                    prev_xln[0] = ev_t
                    ev_last = ev_t
                    continue
                hi3 = xT_hi[:].rearrange("p (dc t) -> p dc t", t=TS)[
                    :, gi * 4:(gi + 1) * 4, tt * 128:(tt + 1) * 128]
                lo3 = xT_lo[:].rearrange("p (dc t) -> p dc t", t=TS)[
                    :, gi * 4:(gi + 1) * 4, tt * 128:(tt + 1) * 128]
                bk3 = bank[:].rearrange("p (k c) -> p k c", c=128)
                ev_hi = P.op("scalar", lambda h, hi3=hi3, bk3=bk3: h.copy(hi3, bk3),
                             waits=[ev_t], inc="se")
                ev_lo = P.op("vector", lambda h, lo3=lo3, bk3=bk3, hi3=hi3: h.tensor_sub(
                    lo3, bk3, hi3), waits=[ev_hi], inc="ve")
                tg[gno[0]] = ev_lo
                gno[0] += 1
                ev_last = ev_lo
            prev_xln[0] = ev_last
            ev_xT.append(ev_last)
        if S1_PART in ("ln", "apply", "trnoev"):
            for tt in range(NTT):
                prev_kst_done[tt] = ev_red[tt]
            continue

        ev_wu = {}
        wno = [0]
        ev_mm = None
        for oi, oc in enumerate(oc_list):
            bb = oi % 2
            ev_bias = P.op("sync", lambda h, oc=oc, bb=bb: h.dma_start(
                binb[bb][:], inp["b_in_bc"][:, oc * 512:(oc + 1) * 512]),
                waits=[prev_bias_done[bb]], inc="d1", inc_k=16)
            for dc in range(NDC):
                wi = wno[0]
                ridx = (wi % 2) * 2
                ev_w1 = P.op("sync", lambda h, oc=oc, dc=dc, ridx=ridx: h.dma_start(
                    wt[ridx][:], scr["winT_hi"][dc * 128:(dc + 1) * 128, oc * 512:(oc + 1) * 512]),
                    waits=[ev_wu.get(wi - 2)], inc="d1", inc_k=16)
                ev_w2 = P.op("sync", lambda h, oc=oc, dc=dc, ridx=ridx: h.dma_start(
                    wt[ridx + 1][:], scr["winT_lo"][dc * 128:(dc + 1) * 128, oc * 512:(oc + 1) * 512]),
                    inc="d1", inc_k=16)

                def mm(h, dc=dc, ridx=ridx):
                    ins = None
                    for tt in range(NTT):
                        stsl = slice(dc * TS + tt * 128, dc * TS + (tt + 1) * 128)
                        ins = h.matmul(PMM[tt][:], xT_hi[:, stsl], wt[ridx][:],
                                       start=(dc == 0), stop=False)
                        ins = h.matmul(PMM[tt][:], xT_hi[:, stsl], wt[ridx + 1][:],
                                       start=False, stop=False)
                        ins = h.matmul(PMM[tt][:], xT_lo[:, stsl], wt[ridx][:],
                                       start=False, stop=(dc == NDC - 1))
                    return ins
                wts = [ev_w1, ev_w2]
                if dc == 0:
                    wts += ev_xT + ev_bank_free
                ev_mm = P.op("tensor", mm, waits=wts, inc="pe")
                ev_wu[wi] = ev_mm
                wno[0] += 1
            for tt in range(NTT):
                if oc < 4:
                    qb_i = tt % 2
                    ev_e = P.op("vector", lambda h, tt=tt, qb_i=qb_i, bb=bb: h.tensor_add(
                        qev[qb_i][:], PMM[tt][:], binb[bb][:]),
                        waits=[ev_mm, ev_bias, prev_qe_done[qb_i]], inc="ve")
                    r0 = tok0 + tt * 128
                    ev_sp = P.op("gpsimd", lambda h, r0=r0, oc=oc, qb_i=qb_i: h.dma_start(
                        scr["q_sp"][r0:r0 + 128, oc * 512:(oc + 1) * 512], qev[qb_i][:]),
                        waits=[ev_e], inc="d2", inc_k=16)
                    prev_qe_done[qb_i] = ev_sp
                    ev_bank_free[tt] = ev_e
                elif oc < 8:
                    ev_e = P.op("vector", lambda h, tt=tt, oc=oc, bb=bb: h.tensor_add(
                        kst[:, tt * D + (oc - 4) * 512: tt * D + (oc - 3) * 512],
                        PMM[tt][:], binb[bb][:]), waits=[ev_mm, ev_bias], inc="ve")
                    ev_bank_free[tt] = ev_e
                else:
                    ev_e = P.op("vector", lambda h, tt=tt, oc=oc, bb=bb: h.tensor_add(
                        vst[:, tt * D + (oc - 8) * 512: tt * D + (oc - 7) * 512],
                        PMM[tt][:], binb[bb][:]), waits=[ev_mm, ev_bias], inc="ve")
                    ev_bank_free[tt] = ev_e
            prev_bias_done[bb] = ev_bank_free[NTT - 1]
        prev_xT_done[0] = ev_mm
        if S1_PART == "mm":
            for tt in range(NTT):
                prev_kst_done[tt] = ev_bank_free[tt]
            continue

        ev_ks = []
        ev_rope_last = None
        for tt in range(NTT):
            base = tt * D
            ev_r = emit_rope(P, kst, base, cs[:, tt * 512:(tt + 1) * 512],
                             sn[:, tt * 512:(tt + 1) * 512], rtmp,
                             waits=[ev_bank_free[tt]] + ev_cs)
            ev_rope_last = ev_r
            ev_ks.append(P.op("scalar", lambda h, base=base, tt=tt: h.activation(
                sejunk[:], kst[:, base:base + D], AF.Square,
                accum_out=stats[:, KS + tt:KS + tt + 1]), waits=[ev_r], inc="se",
                chain="s1jnk"))
        prev_cs_done[0] = ev_rope_last
        P.mset("vector", stats[:, T2:T2 + 64], 1.0)
        ev_msk = P.mset("vector", stats[:, T2:T2 + 64], 1.0)
        ev_sd = P.op("scalar", lambda h: h.sqrt(
            stats[:, T2:T2 + 64], stats[:, KS:KS + 64]), waits=ev_ks + [ev_msk], inc="se")
        ev_cl = P.op("vector", lambda h: h.tensor_single_scalar(
            stats[:, TT_:TT_ + 64], stats[:, T2:T2 + 64], 1e-12, ALU.max),
            waits=[ev_sd], inc="ve")
        ev_rn = P.op("vector", lambda h: h.reciprocal(
            stats[:, RN:RN + 64], stats[:, TT_:TT_ + 64]), waits=[ev_cl], inc="ve")
        if S1_PART == "rope":
            for tt in range(NTT):
                prev_kst_done[tt] = ev_ks[tt]
            continue
        ev_am_prev = None
        for tt in range(NTT):
            ksl = slice(tt * D, (tt + 1) * D)
            ev_f = P.op("vector", lambda h, ksl=ksl, tt=tt: h.scalar_tensor_tensor(
                fmkv[:], kst[:, ksl], stats[:, RN + tt:RN + tt + 1], vst[:, ksl],
                ALU.mult, ALU.mult), waits=[ev_rn, ev_am_prev], inc="ve")
            tile_no[0] += 1
            tn = tile_no[0]

            def amm(h):
                ins = None
                for dc in range(NDC):
                    ins = h.matmul(PA[:, dc:dc + 1], fmkv[:, dc * 128:(dc + 1) * 128],
                                   g["ones_col"][:, 0:1],
                                   start=(dc == 0), stop=(dc == NDC - 1),
                                   skip_group_check=True)
                return ins
            ev_am = P.op("tensor", amm, waits=[ev_f, a_rd_prev[0]], inc="pe")
            ev_aa = P.op("vector", lambda h, tn=tn: h.tensor_add(
                g["A_acc"][tn % 2][:, 0:16], g["A_acc"][(tn - 1) % 2][:, 0:16],
                PA[:, 0:16]), waits=[ev_am], inc="ve")
            a_rd_prev[0] = ev_aa
            ev_am_prev = ev_am
            prev_kst_done[tt] = ev_am


def emit_sweep2(nc, P, ps, inp, scr, out, g):
    PMM = [ps.enter_context(nc.psum_tensor(_nm("pmm"), [128, 512], F32)) for i in range(4)]
    PTR = [ps.enter_context(nc.psum_tensor(_nm("ptrf"), [128, 512], F32)) for i in range(2)]
    PTB = [ps.enter_context(nc.psum_tensor(_nm("ptrb"), [128, 1024], BF16)) for i in range(2)]
    xst = ps.enter_context(nc.sbuf_tensor("s2_xst", [128, NTT * D], F32))
    att = ps.enter_context(nc.sbuf_tensor("s2_att", [128, NTT * D], F32))
    tmpA = ps.enter_context(nc.sbuf_tensor("s2_tmpA", [128, D], F32))
    tmpB = ps.enter_context(nc.sbuf_tensor("s2_tmpB", [128, D], F32))
    ghold = ps.enter_context(nc.sbuf_tensor("s2_gh", [128, NTT * 512], F32))
    uh = ps.enter_context(nc.sbuf_tensor("s2_uh", [128, 512], F32))
    cs = [ps.enter_context(nc.sbuf_tensor("s2_cs0", [128, 512], F32))]
    sn = [ps.enter_context(nc.sbuf_tensor("s2_sn0", [128, 512], F32))]
    bout = [ps.enter_context(nc.sbuf_tensor(f"s2_bo{i}", [128, 512], F32)) for i in range(2)]
    nwb = ps.enter_context(nc.sbuf_tensor("s2_nw", [128, D], F32))
    stats = ps.enter_context(nc.sbuf_tensor("s2_st", [128, 768], F32))
    qT_hi = ps.enter_context(nc.sbuf_tensor("s2_qTh", [128, NDC * TS], BF16))
    qT_lo = ps.enter_context(nc.sbuf_tensor("s2_qTl", [128, NDC * TS], BF16))
    hqA = ps.enter_context(nc.sbuf_tensor("s2_hqA", [128, 32 * TS], BF16))
    hq = ps.enter_context(nc.sbuf_tensor("s2_hq", [128, FF], BF16))
    vvb = ps.enter_context(nc.sbuf_tensor("s2_vv", [128, D], BF16))
    sgnb = ps.enter_context(nc.sbuf_tensor("s2_sgn", [128, D], BF16))
    wt = [ps.enter_context(nc.sbuf_tensor(f"s2_wt{i}", [128, 512], BF16)) for i in range(4)]

    lr0 = ps.enter_context(nc.sbuf_tensor("s2_lr0", [128, D], BF16))
    lr1 = ps.enter_context(nc.sbuf_tensor("s2_lr1", [128, D], BF16))
    sejunk = tmpB[:]
    gx4 = ghold[:, 0:D]
    lad = {"a": tmpB, "vv": vvb, "r0": lr0, "r1": lr1, "sgn": sgnb}

    def hqT_ap(fc, c0, c1):
        if fc < 32:
            base = fc * TS
            return hqA[:, base + c0:base + c1]
        base = (fc - 32) * TS
        return qT_lo[:, base + c0:base + c1]

    ffn = xst

    SUM1, SQS1, M1, T1_, RS1 = 0, 64, 128, 192, 256
    SUM2, SQS2, M2, RS2 = 320, 384, 448, 512
    AMX, SX, SCG = 576, 608, 640
    HM = 704
    _ev1 = P.mset("vector", stats[:, 0:512], 1.0)
    _ev2 = P.mset("vector", stats[:, 512:768], 1.0)
    P.wait("scalar", [_ev1, _ev2])
    ev_nw = P.op("sync", lambda h: h.dma_start(nwb[:], inp["nw_bc"]), inc="d1", inc_k=16)

    prev_stage_done = [None] * NTT
    prev_cs_done = [None]
    prev_qT_done = [None]
    prev_bo_done = [None, None]
    prev_we2 = [None] * NTT
    ev_bank_free = [None] * NTT
    ev_wu = {}
    wno = [0]
    tg = {}
    gno = [0]

    def wload(src_ap, r0, c0, waits=()):
        wi = wno[0]
        ridx = wi % 4
        ev = P.op("sync", lambda h, ridx=ridx: h.dma_start(
            wt[ridx][:], src_ap[r0:r0 + 128, c0:c0 + 512]),
            waits=[ev_wu.get(wi - 4)] + list(waits), inc="d1", inc_k=16)
        wno[0] += 1
        return wi, ridx, ev

    for s in range(NST):
        tok0 = s * TS
        ev_x, ev_q = [], []
        for tt in range(NTT):
            r0 = tok0 + tt * 128
            ev_x.append(P.op("sync", lambda h, tt=tt, r0=r0: h.dma_start(
                xst[:, tt * D:(tt + 1) * D], inp["x"][r0:r0 + 128, :]),
                waits=[prev_stage_done[tt]], inc="d1", inc_k=16))
            ev_q.append(P.op("sync", lambda h, tt=tt, r0=r0: h.dma_start(
                att[:, tt * D:(tt + 1) * D], scr["q_sp"][r0:r0 + 128, :]),
                waits=[prev_stage_done[tt]], inc="d1", inc_k=16))
        ev_red, ev_sq = [], []
        for tt in range(NTT):
            ev_red.append(P.op("vector", lambda h, tt=tt: h.reduce_sum(
                stats[:, SUM1 + tt:SUM1 + tt + 1], xst[:, tt * D:(tt + 1) * D], axis=AX.X),
                waits=[ev_x[tt]], inc="ve"))
            ev_sq.append(P.op("scalar", lambda h, tt=tt: h.activation(
                sejunk, xst[:, tt * D:(tt + 1) * D], AF.Square,
                accum_out=stats[:, SQS1 + tt:SQS1 + tt + 1]), waits=[ev_x[tt]], inc="se",
                chain="s2_tmpB"))
        ev_ch1 = emit_ln_chain(P, stats, SUM1, SQS1, M1, T1_, RS1, waits=ev_red + ev_sq)
        ev_qT = []
        for tt in range(NTT):
            base = tt * D
            pp = 0
            r0 = tok0 + tt * 128
            evc = P.op("sync", lambda h, pp=pp, r0=r0: h.dma_start(
                cs[pp][:], inp["q_cos"][r0:r0 + 128, :]),
                waits=[prev_cs_done[pp]], inc="d1", inc_k=16)
            evs_ = P.op("sync", lambda h, pp=pp, r0=r0: h.dma_start(
                sn[pp][:], inp["q_sin"][r0:r0 + 128, :]), inc="d1", inc_k=16)
            ev_r = emit_rope(P, att, base, cs[pp][:], sn[pp][:], tmpA,
                             waits=[ev_q[tt], evc, evs_], tag="s2_tmpA")
            prev_cs_done[pp] = ev_r
            ev_last = None
            for gi in range(4):
                bank = PTR[gno[0] % 2]

                def tr(h, gi=gi, bank=bank, base=base):
                    ins = None
                    for k in range(4):
                        dc = gi * 4 + k
                        ins = h.transpose(bank[:, k * 128:(k + 1) * 128],
                                          att[:, base + dc * 128:base + (dc + 1) * 128],
                                          g["identf"][:])
                    return ins
                ev_t = P.op("tensor", tr, waits=[ev_r, tg.get(gno[0] - 2)], inc="pe")
                hi3 = qT_hi[:].rearrange("p (dc t) -> p dc t", t=TS)[
                    :, gi * 4:(gi + 1) * 4, tt * 128:(tt + 1) * 128]
                lo3 = qT_lo[:].rearrange("p (dc t) -> p dc t", t=TS)[
                    :, gi * 4:(gi + 1) * 4, tt * 128:(tt + 1) * 128]
                bk3 = bank[:].rearrange("p (k c) -> p k c", c=128)
                ev_hi = P.op("scalar", lambda h, hi3=hi3, bk3=bk3: h.copy(hi3, bk3),
                             waits=[ev_t], inc="se")
                ev_lo = P.op("vector", lambda h, lo3=lo3, bk3=bk3, hi3=hi3: h.tensor_sub(
                    lo3, bk3, hi3), waits=[ev_hi], inc="ve")
                tg[gno[0]] = ev_lo
                gno[0] += 1
                ev_last = ev_lo
            ev_qT.append(ev_last)
        ev_mm = None
        for oc2 in range(4):
            bb = oc2 % 2
            ev_bo = P.op("sync", lambda h, oc2=oc2, bb=bb: h.dma_start(
                bout[bb][:], inp["b_out_bc"][:, oc2 * 512:(oc2 + 1) * 512]),
                waits=[prev_bo_done[bb]], inc="d1", inc_k=16)
            for dc in range(NDC):
                wi1, r1_, ev_w1 = wload(scr["woutT_hi"], dc * 128, oc2 * 512)
                wi2, r2_, ev_w2 = wload(scr["woutT_lo"], dc * 128, oc2 * 512)

                def mm(h, dc=dc, r1_=r1_, r2_=r2_):
                    ins = None
                    for tt in range(NTT):
                        stsl = slice(dc * TS + tt * 128, dc * TS + (tt + 1) * 128)
                        ins = h.matmul(PMM[tt][:], qT_hi[:, stsl], wt[r1_][:],
                                       start=(dc == 0), stop=False)
                        ins = h.matmul(PMM[tt][:], qT_hi[:, stsl], wt[r2_][:],
                                       start=False, stop=False)
                        ins = h.matmul(PMM[tt][:], qT_lo[:, stsl], wt[r1_][:],
                                       start=False, stop=(dc == NDC - 1))
                    return ins
                wts = [ev_w1, ev_w2]
                if dc == 0:
                    wts += ev_qT + ev_bank_free
                ev_mm = P.op("tensor", mm, waits=wts, inc="pe")
                ev_wu[wi1] = ev_mm
                ev_wu[wi2] = ev_mm
            ev_e1_last = None
            for tt in range(NTT):
                csl = slice(oc2 * 512, (oc2 + 1) * 512)
                ev_e1 = P.op("vector", lambda h, tt=tt, bb=bb: h.tensor_add(
                    ghold[:, tt * 512:(tt + 1) * 512], PMM[tt][:], bout[bb][:]),
                    waits=[ev_mm, ev_bo, prev_we2[tt]], inc="ve")
                ev_t1 = P.op("vector", lambda h, tt=tt, csl=csl: h.tensor_scalar(
                    tmpA[:, tt * 512:(tt + 1) * 512],
                    xst[:, tt * D + csl.start:tt * D + csl.stop],
                    stats[:, M1 + tt:M1 + tt + 1], stats[:, RS1 + tt:RS1 + tt + 1],
                    ALU.subtract, ALU.mult),
                    waits=[ev_ch1, prev_we2[tt] if prev_we2[tt] is not None
                           else P.chains.get("s2_tmpA")], inc="ve")
                ev_e2 = P.op("vector", lambda h, tt=tt, csl=csl: h.tensor_add(
                    att[:, tt * D + csl.start:tt * D + csl.stop],
                    ghold[:, tt * 512:(tt + 1) * 512],
                    tmpA[:, tt * 512:(tt + 1) * 512]),
                    waits=[ev_e1, ev_t1], inc="ve")
                prev_we2[tt] = ev_e2
                ev_bank_free[tt] = ev_e1
                ev_e1_last = ev_e1
            prev_bo_done[bb] = ev_e1_last
        prev_qT_done[0] = ev_mm
        P.chains["s2_tmpA"] = prev_we2[NTT - 1]
        ev_red2, ev_sq2 = [], []
        for tt in range(NTT):
            ev_red2.append(P.op("vector", lambda h, tt=tt: h.reduce_sum(
                stats[:, SUM2 + tt:SUM2 + tt + 1], att[:, tt * D:(tt + 1) * D], axis=AX.X),
                waits=[prev_we2[tt]], inc="ve"))
            ev_sq2.append(P.op("scalar", lambda h, tt=tt: h.activation(
                sejunk, att[:, tt * D:(tt + 1) * D], AF.Square,
                accum_out=stats[:, SQS2 + tt:SQS2 + tt + 1]),
                waits=[prev_we2[tt]], inc="se", chain="s2_tmpB"))
        ev_ch2 = emit_ln_chain(P, stats, SUM2, SQS2, M2, T1_, RS2, waits=ev_red2 + ev_sq2)
        ev_x2 = []
        ev_sq3 = []
        for tt in range(NTT):
            tsl = slice(tt * D, (tt + 1) * D)
            ev_a = P.op("vector", lambda h, tsl=tsl, tt=tt: h.tensor_scalar(
                tmpA[:], att[:, tsl], stats[:, M2 + tt:M2 + tt + 1],
                stats[:, RS2 + tt:RS2 + tt + 1], ALU.subtract, ALU.mult),
                waits=[ev_ch2], inc="ve", chain="s2_tmpA")
            ev_b = P.op("vector", lambda h, tsl=tsl, tt=tt: h.scalar_tensor_tensor(
                att[:, tsl], tmpA[:], 1.0, xst[:, tsl], ALU.mult, ALU.add,
                accum_out=stats[:, SUM1 + tt:SUM1 + tt + 1]), waits=[ev_a], inc="ve",
                chain="s2_tmpA")
            ev_x2.append(ev_b)
            ev_sq3.append(P.op("scalar", lambda h, tsl=tsl, tt=tt: h.activation(
                sejunk, att[:, tsl], AF.Square,
                accum_out=stats[:, SQS1 + tt:SQS1 + tt + 1]), waits=[ev_b], inc="se",
                chain="s2_tmpB"))
        ev_ch3 = emit_ln_chain(P, stats, SUM1, SQS1, M1, T1_, RS1, waits=ev_x2 + ev_sq3)
        ev = P.op("vector", lambda h: h.tensor_mul(
            stats[:, T1_:T1_ + 64], stats[:, M1:M1 + 64], stats[:, M1:M1 + 64]),
            waits=[ev_ch3], inc="ve")
        ev = P.op("vector", lambda h: h.scalar_tensor_tensor(
            stats[:, T1_:T1_ + 64], stats[:, SQS1:SQS1 + 64], 1.0 / D, stats[:, T1_:T1_ + 64],
            ALU.mult, ALU.subtract), waits=[ev], inc="ve")
        ev = P.op("vector", lambda h: h.tensor_mul(
            stats[:, SUM2:SUM2 + 64], stats[:, RS1:RS1 + 64], stats[:, RS1:RS1 + 64]),
            waits=[ev], inc="ve")
        ev = P.op("vector", lambda h: h.tensor_mul(
            stats[:, T1_:T1_ + 64], stats[:, T1_:T1_ + 64], stats[:, SUM2:SUM2 + 64]),
            waits=[ev], inc="ve")
        ev = P.op("vector", lambda h: h.tensor_scalar(
            stats[:, SQS2:SQS2 + 64], stats[:, T1_:T1_ + 64], EPS, None, ALU.add),
            waits=[ev], inc="ve")
        ev = P.op("scalar", lambda h: h.sqrt(
            stats[:, T1_:T1_ + 64], stats[:, SQS2:SQS2 + 64]), waits=[ev], inc="se")
        ev = P.op("vector", lambda h: h.reciprocal(
            stats[:, SQS2:SQS2 + 64], stats[:, T1_:T1_ + 64]), waits=[ev], inc="ve")
        ev_c3 = P.op("vector", lambda h: h.tensor_mul(
            stats[:, RS2:RS2 + 64], stats[:, RS1:RS1 + 64], stats[:, SQS2:SQS2 + 64]),
            waits=[ev], inc="ve")
        ev_xqT = []
        ev_lad_prev = None
        for tt in range(NTT):
            tsl = slice(tt * D, (tt + 1) * D)
            ev_p1 = P.op("vector", lambda h, tsl=tsl, tt=tt: h.tensor_scalar(
                tmpA[:], att[:, tsl], stats[:, M1 + tt:M1 + tt + 1],
                stats[:, RS2 + tt:RS2 + tt + 1], ALU.subtract, ALU.mult),
                waits=[ev_c3, ev_lad_prev], inc="ve", chain="s2_tmpA")
            ev_x4 = P.op("vector", lambda h: h.tensor_mul(
                gx4, tmpA[:], nwb[:]), waits=[ev_p1, ev_nw], inc="ve", chain="s2_tmpA")
            ev_mx = P.op("vector", lambda h, tt=tt: h.reduce_max(
                stats[:, AMX + tt:AMX + tt + 1], gx4, axis=AX.X,
                apply_absolute_value=True), waits=[ev_x4], inc="ve")
            P.mset("vector", stats[:, T1_:T1_ + 64], 1.0)
            ev = P.op("vector", lambda h: h.tensor_single_scalar(
                stats[:, T1_:T1_ + 64], stats[:, AMX:AMX + 64], 1e-5, ALU.max),
                waits=[ev_mx], inc="ve")
            ev = P.op("vector", lambda h: h.reciprocal(
                stats[:, SUM2:SUM2 + 64], stats[:, T1_:T1_ + 64]), waits=[ev], inc="ve")
            ev_sx = P.op("vector", lambda h: h.tensor_scalar(
                stats[:, SX:SX + 64], stats[:, SUM2:SUM2 + 64], 127.0, None, ALU.mult),
                waits=[ev], inc="ve")
            ev_sc = P.op("vector", lambda h: h.tensor_scalar(
                stats[:, SCG:SCG + 64], stats[:, T1_:T1_ + 64], g["wconst"][:, 2:3],
                0.5 / 127.0, ALU.mult, ALU.mult), waits=[ev_sx], inc="ve")
            ev_lq = emit_quant_ladder(P, gx4, D, stats[:, SX + tt:SX + tt + 1],
                                      hq[:, 0:D], lad, waits=[ev_sx, ev_lad_prev],
                                      a_chain="s2_tmpB")
            ev_last = None
            for gi in range(4):
                bank = PTB[gno[0] % 2]

                def tr(h, gi=gi, bank=bank):
                    ins = None
                    for k in range(4):
                        dc = gi * 4 + k
                        ins = h.transpose(bank[:, k * 128:(k + 1) * 128],
                                          hq[:, dc * 128:(dc + 1) * 128], g["identb"][:])
                    return ins
                ev_t = P.op("tensor", tr, waits=[ev_lq, tg.get(gno[0] - 2)], inc="pe")
                dst3 = qT_hi[:].rearrange("p (dc t) -> p dc t", t=TS)[
                    :, gi * 4:(gi + 1) * 4, tt * 128:(tt + 1) * 128]
                bk3 = bank[:].rearrange("p (k c) -> p k c", c=128)[:, 0:4, :]
                ev_e = P.op("vector", lambda h, dst3=dst3, bk3=bk3: h.tensor_copy(dst3, bk3),
                            waits=[ev_t], inc="ve")
                tg[gno[0]] = ev_e
                gno[0] += 1
                ev_last = ev_e
            ev_lad_prev = ev_last
            ev_xqT.append(ev_last)
        ev_h_sp = {}
        prev_gh = [[] for _ in range(NTT)]
        ev_hmax = []
        for j in range(11):
            for phase, oc3 in (("g", j), ("u", 11 + j)):
                for dc in range(NDC):
                    wi1, r1_, ev_w1 = wload(scr["guT_q"], dc * 128, oc3 * 512)

                    def mm(h, dc=dc, r1_=r1_):
                        ins = None
                        for tt in range(NTT):
                            stsl = slice(dc * TS + tt * 128, dc * TS + (tt + 1) * 128)
                            ins = h.matmul(PMM[tt][:], qT_hi[:, stsl], wt[r1_][:],
                                           start=(dc == 0), stop=(dc == NDC - 1))
                        return ins
                    wts = [ev_w1]
                    if dc == 0:
                        wts += ev_xqT + ev_bank_free
                    ev_mm = P.op("tensor", mm, waits=wts, inc="pe")
                    ev_wu[wi1] = ev_mm
                for tt in range(NTT):
                    if phase == "g":
                        ev_e = P.op("vector", lambda h, tt=tt: h.tensor_scalar(
                            ghold[:, tt * 512:(tt + 1) * 512], PMM[tt][:],
                            stats[:, SCG + tt:SCG + tt + 1], 0.0, ALU.mult, ALU.max),
                            waits=[ev_mm] + prev_gh[tt], inc="ve", chain="s2_gx4")
                        ev_bank_free[tt] = ev_e
                    else:
                        ev_u = P.op("vector", lambda h, tt=tt: h.tensor_scalar(
                            uh[:], PMM[tt][:], stats[:, SCG + tt:SCG + tt + 1], None,
                            ALU.mult), waits=[ev_mm], inc="ve", chain="s2_uh")
                        ev_s = P.op("scalar", lambda h, tt=tt: h.square(
                            tmpA[:, 0:512], ghold[:, tt * 512:(tt + 1) * 512]),
                            waits=[ev_bank_free[tt], ev_h_sp.get(("hh_prev",))], inc="se",
                            chain="s2_tmpA")
                        hb = ghold[:, tt * 512:(tt + 1) * 512]
                        ev_hh = P.op("vector", lambda h, hb=hb: h.tensor_mul(
                            hb, tmpA[:, 0:512], uh[:]),
                            waits=[ev_u, ev_s], inc="ve", chain="s2_uh")
                        ev_h_sp[("hh_prev",)] = ev_hh
                        ev_hm = P.op("vector", lambda h, tt=tt, j=j, hb=hb: h.reduce_max(
                            stats[:, HM + tt * 16 + j:HM + tt * 16 + j + 1], hb,
                            axis=AX.X, apply_absolute_value=True), waits=[ev_hh], inc="ve")
                        ev_hmax.append(ev_hm)
                        ev_d = P.op("gpsimd", lambda h, tt=tt, j=j, hb=hb: h.dma_start(
                            scr["h_sp"][tt * 128:(tt + 1) * 128, j * 512:(j + 1) * 512], hb),
                            waits=[ev_hh], inc="d2", inc_k=16)
                        ev_h_sp[(j, tt)] = ev_d
                        prev_gh[tt] = [ev_d, ev_hm]
                        ev_bank_free[tt] = ev_hh
        ev_hqT = [None] * NTT
        ev_lad2 = None
        for tt in range(NTT):
            ev_am = P.op("vector", lambda h, tt=tt: h.reduce_max(
                stats[:, AMX + tt:AMX + tt + 1],
                stats[:, HM + tt * 16:HM + tt * 16 + 11], axis=AX.X), waits=ev_hmax, inc="ve")
            P.mset("vector", stats[:, T1_:T1_ + 64], 1.0)
            ev = P.op("vector", lambda h: h.tensor_single_scalar(
                stats[:, T1_:T1_ + 64], stats[:, AMX:AMX + 64], 1e-5, ALU.max),
                waits=[ev_am], inc="ve")
            ev = P.op("vector", lambda h: h.reciprocal(
                stats[:, SUM2:SUM2 + 64], stats[:, T1_:T1_ + 64]), waits=[ev], inc="ve")
            ev_sx2 = P.op("vector", lambda h: h.tensor_scalar(
                stats[:, SX:SX + 64], stats[:, SUM2:SUM2 + 64], 127.0, None, ALU.mult),
                waits=[ev], inc="ve")
            ev_sc2 = P.op("vector", lambda h: h.tensor_scalar(
                stats[:, SCG:SCG + 64], stats[:, T1_:T1_ + 64], g["wconst"][:, 5:6],
                0.5 / 127.0, ALU.mult, ALU.mult), waits=[ev_sx2], inc="ve")
            ev_lads = []
            cbuf = [tmpA, ghold]
            for ci, (c0, cw) in enumerate([(0, 2048), (2048, 2048), (4096, 1536)]):
                cb = cbuf[ci % 2]
                chn = "s2_tmpA" if ci % 2 == 0 else "s2_gx4"
                ev_ld = P.op("sync", lambda h, tt=tt, c0=c0, cw=cw, cb=cb: h.dma_start(
                    cb[:, 0:cw], scr["h_sp"][tt * 128:(tt + 1) * 128, c0:c0 + cw]),
                    waits=sum(prev_gh, [])
                    + [ev_lads[-2] if len(ev_lads) >= 2 else ev_lad2],
                    inc="d1", inc_k=16, chain=chn)
                ev_lad2 = emit_quant_ladder(
                    P, cb[:, 0:cw], cw, stats[:, SX + tt:SX + tt + 1],
                    hq[:, c0:c0 + cw], lad,
                    waits=[ev_ld, ev_sx2, ev_lad2] + (ev_xqT if (tt == 0 and ci == 0) else []),
                    a_chain="s2_tmpB")
                ev_lads.append(ev_lad2)
            ev_last = None
            chunk_of = [0, 0, 1, 1, 2, 2]
            for gi in range(6):
                n_in = min(8, NFC - gi * 8)
                bank = PTB[gno[0] % 2]

                def tr(h, gi=gi, n_in=n_in, bank=bank):
                    ins = None
                    for k in range(n_in):
                        fc = gi * 8 + k
                        ins = h.transpose(bank[:, k * 128:(k + 1) * 128],
                                          hq[:, fc * 128:(fc + 1) * 128], g["identb"][:])
                    return ins
                ev_t = P.op("tensor", tr, waits=[ev_lads[chunk_of[gi]],
                                                 tg.get(gno[0] - 2)], inc="pe")
                arena = hqA[:] if gi < 4 else qT_lo[:]
                fc0 = gi * 8 if gi < 4 else gi * 8 - 32
                dst3 = arena.rearrange("p (fc t) -> p fc t", t=TS)[
                    :, fc0:fc0 + n_in, tt * 128:(tt + 1) * 128]
                bk3 = bank[:].rearrange("p (k c) -> p k c", c=128)[:, 0:n_in, :]
                ev_e = P.op("vector", lambda h, dst3=dst3, bk3=bk3: h.tensor_copy(dst3, bk3),
                            waits=[ev_t], inc="ve")
                tg[gno[0]] = ev_e
                gno[0] += 1
                ev_last = ev_e
            ev_lad2 = ev_last
            ev_hqT[tt] = ev_last
        for oc4 in range(4):
            for fc in range(NFC):
                wi1, r1_, ev_w1 = wload(scr["dnT_q"], fc * 128, oc4 * 512)

                def mm(h, fc=fc, r1_=r1_):
                    ins = None
                    for tt in range(NTT):
                        ins = h.matmul(PMM[tt][:], hqT_ap(fc, tt * 128, (tt + 1) * 128),
                                       wt[r1_][:], start=(fc == 0), stop=(fc == NFC - 1))
                    return ins
                wts = [ev_w1]
                if fc == 0:
                    wts += ev_hqT + ev_bank_free
                ev_mm = P.op("tensor", mm, waits=wts, inc="pe")
                ev_wu[wi1] = ev_mm
            for tt in range(NTT):
                csl = slice(oc4 * 512, (oc4 + 1) * 512)
                ev_e = P.op("vector", lambda h, tt=tt, csl=csl: h.tensor_scalar(
                    ffn[:, tt * D + csl.start: tt * D + csl.stop], PMM[tt][:],
                    stats[:, SCG + tt:SCG + tt + 1], None, ALU.mult),
                    waits=[ev_mm], inc="ve")
                ev_bank_free[tt] = ev_e
        ev_red4, ev_sq4 = [], []
        for tt in range(NTT):
            ev_red4.append(P.op("vector", lambda h, tt=tt: h.reduce_sum(
                stats[:, SUM1 + tt:SUM1 + tt + 1], ffn[:, tt * D:(tt + 1) * D], axis=AX.X),
                waits=[ev_bank_free[tt]], inc="ve"))
            ev_sq4.append(P.op("scalar", lambda h, tt=tt: h.activation(
                sejunk, ffn[:, tt * D:(tt + 1) * D], AF.Square,
                accum_out=stats[:, SQS1 + tt:SQS1 + tt + 1]),
                waits=[ev_bank_free[tt]], inc="se", chain="s2_tmpB"))
        ev_ch4 = emit_ln_chain(P, stats, SUM1, SQS1, M1, T1_, RS1, waits=ev_red4 + ev_sq4)
        prev_st = None
        prev_o = None
        for tt in range(NTT):
            tsl = slice(tt * D, (tt + 1) * D)
            ev_a = P.op("vector", lambda h, tsl=tsl, tt=tt: h.tensor_scalar(
                tmpA[:], ffn[:, tsl], stats[:, M1 + tt:M1 + tt + 1],
                stats[:, RS1 + tt:RS1 + tt + 1], ALU.subtract, ALU.mult),
                waits=[ev_ch4, prev_o], inc="ve", chain="s2_tmpA")
            ev_o = P.op("vector", lambda h, tsl=tsl: h.tensor_add(
                tmpB[:], tmpA[:], att[:, tsl]), waits=[ev_a, prev_st], inc="ve",
                chain="s2_tmpB")
            prev_o = ev_o
            r0 = tok0 + tt * 128
            ev_st = P.op("gpsimd", lambda h, r0=r0: h.dma_start(
                out[r0:r0 + 128, :], tmpB[:]), waits=[ev_o], inc="d2", inc_k=16,
                chain="s2_tmpB")
            prev_st = ev_st
            prev_stage_done[tt] = ev_st
    P.wait("gpsimd", [prev_stage_done[tt] for tt in range(NTT)])


def kernel(L0_attn_X, L0_attn_W_in, L0_attn_b_in, L0_attn_W_out, L0_attn_b_out,
           L0_ffn_w_gate_up, L0_ffn_w_down, L0_ffn_norm_weight):
    X = np.ascontiguousarray(np.asarray(L0_attn_X, dtype=np.float32))
    W_in = np.ascontiguousarray(np.asarray(L0_attn_W_in, dtype=np.float32))
    b_in = np.asarray(L0_attn_b_in, dtype=np.float32)
    W_out = np.ascontiguousarray(np.asarray(L0_attn_W_out, dtype=np.float32))
    b_out = np.asarray(L0_attn_b_out, dtype=np.float32)
    w_gu = np.ascontiguousarray(np.asarray(L0_ffn_w_gate_up, dtype=np.float32))
    w_dn = np.ascontiguousarray(np.asarray(L0_ffn_w_down, dtype=np.float32))
    nw = np.asarray(L0_ffn_norm_weight, dtype=np.float32)
    return _run(X, W_in, b_in, W_out, b_out, w_gu, w_dn, nw, build_program())


def _run(X, W_in, b_in, W_out, b_out, w_gu, w_dn, nw, nc, extra_keys=()):
    b_in_bc = np.ascontiguousarray(np.broadcast_to(b_in[None, :], (128, E)))
    b_out_bc = np.ascontiguousarray(np.broadcast_to(b_out[None, :], (128, D)))
    nw_bc = np.ascontiguousarray(np.broadcast_to(nw[None, :], (128, D)))
    in_maps = []
    for c in range(8):
        b, half = c // 2, c % 2
        qc, qs = _rope_tables(np.arange(half * T, (half + 1) * T))
        oc_, os_ = _rope_tables(np.arange((1 - half) * T, (2 - half) * T))
        in_maps.append({
            "x": np.ascontiguousarray(X[b, half * T:(half + 1) * T, :]),
            "xo": np.ascontiguousarray(X[b, (1 - half) * T:(2 - half) * T, :]),
            "q_cos": qc, "q_sin": qs, "o_cos": oc_, "o_sin": os_,
            "w_in": W_in, "w_out": W_out, "w_gu": w_gu, "w_dn": w_dn,
            "b_in_bc": b_in_bc, "b_out_bc": b_out_bc, "nw_bc": nw_bc,
        })
    res = run_bass_kernel_spmd(nc, in_maps, core_ids=list(range(8)))
    full = np.empty((4, 4096, D), dtype=np.float32)
    for c in range(8):
        b, half = c // 2, c % 2
        full[b, half * T:(half + 1) * T, :] = res.results[c]["out"]
    if extra_keys:
        return full, res
    return full
